# version 10
# baseline (speedup 1.0000x reference)
import numpy as np
from contextlib import ExitStack
import concourse.bass as bass
import concourse.mybir as mybir
from concourse.bass_utils import run_bass_kernel_spmd

F32 = mybir.dt.float32
BF16 = mybir.dt.bfloat16
AF = mybir.ActivationFunctionType
ALU = mybir.AluOpType
AX = mybir.AxisListType.X
C_RND = 12582912.0
D_MODEL = 1024
D_INNER = 2048
D_FFN = 3072
L_SEQ = 2048
ENGINES = ("sync", "scalar", "vector", "tensor", "gpsimd")


def _key(x):
    if isinstance(x, str):
        return x
    t = getattr(x, "tensor", None)
    if t is not None:
        return t.name
    return x.name


class Sched:

    def __init__(self, name):
        self.nc = bass.Bass("TRN2", target_bir_lowering=False, debug=False)
        self.es = ExitStack()
        self.q = {e: [] for e in ENGINES}
        self.sems = {}
        self.last_w = {}
        self.readers = {}
        self.waited = {e: {} for e in ENGINES}
        for e in ENGINES:
            self._mk_sem("q_" + e)
        self.dma_issued = {e: {} for e in ENGINES}
        self.nops = 0

    def _mk_sem(self, name):
        h = self.es.enter_context(self.nc.semaphore(name))
        self.sems[name] = [h, 0]

    def sb(self, name, shape, dt):
        cm = self.nc.sbuf_tensor(name, shape, dt)
        h = cm.__enter__()
        self._cms = getattr(self, "_cms", {})
        self._alloc_idx = getattr(self, "_alloc_idx", 0) + 1
        self._cms[name] = (self._alloc_idx, cm)
        return h

    def psum(self, name, shape, dt=F32):
        cm = self.nc.psum_tensor(name, shape, dt)
        h = cm.__enter__()
        self._cms = getattr(self, "_cms", {})
        self._alloc_idx = getattr(self, "_alloc_idx", 0) + 1
        self._cms[name] = (self._alloc_idx, cm)
        return h

    def release(self, *handles):
        items = sorted((self._cms[h.name][0], h.name) for h in handles)
        for _, name in reversed(items):
            self._cms.pop(name)[1].__exit__(None, None, None)

    def val(self, sem):
        return self.sems[sem][1]

    def _emit_wait(self, eng, sem_name, v):
        if v is None or v <= 0:
            return
        wd = self.waited[eng]
        if wd.get(sem_name, 0) >= v:
            return
        wd[sem_name] = v
        h = self.sems[sem_name][0]
        self.q[eng].append(lambda e, h=h, v=v: e.wait_ge(h, v))

    def _dep_waits(self, eng, r, w):
        need = {}
        for x in r:
            k = _key(x)
            lw = self.last_w.get(k)
            if lw:
                need[lw[0]] = max(need.get(lw[0], 0), lw[1])
        for x in w:
            k = _key(x)
            lw = self.last_w.get(k)
            if lw:
                need[lw[0]] = max(need.get(lw[0], 0), lw[1])
            for sn, v in self.readers.get(k, {}).items():
                need[sn] = max(need.get(sn, 0), v)
        for sn, v in need.items():
            self._emit_wait(eng, sn, v)

    def _dep_update(self, sem_name, v, r, w):
        for x in w:
            k = _key(x)
            self.last_w[k] = (sem_name, v)
            self.readers[k] = {}
        for x in r:
            k = _key(x)
            d = self.readers.setdefault(k, {})
            d[sem_name] = max(d.get(sem_name, 0), v)

    def aop(self, eng, meth, *args, r=(), w=(), sem_name=None, k=1, **kw):
        self._dep_waits(eng, r, w)
        if sem_name is None:
            sem_name = "q_" + eng
        s = self.sems[sem_name]
        s[1] += k
        v = s[1]
        h = s[0]

        def fn(e, meth=meth, args=args, kw=kw, h=h, k=k):
            getattr(e, meth)(*args, **kw).then_inc(h, k)
        self.q[eng].append(fn)
        self._dep_update(sem_name, v, r, w)
        self.nops += 1
        return sem_name, v

    def dma(self, eng, out, in_, **kw):
        sb_side = out if out.space == bass.MemorySpace.SBUF else in_
        sem_name = "dma_" + eng + "_" + _key(sb_side)
        if sem_name not in self.sems:
            self._mk_sem(sem_name)
        v = self.aop(eng, "dma_start", r=[in_], w=[out], sem_name=sem_name,
                     k=16, out=out, in_=in_, **kw)
        d = self.dma_issued[eng]
        d[sem_name] = self.sems[sem_name][1]
        return v

    def mm_group(self, psum_ap, pairs, r_extra=(), eng="tensor"):
        nk = len(pairs)
        r = ([p[0] for p in pairs] + [p[1] for p in pairs] + list(r_extra))
        w = [psum_ap]
        self._dep_waits(eng, r, w)
        sem_name = "q_" + eng
        s = self.sems[sem_name]
        s[1] += 1
        v = s[1]
        h = s[0]

        def fn(e, pairs=pairs, psum_ap=psum_ap, h=h, nk=nk):
            ins = None
            for j, (lhsT, rhs) in enumerate(pairs):
                ins = e.matmul(psum_ap, lhsT, rhs, start=(j == 0),
                               stop=(j == nk - 1))
            ins.then_inc(h, 1)
        self.q[eng].append(fn)
        self._dep_update(sem_name, v, r, w)
        self.nops += nk
        return sem_name, v

    def raw(self, eng, fn):
        self.q[eng].append(fn)

    def flush_engine(self, eng):
        self._emit_wait(eng, "q_" + eng, self.sems["q_" + eng][1])
        for sn, v in self.dma_issued[eng].items():
            self._emit_wait(eng, sn, v)

    def barrier(self, engines=ENGINES):
        if "bar" not in self.sems:
            self._mk_sem("bar")
        s = self.sems["bar"]
        tgt = s[1] + len(engines)
        for e in engines:
            self.flush_engine(e)
            h = s[0]
            self.q[e].append(lambda e_, h=h: e_.sem_inc(h, 1))
        s[1] = tgt
        for e in engines:
            self._emit_wait(e, "bar", tgt)
        self.last_w = {k: ("bar", tgt) for k in self.last_w}
        self.readers = {}

    def build(self):
        with self.nc.Block() as block:
            for eng in ENGINES:
                ops = self.q[eng]
                if not ops:
                    continue
                getattr(block, eng)(lambda e, ops=ops: [fn(e) for fn in ops])
        for _, (_, cm) in sorted(getattr(self, "_cms", {}).items(),
                                 key=lambda kv: -kv[1][0]):
            cm.__exit__(None, None, None)
        self._cms = {}
        self.es.close()
        return self.nc


def emit_identity(S, idn, fill=1.0):
    S.aop("gpsimd", "memset", idn[:], 0.0, w=[idn])
    S.aop("gpsimd", "affine_select", out=idn[:], in_=idn[:],
          compare_op=ALU.not_equal, fill=fill, base=0,
          pattern=[[-1, idn.shape[0]]], channel_multiplier=1, r=[idn], w=[idn])


def _rep(h, nseg, reps):
    return h[:].unsqueeze(2).broadcast_to([128, nseg, reps])


def emit_scale_bcast(S, mx_ap, sca, sca_bc, ones1, ps_bc, rq_cell, mx_keys=()):
    S.aop("vector", "tensor_reduce", sca[0:1, 0:1], mx_ap, AX, ALU.max,
          r=[mx_ap] + list(mx_keys), w=[sca])
    S.aop("vector", "tensor_scalar", sca[0:1, 0:1], sca[0:1, 0:1],
          rq_cell[0:1, 0:1], 1e-12, ALU.mult, ALU.max,
          r=[sca, rq_cell], w=[sca])
    S.aop("vector", "reciprocal", sca[0:1, 1:2], sca[0:1, 0:1], r=[sca], w=[sca])
    S.aop("tensor", "matmul", ps_bc[:, 0:2], ones1[0:1, :], sca[0:1, 0:2],
          start=True, stop=True, r=[ones1, sca], w=[ps_bc])
    S.aop("scalar", "activation", sca_bc[:, 0:2], ps_bc[:, 0:2], AF.Copy,
          r=[ps_bc], w=[sca_bc])


def emit_consts(S):
    ones1 = S.sb("ones1", [1, 128], F32)
    rq_cell = S.sb("rq_cell", [1, 1], F32)
    r7 = S.sb("r7", [128, 1], F32)
    S.aop("vector", "memset", ones1[:], 1.0, w=[ones1])
    S.aop("vector", "memset", rq_cell[:], 32767.0, w=[rq_cell])
    S.aop("vector", "reciprocal", rq_cell[:], rq_cell[:], r=[rq_cell], w=[rq_cell])
    S.aop("vector", "memset", r7[:], 7.0, w=[r7])
    S.aop("vector", "reciprocal", r7[:], r7[:], r=[r7], w=[r7])
    return ones1, rq_cell, r7


def emit_qdq_rounds(S, out_ap, in_ap, sca_bc, tmp_ap, r_keys, w_keys):
    S.aop("vector", "tensor_scalar", tmp_ap, in_ap, sca_bc[:, 1:2], C_RND,
          ALU.mult, ALU.add, r=list(r_keys) + [sca_bc], w=[tmp_ap])
    S.aop("vector", "tensor_scalar", out_ap, tmp_ap, C_RND, sca_bc[:, 0:1],
          ALU.subtract, ALU.mult, r=[tmp_ap, sca_bc], w=list(w_keys))


def build_l2():
    S = Sched("l2")
    nc = S.nc
    x1n = nc.dram_tensor("x1n", [D_MODEL, L_SEQ], F32, kind="ExternalInput").ap()
    mx8 = nc.dram_tensor("mx8", [1, 8], F32, kind="ExternalInput").ap()
    w_hbm = [nc.dram_tensor("w_up", [D_FFN, D_MODEL], F32, kind="ExternalInput").ap(),
             nc.dram_tensor("w_gate", [D_FFN, D_MODEL], F32, kind="ExternalInput").ap()]
    out_hbm = [nc.dram_tensor("up_o", [D_FFN, L_SEQ], F32, kind="ExternalOutput").ap(),
               nc.dram_tensor("gate_o", [D_FFN, L_SEQ], F32, kind="ExternalOutput").ap()]
    m_o = [nc.dram_tensor("m_up", [1, 1], F32, kind="ExternalOutput").ap(),
           nc.dram_tensor("m_gate", [1, 1], F32, kind="ExternalOutput").ap()]

    ND = D_MODEL // 128
    NI = D_FFN // 128

    idn_bf = S.sb("idn_bf", [128, 128], BF16)
    idn_f = S.sb("idn_f", [128, 128], F32)
    mx_sb = S.sb("mx_sb", [1, 8], F32)
    sca = S.sb("sca", [1, 2], F32)
    sca_bc = S.sb("sca_bc", [128, 2], F32)
    xq = [S.sb(f"xq{k}", [128, L_SEQ], BF16) for k in range(ND)]
    xstage = [S.sb(f"xstage{j}", [128, L_SEQ], F32) for j in range(2)]
    xtmp = S.sb("xtmp", [128, L_SEQ], F32)
    wstage = [S.sb(f"wstage{j}", [128, D_MODEL], F32) for j in range(2)]
    wq_bufs = [S.sb(f"wqb{j}", [128, D_MODEL], BF16) for j in range(2)]
    sblk = S.sb("sblk", [128, 32], F32)
    rblk = S.sb("rblk", [128, 32], F32)
    wr_t = S.sb("wr_t", [128, D_MODEL], F32)
    wqT = [[S.sb(f"wqT{w}_{k}", [128, D_FFN], BF16) for k in range(ND)]
           for w in range(2)]
    ostage = [S.sb(f"ostage{j}", [128, L_SEQ], F32) for j in range(2)]
    mrun = S.sb("mrun", [128, 2], F32)
    mtmp = S.sb("mtmp", [128, 1], F32)
    mrow = S.sb("mrow", [2, 128], F32)
    mfin = S.sb("mfin", [2, 1], F32)

    ps_bc = S.psum("ps_bc", [128, 2])
    ps_tr = [S.psum(f"ps_tr{j}", [128, 512], BF16) for j in range(2)]
    ps_mm = [S.psum(f"ps_mm{j}", [128, 512]) for j in range(2)]
    ps_fin = S.psum("ps_fin", [2, 128])

    emit_identity(S, idn_bf)
    emit_identity(S, idn_f)
    ones1, rq_cell, r7 = emit_consts(S)
    S.aop("vector", "memset", mrun[:], 0.0, w=[mrun])
    S.dma("sync", mx_sb[:], mx8[:])

    emit_scale_bcast(S, mx_sb[0:1, 0:8], sca, sca_bc, ones1, ps_bc, rq_cell)

    for k in range(ND):
        stage = xstage[k % 2]
        S.dma("sync", stage[:], x1n[k * 128:(k + 1) * 128, :])
        emit_qdq_rounds(S, xq[k][:], stage[:], sca_bc, xtmp[:],
                        r_keys=[stage], w_keys=[xq[k]])

    for w in range(2):
        for t in range(NI):
            st = wstage[t % 2]
            S.dma("sync", st[:], w_hbm[w][t * 128:(t + 1) * 128, :])
            S.aop("vector", "tensor_reduce", sblk[:],
                  st[:].rearrange("p (a b) -> p a b", a=32), AX, ALU.max,
                  apply_absolute_value=True, r=[st], w=[sblk])
            S.aop("vector", "tensor_scalar", sblk[:], sblk[:], r7[:, 0:1], 1e-12,
                  ALU.mult, ALU.max, r=[sblk, r7], w=[sblk])
            S.aop("vector", "reciprocal", rblk[:], sblk[:], r=[sblk], w=[rblk])
            S.aop("vector", "tensor_tensor",
                  out=wr_t[:].rearrange("p (a b) -> p a b", a=32),
                  in0=st[:].rearrange("p (a b) -> p a b", a=32),
                  in1=_rep(rblk, 32, 32), op=ALU.mult,
                  r=[st, rblk], w=[wr_t])
            S.aop("vector", "tensor_scalar", wr_t[:], wr_t[:], C_RND, C_RND,
                  ALU.add, ALU.subtract, r=[wr_t], w=[wr_t])
            wq = wq_bufs[t % 2]
            S.aop("vector", "tensor_tensor",
                  out=wq[:].rearrange("p (a b) -> p a b", a=32),
                  in0=wr_t[:].rearrange("p (a b) -> p a b", a=32),
                  in1=_rep(sblk, 32, 32), op=ALU.mult,
                  r=[wr_t, sblk], w=[wq])
            for half in range(2):
                bank = ps_tr[half]
                for j in range(4):
                    kblk = half * 4 + j
                    S.aop("tensor", "transpose", bank[:, j * 128:(j + 1) * 128],
                          wq[:, kblk * 128:(kblk + 1) * 128], idn_bf[:],
                          r=[wq, idn_bf], w=[bank])
                for j in range(4):
                    kblk = half * 4 + j
                    S.aop("scalar", "activation",
                          wqT[w][kblk][:, t * 128:(t + 1) * 128],
                          bank[:, j * 128:(j + 1) * 128], AF.Copy,
                          r=[bank], w=[f"wqT{w}_{kblk}:{t}"])
    for w in range(2):
        for t in range(NI):
            stg = ostage[t % 2]
            for lc in range(4):
                bank = ps_mm[lc % 2]
                pairs = [(wqT[w][k][:, t * 128:(t + 1) * 128],
                          xq[k][:, lc * 512:(lc + 1) * 512]) for k in range(ND)]
                S.mm_group(bank[:], pairs,
                           r_extra=[f"wqT{w}_{k}:{t}" for k in range(ND)])
                S.aop("scalar", "activation", stg[:, lc * 512:(lc + 1) * 512],
                      bank[:], AF.Copy, r=[bank], w=[stg])
            S.dma("sync", out_hbm[w][t * 128:(t + 1) * 128, :], stg[:])
            S.aop("vector", "tensor_reduce", mtmp[:], stg[:], AX, ALU.max,
                  apply_absolute_value=True, r=[stg], w=[mtmp])
            S.aop("vector", "tensor_tensor", mrun[:, w:w + 1], mrun[:, w:w + 1],
                  mtmp[:], ALU.max, r=[mrun, mtmp], w=[mrun])

    S.aop("tensor", "transpose", ps_fin[:], mrun[:, 0:2], idn_f[:],
          r=[mrun, idn_f], w=[ps_fin])
    S.aop("scalar", "activation", mrow[:], ps_fin[:], AF.Copy, r=[ps_fin], w=[mrow])
    S.aop("vector", "tensor_reduce", mfin[:], mrow[:], AX, ALU.max,
          r=[mrow], w=[mfin])
    S.dma("sync", m_o[0][:], mfin[0:1, :])
    S.dma("sync", m_o[1][:], mfin[1:2, :])
    S.flush_engine("sync")
    return S.build()


def host_l2_inputs(x1n_fm, mx8, w_up, w_gate):
    return {"x1n": np.ascontiguousarray(x1n_fm), "mx8": mx8,
            "w_up": w_up, "w_gate": w_gate}


def build_l3():
    S = Sched("l3")
    nc = S.nc
    up_i = nc.dram_tensor("up_i", [D_FFN, L_SEQ], F32, kind="ExternalInput").ap()
    gate_i = nc.dram_tensor("gate_i", [D_FFN, L_SEQ], F32, kind="ExternalInput").ap()
    mup8 = nc.dram_tensor("mup8", [1, 8], F32, kind="ExternalInput").ap()
    mg8 = nc.dram_tensor("mg8", [1, 8], F32, kind="ExternalInput").ap()
    upq_o = nc.dram_tensor("upq_o", [D_FFN, L_SEQ], F32, kind="ExternalOutput").ap()
    gate2_o = nc.dram_tensor("gate2_o", [D_FFN, L_SEQ], F32, kind="ExternalOutput").ap()
    m_g2 = nc.dram_tensor("m_g2", [1, 1], F32, kind="ExternalOutput").ap()
    NI = D_FFN // 128

    idn_f = S.sb("idn_f", [128, 128], F32)
    mx_sb = S.sb("mx_sb", [1, 16], F32)
    sca_u = S.sb("sca_u", [1, 2], F32)
    sca_g = S.sb("sca_g", [1, 2], F32)
    scb_u = S.sb("scb_u", [128, 2], F32)
    scb_g = S.sb("scb_g", [128, 2], F32)
    ustage = [S.sb(f"ustage{j}", [128, L_SEQ], F32) for j in range(3)]
    gstage = [S.sb(f"gstage{j}", [128, L_SEQ], F32) for j in range(3)]
    uq = [S.sb(f"uq{j}", [128, L_SEQ], F32) for j in range(3)]
    gq = [S.sb(f"gq{j}", [128, L_SEQ], F32) for j in range(3)]
    sig_t = S.sb("sig_t", [128, L_SEQ], F32)
    g2 = [S.sb(f"g2_{j}", [128, L_SEQ], F32) for j in range(3)]
    xtmp = S.sb("xtmp", [128, L_SEQ], F32)
    mrun = S.sb("mrun", [128, 1], F32)
    mtmp = S.sb("mtmp", [128, 1], F32)
    mrow = S.sb("mrow", [1, 128], F32)
    mfin = S.sb("mfin", [1, 1], F32)
    ps_bc = S.psum("ps_bc", [128, 4])
    ps_fin = S.psum("ps_fin", [1, 128])

    emit_identity(S, idn_f)
    ones1, rq_cell, r7 = emit_consts(S)
    S.aop("vector", "memset", mrun[:], 0.0, w=[mrun])
    S.dma("sync", mx_sb[0:1, 0:8], mup8[:])
    S.dma("sync", mx_sb[0:1, 8:16], mg8[:])
    emit_scale_bcast(S, mx_sb[0:1, 0:8], sca_u, scb_u, ones1, ps_bc[:, 0:2], rq_cell)
    emit_scale_bcast(S, mx_sb[0:1, 8:16], sca_g, scb_g, ones1, ps_bc[:, 2:4], rq_cell)

    for t in range(NI):
        ust = ustage[t % 3]
        gst = gstage[t % 3]
        S.dma("sync", ust[:], up_i[t * 128:(t + 1) * 128, :])
        S.dma("gpsimd", gst[:], gate_i[t * 128:(t + 1) * 128, :])
        emit_qdq_rounds(S, uq[t % 3][:], ust[:], scb_u, xtmp[:],
                        r_keys=[ust], w_keys=[uq[t % 3]])
        S.dma("scalar", upq_o[t * 128:(t + 1) * 128, :], uq[t % 3][:])
        emit_qdq_rounds(S, gq[t % 3][:], gst[:], scb_g, xtmp[:],
                        r_keys=[gst], w_keys=[gq[t % 3]])
        S.aop("scalar", "activation", sig_t[:], gq[t % 3][:], AF.Sigmoid,
              r=[gq[t % 3]], w=[sig_t])
        S.aop("vector", "tensor_scalar", sig_t[:], sig_t[:], 65536.0, C_RND,
              ALU.mult, ALU.add, r=[sig_t], w=[sig_t])
        S.aop("vector", "tensor_scalar", sig_t[:], sig_t[:], C_RND, 65535.0,
              ALU.subtract, ALU.min, r=[sig_t], w=[sig_t])
        S.aop("vector", "scalar_tensor_tensor", out=g2[t % 3][:], in0=sig_t[:],
              scalar=1.0 / 65536.0, in1=gq[t % 3][:], op0=ALU.mult, op1=ALU.mult,
              r=[sig_t, gq[t % 3]], w=[g2[t % 3]])
        S.dma("scalar", gate2_o[t * 128:(t + 1) * 128, :], g2[t % 3][:])
        S.aop("vector", "tensor_reduce", mtmp[:], g2[t % 3][:], AX, ALU.max,
              apply_absolute_value=True, r=[g2[t % 3]], w=[mtmp])
        S.aop("vector", "tensor_tensor", mrun[:], mrun[:], mtmp[:], ALU.max,
              r=[mrun, mtmp], w=[mrun])

    S.aop("tensor", "transpose", ps_fin[:], mrun[:], idn_f[:],
          r=[mrun, idn_f], w=[ps_fin])
    S.aop("scalar", "activation", mrow[:], ps_fin[:], AF.Copy, r=[ps_fin], w=[mrow])
    S.aop("vector", "tensor_reduce", mfin[:], mrow[:], AX, ALU.max, r=[mrow], w=[mfin])
    S.dma("sync", m_g2[:], mfin[:])
    S.flush_engine("sync")
    return S.build()


def build_l4():
    S = Sched("l4")
    nc = S.nc
    upq_i = nc.dram_tensor("upq_i", [D_FFN, L_SEQ], F32, kind="ExternalInput").ap()
    g2_i = nc.dram_tensor("g2_i", [D_FFN, L_SEQ], F32, kind="ExternalInput").ap()
    mg28 = nc.dram_tensor("mg28", [1, 8], F32, kind="ExternalInput").ap()
    o_o = nc.dram_tensor("o_o", [D_FFN, L_SEQ], F32, kind="ExternalOutput").ap()
    m_o = nc.dram_tensor("m_o", [1, 1], F32, kind="ExternalOutput").ap()
    NI = D_FFN // 128

    idn_f = S.sb("idn_f", [128, 128], F32)
    mx_sb = S.sb("mx_sb", [1, 8], F32)
    sca = S.sb("sca", [1, 2], F32)
    scb = S.sb("scb", [128, 2], F32)
    ustage = [S.sb(f"ustage{j}", [128, L_SEQ], F32) for j in range(3)]
    gstage = [S.sb(f"gstage{j}", [128, L_SEQ], F32) for j in range(3)]
    ot = [S.sb(f"ot{j}", [128, L_SEQ], F32) for j in range(3)]
    xtmp = S.sb("xtmp", [128, L_SEQ], F32)
    mrun = S.sb("mrun", [128, 1], F32)
    mtmp = S.sb("mtmp", [128, 1], F32)
    mrow = S.sb("mrow", [1, 128], F32)
    mfin = S.sb("mfin", [1, 1], F32)
    ps_bc = S.psum("ps_bc", [128, 2])
    ps_fin = S.psum("ps_fin", [1, 128])

    emit_identity(S, idn_f)
    ones1, rq_cell, r7 = emit_consts(S)
    S.aop("vector", "memset", mrun[:], 0.0, w=[mrun])
    S.dma("sync", mx_sb[:], mg28[:])
    emit_scale_bcast(S, mx_sb[0:1, 0:8], sca, scb, ones1, ps_bc[:, 0:2], rq_cell)

    for t in range(NI):
        ust = ustage[t % 3]
        gst = gstage[t % 3]
        S.dma("sync", ust[:], upq_i[t * 128:(t + 1) * 128, :])
        S.dma("gpsimd", gst[:], g2_i[t * 128:(t + 1) * 128, :])
        emit_qdq_rounds(S, gst[:], gst[:], scb, xtmp[:],
                        r_keys=[gst], w_keys=[gst])
        S.aop("vector", "tensor_tensor", ot[t % 3][:], gst[:], ust[:], ALU.mult,
              r=[gst, ust], w=[ot[t % 3]])
        S.dma("scalar", o_o[t * 128:(t + 1) * 128, :], ot[t % 3][:])
        S.aop("vector", "tensor_reduce", mtmp[:], ot[t % 3][:], AX, ALU.max,
              apply_absolute_value=True, r=[ot[t % 3]], w=[mtmp])
        S.aop("vector", "tensor_tensor", mrun[:], mrun[:], mtmp[:], ALU.max,
              r=[mrun, mtmp], w=[mrun])

    S.aop("tensor", "transpose", ps_fin[:], mrun[:], idn_f[:],
          r=[mrun, idn_f], w=[ps_fin])
    S.aop("scalar", "activation", mrow[:], ps_fin[:], AF.Copy, r=[ps_fin], w=[mrow])
    S.aop("vector", "tensor_reduce", mfin[:], mrow[:], AX, ALU.max, r=[mrow], w=[mfin])
    S.dma("sync", m_o[:], mfin[:])
    S.flush_engine("sync")
    return S.build()


class PlainLN:

    def __init__(self, S, nd, L, pref, ones1, ones_col, idn_f, eps=1e-5,
                 ps_banks=None):
        self.S, self.nd, self.L = S, nd, L
        self.ones1, self.ones_col, self.idn_f = ones1, ones_col, idn_f
        self.NLS = L // 128
        assert self.NLS <= 16
        self.sc = 1.0 / (nd * 128)
        self.ps_su = S.psum(pref + "ps_su", [128, self.NLS])
        self.ps_tp = S.psum(pref + "ps_tp", [1, 512])
        self.ps_bc = (ps_banks if ps_banks is not None else
                      [S.psum(pref + f"ps_bc{j}", [128, 512]) for j in range(2)])
        self.mu_cols = S.sb(pref + "mu_cols", [128, self.NLS], F32)
        self.st_row = S.sb(pref + "st_row", [1, L], F32)
        self.mu_bc = S.sb(pref + "mu_bc", [128, L], F32)
        self.rstd_bc = S.sb(pref + "rstd_bc", [128, L], F32)
        self.var_cols = S.sb(pref + "var_cols", [128, self.NLS], F32)
        self.eps_col = S.sb(pref + "eps_col", [128, 1], F32)
        S.aop("vector", "memset", self.eps_col[:], eps, w=[self.eps_col])

    def _stat_pass(self, tiles, out_cols):
        S, NLS, nd = self.S, self.NLS, self.nd
        for ls in range(NLS):
            pairs = [(tiles[k][:, ls * 128:(ls + 1) * 128], self.ones_col[:, 0:1])
                     for k in range(nd)]
            S.mm_group(self.ps_su[:, ls:ls + 1], pairs)
        S.aop("scalar", "activation", out_cols[:], self.ps_su[:], AF.Copy,
              scale=self.sc, r=[self.ps_su], w=[out_cols])

    def _bcast_cols(self, cols, dst_bc):
        S, NLS = self.S, self.NLS
        for g in range(0, NLS, 4):
            gn = min(4, NLS - g)
            for j in range(gn):
                S.aop("tensor", "transpose", self.ps_tp[0:1, j * 128:(j + 1) * 128],
                      cols[:, g + j:g + j + 1], self.idn_f[:],
                      r=[cols, self.idn_f], w=[self.ps_tp])
            S.aop("scalar", "activation", self.st_row[0:1, g * 128:(g + gn) * 128],
                  self.ps_tp[0:1, 0:gn * 128], AF.Copy, r=[self.ps_tp],
                  w=[self.st_row])
        for g in range(0, NLS, 4):
            gn = min(4, NLS - g)
            bank = self.ps_bc[(g // 4) % 2]
            S.aop("tensor", "matmul", bank[:, 0:gn * 128], self.ones1[0:1, :],
                  self.st_row[0:1, g * 128:(g + gn) * 128], start=True, stop=True,
                  r=[self.ones1, self.st_row], w=[bank])
            S.aop("scalar", "activation", dst_bc[:, g * 128:(g + gn) * 128],
                  bank[:, 0:gn * 128], AF.Copy, r=[bank], w=[dst_bc])

    def emit_from_stats(self, tiles_in, tiles_out, mu_cols_ap, var_cols_ap,
                        extra_r=()):
        S, nd = self.S, self.nd
        S.aop("scalar", "activation", var_cols_ap, var_cols_ap, AF.Ln,
              bias=self.eps_col[:, 0:1],
              r=[var_cols_ap, self.eps_col] + list(extra_r), w=[var_cols_ap])
        S.aop("scalar", "activation", var_cols_ap, var_cols_ap, AF.Exp,
              scale=-0.5, r=[var_cols_ap], w=[var_cols_ap])
        self._bcast_cols(mu_cols_ap, self.mu_bc)
        self._bcast_cols(var_cols_ap, self.rstd_bc)
        for k in range(nd):
            S.aop("vector", "tensor_sub", tiles_in[k], tiles_in[k], self.mu_bc[:],
                  r=[tiles_in[k], self.mu_bc], w=[tiles_in[k]])
            S.aop("vector", "tensor_mul", tiles_out[k], tiles_in[k],
                  self.rstd_bc[:], r=[tiles_in[k], self.rstd_bc],
                  w=[tiles_out[k]])

    def emit(self, tiles_in, tiles_out, tiles_tmp):
        S, nd = self.S, self.nd
        self._stat_pass(tiles_in, self.mu_cols)
        self._bcast_cols(self.mu_cols, self.mu_bc)
        for k in range(nd):
            S.aop("vector", "tensor_sub", tiles_out[k], tiles_in[k], self.mu_bc[:],
                  r=[tiles_in[k], self.mu_bc], w=[tiles_out[k]])
            S.aop("scalar", "activation", tiles_tmp[k], tiles_out[k], AF.Square,
                  r=[tiles_out[k]], w=[tiles_tmp[k]])
        self._stat_pass(tiles_tmp, self.var_cols)
        S.aop("scalar", "activation", self.var_cols[:], self.var_cols[:], AF.Ln,
              bias=self.eps_col[:, 0:1], r=[self.var_cols, self.eps_col],
              w=[self.var_cols])
        S.aop("scalar", "activation", self.var_cols[:], self.var_cols[:], AF.Exp,
              scale=-0.5, r=[self.var_cols], w=[self.var_cols])
        self._bcast_cols(self.var_cols, self.rstd_bc)
        for k in range(nd):
            S.aop("vector", "tensor_mul", tiles_out[k], tiles_out[k],
                  self.rstd_bc[:], r=[tiles_out[k], self.rstd_bc],
                  w=[tiles_out[k]])


def build_l5():
    S = Sched("l5")
    nc = S.nc
    o_i = nc.dram_tensor("o_i", [D_FFN, L_SEQ], F32, kind="ExternalInput").ap()
    x1n = nc.dram_tensor("x1n", [D_MODEL, L_SEQ], F32, kind="ExternalInput").ap()
    mo8 = nc.dram_tensor("mo8", [1, 8], F32, kind="ExternalInput").ap()
    w_down = nc.dram_tensor("w_down", [D_MODEL, D_FFN], F32, kind="ExternalInput").ap()
    out_o = nc.dram_tensor("out_o", [L_SEQ, D_MODEL], F32, kind="ExternalOutput").ap()
    NI = D_FFN // 128
    ND = D_MODEL // 128

    idn_f = S.sb("idn_f", [128, 128], F32)
    idn_bf = S.sb("idn_bf", [128, 128], BF16)
    ones_col = S.sb("ones_col", [128, 1], F32)
    mx_sb = S.sb("mx_sb", [1, 8], F32)
    sca = S.sb("sca", [1, 2], F32)
    scb = S.sb("scb", [128, 2], F32)
    ps_bc = S.psum("ps_bc", [128, 2])
    emit_identity(S, idn_f)
    emit_identity(S, idn_bf)
    ones1, rq_cell, r7 = emit_consts(S)
    S.aop("vector", "memset", ones_col[:], 1.0, w=[ones_col])
    S.dma("sync", mx_sb[:], mo8[:])
    emit_scale_bcast(S, mx_sb[0:1, 0:8], sca, scb, ones1, ps_bc[:, 0:2], rq_cell)
    x2 = [S.sb(f"x2_{k}", [128, L_SEQ], F32) for k in range(ND)]
    wdT = [S.sb(f"wdT{k}", [128, D_MODEL], BF16) for k in range(NI)]

    wstage = [S.sb("wstage0", [128, D_FFN], F32)] * 2
    wq_bufs = [S.sb(f"wqb{j}", [128, D_FFN], BF16) for j in range(2)]
    sblk = S.sb("sblk", [128, 96], F32)
    rblk = S.sb("rblk", [128, 96], F32)
    ps_tr = [S.psum(f"ps_tr{j}", [128, 512], BF16) for j in range(2)]
    oq = [S.sb(f"oq{k}", [128, 1024], BF16) for k in range(NI)]
    ostage = [S.sb(f"ostage{j}", [128, 1024], F32) for j in range(2)]
    xtmp = S.sb("xtmp", [128, 1024], F32)
    x1st = [S.sb(f"x1st{j}", [128, 512], F32) for j in range(2)]
    ps_mm = [S.psum(f"ps_mm{j}", [128, 512]) for j in range(2)]
    for lh in range(2):
        co = lh * 1024
        for t in range(NI):
            st = ostage[t % 2]
            S.dma("sync", st[:], o_i[t * 128:(t + 1) * 128, co:co + 1024])
            emit_qdq_rounds(S, oq[t][:], st[:], scb, xtmp[:],
                            r_keys=[st], w_keys=[oq[t]])
        for t in range(ND):
            if lh == 0:
                st = wstage[t % 2]
                S.dma("sync", st[:], w_down[t * 128:(t + 1) * 128, :])
                S.aop("vector", "tensor_reduce", sblk[:],
                      st[:].rearrange("p (a b) -> p a b", a=96), AX, ALU.max,
                      apply_absolute_value=True, r=[st], w=[sblk])
                S.aop("vector", "tensor_scalar", sblk[:], sblk[:], r7[:, 0:1],
                      1e-12, ALU.mult, ALU.max, r=[sblk, r7], w=[sblk])
                S.aop("vector", "reciprocal", rblk[:], sblk[:], r=[sblk], w=[rblk])
                S.aop("vector", "tensor_tensor",
                      out=st[:].rearrange("p (a b) -> p a b", a=96),
                      in0=st[:].rearrange("p (a b) -> p a b", a=96),
                      in1=_rep(rblk, 96, 32), op=ALU.mult, r=[st, rblk], w=[st])
                S.aop("vector", "tensor_scalar", st[:], st[:], C_RND, C_RND,
                      ALU.add, ALU.subtract, r=[st], w=[st])
                wq = wq_bufs[t % 2]
                S.aop("vector", "tensor_tensor",
                      out=wq[:].rearrange("p (a b) -> p a b", a=96),
                      in0=st[:].rearrange("p (a b) -> p a b", a=96),
                      in1=_rep(sblk, 96, 32), op=ALU.mult, r=[st, sblk], w=[wq])
                for half in range(6):
                    bank = ps_tr[half % 2]
                    for j in range(4):
                        kblk = half * 4 + j
                        S.aop("tensor", "transpose",
                              bank[:, j * 128:(j + 1) * 128],
                              wq[:, kblk * 128:(kblk + 1) * 128], idn_bf[:],
                              r=[wq, idn_bf], w=[bank])
                    for j in range(4):
                        kblk = half * 4 + j
                        S.aop("scalar", "activation",
                              wdT[kblk][:, t * 128:(t + 1) * 128],
                              bank[:, j * 128:(j + 1) * 128], AF.Copy,
                              r=[bank], w=[f"wdT{kblk}:{t}"])
            for lc in range(2):
                sl = co + lc * 512
                xs = x1st[lc % 2]
                S.dma("sync", xs[:], x1n[t * 128:(t + 1) * 128, sl:sl + 512])
                bank = ps_mm[lc % 2]
                pairs = [(wdT[k][:, t * 128:(t + 1) * 128],
                          oq[k][:, lc * 512:(lc + 1) * 512]) for k in range(NI)]
                S.mm_group(bank[:], pairs,
                           r_extra=[f"wdT{k}:{t}" for k in range(NI)])
                S.aop("vector", "tensor_add", x2[t][:, sl:sl + 512],
                      bank[:], xs[:], r=[bank, xs], w=[x2[t]])
    S.barrier()
    S.release(*oq, ostage[0], ostage[1], xtmp, x1st[0], x1st[1],
              ps_mm[0], ps_mm[1], ps_tr[0], ps_tr[1], wstage[0],
              wq_bufs[0], wq_bufs[1], sblk, rblk)
    S.release(*wdT)

    x2t = [S.sb(f"x2t_{k}", [128, L_SEQ], F32) for k in range(ND)]
    ln2 = PlainLN(S, ND, L_SEQ, "ln2", ones1, ones_col, idn_f)
    ln2.emit([t[:] for t in x2], [t[:] for t in x2t], [t[:] for t in x2])
    otm = [S.sb(f"otm{j}", [128, D_MODEL], F32) for j in range(2)]
    ps_otr = [S.psum(f"ps_otr{j}", [128, 512]) for j in range(2)]
    for ls in range(L_SEQ // 128):
        buf = otm[ls % 2]
        for half in range(2):
            bank = ps_otr[half]
            for j in range(4):
                k = half * 4 + j
                S.aop("tensor", "transpose", bank[:, j * 128:(j + 1) * 128],
                      x2t[k][:, ls * 128:(ls + 1) * 128], idn_f[:],
                      r=[x2t[k], idn_f], w=[bank])
            S.aop("scalar", "activation", buf[:, half * 512:(half + 1) * 512],
                  bank[:], AF.Copy, r=[bank], w=[buf])
        S.dma("sync", out_o[ls * 128:(ls + 1) * 128, :], buf[:])
    S.flush_engine("sync")
    return S.build()


def build_l1():
    S = Sched("l1")
    nc = S.nc
    x_i = nc.dram_tensor("x", [L_SEQ, D_MODEL], F32, kind="ExternalInput").ap()
    ln_g = nc.dram_tensor("ln_g", [1, D_MODEL], F32, kind="ExternalInput").ap()
    ln_b = nc.dram_tensor("ln_b", [1, D_MODEL], F32, kind="ExternalInput").ap()
    W_in = nc.dram_tensor("W_in", [D_MODEL, 2 * D_INNER], F32, kind="ExternalInput").ap()
    conv_w = nc.dram_tensor("conv_w", [D_INNER, 4], F32, kind="ExternalInput").ap()
    conv_b = nc.dram_tensor("conv_b", [1, D_INNER], F32, kind="ExternalInput").ap()
    W_xp = nc.dram_tensor("W_xp", [D_INNER, 96], F32, kind="ExternalInput").ap()
    W_dt = nc.dram_tensor("W_dt", [64, D_INNER], F32, kind="ExternalInput").ap()
    b_dt = nc.dram_tensor("b_dt", [1, D_INNER], F32, kind="ExternalInput").ap()
    A_log = nc.dram_tensor("A_log", [D_INNER, 16], F32, kind="ExternalInput").ap()
    D_par = nc.dram_tensor("D_par", [1, D_INNER], F32, kind="ExternalInput").ap()
    res_sc = nc.dram_tensor("res_sc", [1, 1], F32, kind="ExternalInput").ap()
    W_out_h = nc.dram_tensor("W_out", [D_INNER, D_MODEL], F32, kind="ExternalInput").ap()
    x1n_o = nc.dram_tensor("x1n_o", [D_MODEL, L_SEQ], F32, kind="ExternalOutput").ap()
    mx_o = nc.dram_tensor("mx_o", [1, 1], F32, kind="ExternalOutput").ap()
    sp_u2 = nc.dram_tensor("sp_u2", [D_INNER, L_SEQ], F32, kind="Internal").ap()
    sp_dt = nc.dram_tensor("sp_dt", [D_INNER, L_SEQ], F32, kind="Internal").ap()
    sp_z = nc.dram_tensor("sp_z", [D_INNER, L_SEQ], F32, kind="Internal").ap()
    sp_x2 = nc.dram_tensor("sp_x2", [D_MODEL, L_SEQ], F32, kind="Internal").ap()

    NE = D_INNER // 128
    ND = D_MODEL // 128
    TL1 = 512
    NCH1 = L_SEQ // TL1
    TL2 = 256
    NCH2 = L_SEQ // TL2
    NST = 16

    idn_f = S.sb("idn_f", [128, 128], F32)
    ones1 = S.sb("ones1", [1, 128], F32)
    ones_col = S.sb("ones_col", [128, 1], F32)
    gcol = S.sb("gcol", [128, ND], F32)
    bcol = S.sb("bcol", [128, ND], F32)
    cw = S.sb("cw", [128, 4 * NE], F32)
    cb = S.sb("cb", [128, NE], F32)
    bdt = S.sb("bdt", [128, NE], F32)
    Dp = S.sb("Dp", [128, NE], F32)
    A_sb = S.sb("A_sb", [128, NST * NE], F32)
    rsc = S.sb("rsc", [128, 1], F32)
    rsc_st = S.sb("rsc_st", [1, 1], F32)
    bu_acc = S.sb("bu_acc", [128, 32], F32)
    butmp = S.sb("butmp", [128, 32], F32)
    wxp_bf = S.sb("wxp_bf", [128, 96 * NE], BF16)
    wdt_bf = S.sb("wdt_bf", [64, D_INNER], BF16)
    bc_sb = S.sb("bc_sb", [32, L_SEQ], F32)
    sel_sb = S.sb("sel_sb", [32, 32 * 128], F32)
    mxrun = S.sb("mxrun", [128, 1], F32)
    mxtmp = S.sb("mxtmp", [128, 1], F32)
    mrow = S.sb("mrow", [1, 128], F32)
    mfin = S.sb("mfin", [1, 1], F32)

    emit_identity(S, idn_f)
    S.aop("vector", "memset", ones1[:], 1.0, w=[ones1])
    S.aop("vector", "memset", ones_col[:], 1.0, w=[ones_col])
    S.aop("vector", "memset", mxrun[:], 0.0, w=[mxrun])
    S.aop("gpsimd", "memset", sel_sb[:], 0.0, w=[sel_sb])
    S.aop("gpsimd", "affine_select",
          out=sel_sb[:].rearrange("p (a b) -> p a b", a=32),
          in_=sel_sb[:].rearrange("p (a b) -> p a b", a=32),
          compare_op=ALU.not_equal, fill=1.0, base=0,
          pattern=[[-1, 32], [0, 128]], channel_multiplier=1,
          r=[sel_sb], w=[sel_sb])
    for k in range(ND):
        S.dma("sync", gcol[:, k:k + 1], ln_g[0:1, k * 128:(k + 1) * 128])
        S.dma("sync", bcol[:, k:k + 1], ln_b[0:1, k * 128:(k + 1) * 128])
    for t in range(NE):
        S.dma("sync", cw[:, 4 * t:4 * t + 4], conv_w[t * 128:(t + 1) * 128, :])
        S.dma("sync", cb[:, t:t + 1], conv_b[0:1, t * 128:(t + 1) * 128])
        S.dma("sync", bdt[:, t:t + 1], b_dt[0:1, t * 128:(t + 1) * 128])
        S.dma("sync", Dp[:, t:t + 1], D_par[0:1, t * 128:(t + 1) * 128])
        S.dma("sync", A_sb[:, NST * t:NST * (t + 1)],
              A_log[t * 128:(t + 1) * 128, :])
        S.dma("gpsimd", wxp_bf[:, 96 * t:96 * (t + 1)],
              W_xp[t * 128:(t + 1) * 128, :])
    S.dma("gpsimd", wdt_bf[:], W_dt[:])
    S.dma("sync", rsc_st[:], res_sc[:])
    S.aop("scalar", "activation", A_sb[:], A_sb[:], AF.Exp, r=[A_sb], w=[A_sb])
    S.aop("vector", "tensor_scalar", A_sb[:], A_sb[:], -1.0, None, ALU.mult,
          r=[A_sb], w=[A_sb])

    win_bf = [S.sb(f"winbf{k}", [128, 2 * D_INNER], BF16) for k in range(ND)]
    winst2 = [S.sb("winst0", [128, 2 * D_INNER], F32)] * 2
    xdbl_dt = S.sb("xdbl_dt", [64, L_SEQ], BF16)
    ps_a = S.psum("ps_a", [128, 512])
    ps_b = S.psum("ps_b", [128, 512])
    ps_c = S.psum("ps_c", [128, 512])
    ps_d = S.psum("ps_d", [128, 512])
    ps_e = S.psum("ps_e", [128, 512])
    ps_f = S.psum("ps_f", [128, 512])
    ps_bu = ps_e
    ps4 = [ps_a, ps_b, ps_c, ps_d]

    S.aop("tensor", "matmul", ps_bu[:, 0:1], ones1[0:1, :], rsc_st[0:1, 0:1],
          start=True, stop=True, r=[ones1, rsc_st], w=["psbu0"])
    S.aop("scalar", "activation", rsc[:], ps_bu[:, 0:1], AF.Copy,
          r=["psbu0"], w=[rsc])

    S.aop("vector", "memset", bu_acc[:], 0.0, w=[bu_acc])
    for k in range(ND):
        winst = winst2[k % 2]
        S.dma("gpsimd", winst[:], W_in[k * 128:(k + 1) * 128, :])
        for s in range(32):
            S.aop("tensor", "matmul", ps_bu[:, s:s + 1],
                  winst[:, s * 128:(s + 1) * 128], bcol[:, k:k + 1],
                  start=True, stop=True, r=[winst, bcol], w=[f"psbu{s}"])
        S.aop("scalar", "activation", butmp[:], ps_bu[:, 0:32], AF.Copy,
              r=[f"psbu{s}" for s in range(32)], w=[butmp])
        S.aop("vector", "tensor_add", bu_acc[:], bu_acc[:], butmp[:],
              r=[bu_acc, butmp], w=[bu_acc])
        S.aop("scalar", "activation", win_bf[k][:], winst[:], AF.Identity,
              scale=gcol[:, k:k + 1], r=[winst, gcol], w=[win_bf[k]])

    xtm = [S.sb(f"xtm{j}", [128, D_MODEL], F32) for j in range(3)]
    x_fm = [S.sb(f"xfm{k}", [128, TL1], F32) for k in range(ND)]
    xn_bf = [S.sb(f"xnbf{k}", [128, TL1], BF16) for k in range(ND)]
    bnraw = S.sb("bnraw", [128, 12], F32)
    bnagg = S.sb("bnagg", [128, 8], F32)
    x2st = S.sb("x2st", [128, TL1], F32)
    u_sb = [S.sb(f"usb{j}", [128, TL1 + 3], F32) for j in range(2)]
    u_tail = S.sb("u_tail", [128, 3 * NE], F32)
    cacc = S.sb("cacc", [128, TL1], F32)
    u2f = S.sb("u2f", [128, TL1], F32)
    u2bf = [S.sb(f"u2bf{t}", [128, TL1], BF16) for t in range(NE)]
    zs_sb = S.sb("zs_sb", [128, TL1], F32)
    dtf = S.sb("dtf", [128, TL1], F32)
    sp_t1 = S.sb("sp_t1", [128, TL1], F32)
    sp_t2 = S.sb("sp_t2", [128, TL1], F32)
    ln0 = PlainLN(S, ND, TL1, "ln0", ones1, ones_col, idn_f,
                  ps_banks=[ps_c, ps_d])
    S.aop("vector", "memset", u_tail[:], 0.0, w=[u_tail])

    for c in range(NCH1):
        c0 = c * TL1
        for j in range(4):
            stg = xtm[j % 3]
            S.dma("sync", stg[:], x_i[c0 + j * 128: c0 + (j + 1) * 128, :])
            S.aop("vector", "bn_stats", bnraw[:, 0:6], stg[:, 0:512],
                  r=[stg], w=[bnraw])
            S.aop("vector", "bn_stats", bnraw[:, 6:12], stg[:, 512:1024],
                  r=[stg, bnraw], w=[bnraw])
            S.aop("vector", "bn_aggr", bnagg[:, j:j + 5:4], bnraw[:],
                  r=[bnraw], w=[bnagg])
            for half in range(2):
                bank = ps_c if half == 0 else ps_d
                for kk in range(4):
                    k = half * 4 + kk
                    S.aop("tensor", "transpose", bank[:, kk * 128:(kk + 1) * 128],
                          stg[:, k * 128:(k + 1) * 128], idn_f[:],
                          r=[stg, idn_f], w=[bank])
                for kk in range(4):
                    k = half * 4 + kk
                    S.aop("vector", "tensor_copy",
                          x_fm[k][:, j * 128:(j + 1) * 128],
                          bank[:, kk * 128:(kk + 1) * 128],
                          r=[bank], w=[x_fm[k]])
        for k in range(ND):
            S.aop("vector", "tensor_scalar", x2st[:], x_fm[k][:], 2.0, None,
                  ALU.mult, r=[x_fm[k]], w=[x2st])
            S.dma("gpsimd", sp_x2[k * 128:(k + 1) * 128, c0:c0 + TL1], x2st[:])
        ln0.emit_from_stats([t[:] for t in x_fm], [t[:] for t in xn_bf],
                            bnagg[:, 0:4], bnagg[:, 4:8])
        for t in range(NE):
            bank = ps4[t % 4]
            pairs = [(win_bf[k][:, t * 128:(t + 1) * 128], xn_bf[k][:])
                     for k in range(ND)]
            S.mm_group(bank[:], pairs)
            usb = u_sb[t % 2]
            S.aop("vector", "tensor_copy", usb[:, 0:3], u_tail[:, 3 * t:3 * t + 3],
                  r=[u_tail], w=[usb])
            S.aop("scalar", "activation", usb[:, 3:3 + TL1], bank[:],
                  AF.Identity, bias=bu_acc[:, t:t + 1], r=[bank, bu_acc],
                  w=[usb])
            S.aop("vector", "tensor_copy", u_tail[:, 3 * t:3 * t + 3],
                  usb[:, TL1:TL1 + 3], r=[usb], w=[u_tail])
            S.aop("vector", "tensor_scalar", cacc[:], usb[:, 0:TL1],
                  cw[:, 4 * t:4 * t + 1], None, ALU.mult,
                  r=[usb, cw], w=[cacc])
            for q in range(1, 4):
                S.aop("vector", "scalar_tensor_tensor", out=cacc[:],
                      in0=usb[:, q:q + TL1], scalar=cw[:, 4 * t + q:4 * t + q + 1],
                      in1=cacc[:], op0=ALU.mult, op1=ALU.add,
                      r=[usb, cw, cacc], w=[cacc])
            S.aop("scalar", "activation", u2f[:], cacc[:], AF.Silu,
                  bias=cb[:, t:t + 1], r=[cacc, cb], w=[u2f])
            S.aop("vector", "tensor_copy", u2bf[t][:], u2f[:],
                  r=[u2f], w=[u2bf[t]])
            S.dma("gpsimd", sp_u2[t * 128:(t + 1) * 128, c0:c0 + TL1], u2f[:])
        for t in range(NE):
            bank = ps4[t % 4]
            pairs = [(win_bf[k][:, 2048 + t * 128:2048 + (t + 1) * 128],
                      xn_bf[k][:]) for k in range(ND)]
            S.mm_group(bank[:], pairs)
            S.aop("scalar", "activation", zs_sb[:], bank[:], AF.Silu,
                  bias=bu_acc[:, 16 + t:16 + t + 1], r=[bank, bu_acc], w=[zs_sb])
            S.dma("gpsimd", sp_z[t * 128:(t + 1) * 128, c0:c0 + TL1], zs_sb[:])
        pairs_dt = [(wxp_bf[:, 96 * t:96 * t + 64], u2bf[t][:]) for t in range(NE)]
        S.mm_group(ps_a[0:64, :], pairs_dt)
        S.aop("scalar", "activation", xdbl_dt[:, c0:c0 + TL1], ps_a[0:64, :],
              AF.Copy, r=[ps_a], w=[xdbl_dt])
        pairs_bc = [(wxp_bf[:, 96 * t + 64:96 * (t + 1)], u2bf[t][:])
                    for t in range(NE)]
        S.mm_group(ps_b[0:32, :], pairs_bc)
        S.aop("scalar", "activation", bc_sb[:, c0:c0 + TL1], ps_b[0:32, :],
              AF.Copy, r=[ps_b], w=[bc_sb])
        for t in range(NE):
            bank = ps_e if t % 2 == 0 else ps_f
            S.mm_group(bank[:], [(wdt_bf[:, t * 128:(t + 1) * 128],
                                  xdbl_dt[:, c0:c0 + TL1])])
            S.aop("scalar", "activation", sp_t1[:], bank[:], AF.Abs,
                  bias=bdt[:, t:t + 1], r=[bank, bdt], w=[sp_t1])
            S.aop("scalar", "activation", sp_t1[:], sp_t1[:], AF.Exp, scale=-1.0,
                  r=[sp_t1], w=[sp_t1])
            S.aop("scalar", "activation", sp_t1[:], sp_t1[:], AF.Ln, bias=1.0,
                  r=[sp_t1], w=[sp_t1])
            S.aop("scalar", "activation", sp_t2[:], bank[:], AF.Relu,
                  bias=bdt[:, t:t + 1], r=[bank, bdt], w=[sp_t2])
            S.aop("vector", "tensor_add", dtf[:], sp_t1[:], sp_t2[:],
                  r=[sp_t1, sp_t2], w=[dtf])
            S.dma("gpsimd", sp_dt[t * 128:(t + 1) * 128, c0:c0 + TL1], dtf[:])

    S.barrier()
    S.release(*win_bf, winst2[0], xdbl_dt, ps_a, ps_b, ps_c, ps_d,
              ps_e, ps_f,
              *xtm, *x_fm, *xn_bf, x2st, bnraw, bnagg,
              u_sb[0], u_sb[1], u_tail, cacc, u2f, *u2bf, zs_sb, dtf,
              sp_t1, sp_t2,
              ln0.ps_su, ln0.ps_tp, ln0.mu_cols,
              ln0.st_row, ln0.mu_bc, ln0.rstd_bc, ln0.var_cols, ln0.eps_col)

    wout_bf = [S.sb(f"wout{t}", [128, D_MODEL], BF16) for t in range(NE)]
    for t in range(NE):
        S.dma("gpsimd", wout_bf[t][:], W_out_h[t * 128:(t + 1) * 128, :])
    dt_c = S.sb("dt_c", [128, NE * TL2], F32)
    dtu_c = S.sb("dtu_c", [128, NE * TL2], F32)
    y_c = S.sb("y_c", [128, NE * TL2], F32)
    x2_c = S.sb("x2_c", [128, ND * TL2], F32)
    PITCH = TL2 + 1
    dA_g = [S.sb(f"dAg{j}", [128, NE * PITCH], F32) for j in range(2)]
    dtuB_g = S.sb("dtuB_g", [128, NE * PITCH], F32)
    h_sc = S.sb("h_sc", [128, NE * PITCH], F32)
    h_g = dtuB_g
    carry = S.sb("carry", [128, NST * NE], F32)
    bcn = [S.sb(f"bcn{j}", [128, 2 * TL2], F32) for j in range(4)]
    x1_c = S.sb("x1_c", [128, ND * TL2], F32)
    x1n_c = x2_c
    ps_bcn = [S.psum(f"ps_bcn{j}", [128, 2 * TL2]) for j in range(2)]
    ps_wo = [S.psum(f"ps_wo{j}", [128, TL2]) for j in range(2)]
    ln1 = PlainLN(S, ND, TL2, "ln1", ones1, ones_col, idn_f,
                  ps_banks=[ps_bcn[0], ps_bcn[1]])
    S.aop("vector", "memset", carry[:], 0.0, w=[carry])
    S.aop("vector", "memset", dA_g[0][:], 0.0, w=[dA_g[0]])
    S.aop("vector", "memset", dA_g[1][:], 0.0, w=[dA_g[1]])
    S.aop("vector", "memset", dtuB_g[:, NE * PITCH - 1:NE * PITCH], 0.0,
          w=[dtuB_g])

    def vp(t3, width=TL2):
        return t3[:].rearrange("p (a b) -> p a b", a=NE)[:, :, 0:width]
    y2bf = dtu_c

    def v3(t3):
        return t3[:].rearrange("p (a b) -> p a b", a=NE)

    for c in range(NCH2):
        c0 = c * TL2
        S.dma("sync", dt_c[:].rearrange("p (a b) -> p a b", a=NE),
              sp_dt[:, c0:c0 + TL2].rearrange("(a p) b -> p a b", p=128))
        S.dma("sync", vp(h_g),
              sp_u2[:, c0:c0 + TL2].rearrange("(a p) b -> p a b", p=128))
        S.dma("sync", x2_c[:].rearrange("p (a b) -> p a b", a=ND),
              sp_x2[:, c0:c0 + TL2].rearrange("(a p) b -> p a b", p=128))
        S.aop("vector", "tensor_mul", out=v3(dtu_c), in0=v3(dt_c),
              in1=vp(h_g), op=None, r=[dt_c, h_g], w=[dtu_c]) if False else \
        S.aop("vector", "tensor_tensor", out=v3(dtu_c), in0=v3(dt_c),
              in1=vp(h_g), op=ALU.mult, r=[dt_c, h_g], w=[dtu_c])
        for n in range(NST):
            if n % 4 == 0:
                for nn in range(n, n + 4):
                    pb = ps_bcn[nn % 2]
                    S.aop("tensor", "matmul", pb[:, 0:TL2],
                          sel_sb[:, nn * 128:(nn + 1) * 128],
                          bc_sb[:, c0:c0 + TL2], start=True, stop=True,
                          r=[sel_sb, bc_sb], w=[pb])
                    S.aop("tensor", "matmul", pb[:, TL2:2 * TL2],
                          sel_sb[:, (16 + nn) * 128:(17 + nn) * 128],
                          bc_sb[:, c0:c0 + TL2], start=True, stop=True,
                          r=[sel_sb, bc_sb], w=[pb])
                    S.aop("scalar", "activation", bcn[nn % 4][:], pb[:], AF.Copy,
                          r=[pb], w=[bcn[nn % 4]])
            bb = bcn[n % 4]
            dag = dA_g[n % 2]
            for t in range(NE):
                S.aop("scalar", "activation",
                      dag[:, t * PITCH:t * PITCH + TL2],
                      dt_c[:, t * TL2:(t + 1) * TL2], AF.Exp,
                      scale=A_sb[:, NST * t + n:NST * t + n + 1],
                      r=[dt_c, A_sb], w=[dag])
            S.aop("vector", "tensor_tensor", out=vp(dtuB_g), in0=v3(dtu_c),
                  in1=bb[:, 0:TL2].unsqueeze(1).broadcast_to([128, NE, TL2]),
                  op=ALU.mult, r=[dtu_c, bb], w=[dtuB_g])
            S.aop("vector", "tensor_copy",
                  dtuB_g[:].rearrange("p (a b) -> p a b", a=NE)[:, 0:NE - 1,
                                                               TL2:TL2 + 1],
                  carry[:, n * NE + 1:(n + 1) * NE].unsqueeze(2),
                  r=[carry], w=[dtuB_g])
            S.aop("vector", "tensor_tensor_scan",
                  h_sc[:], dag[:], dtuB_g[:],
                  carry[:, n * NE:n * NE + 1], ALU.mult, ALU.add,
                  r=[dag, dtuB_g, carry], w=[h_sc])
            S.aop("vector", "tensor_copy",
                  carry[:, n * NE:(n + 1) * NE],
                  h_sc[:].rearrange("p (a b) -> p a b", a=NE)[:, :,
                      TL2 - 1:TL2].rearrange("p a b -> p (a b)"),
                  r=[h_sc], w=[carry])
            if n == 0:
                S.aop("vector", "tensor_tensor", out=v3(y_c), in0=vp(h_sc),
                      in1=bb[:, TL2:2 * TL2].unsqueeze(1).broadcast_to(
                          [128, NE, TL2]),
                      op=ALU.mult, r=[h_sc, bb], w=[y_c])
            else:
                S.aop("vector", "tensor_tensor", out=vp(dtuB_g), in0=vp(h_sc),
                      in1=bb[:, TL2:2 * TL2].unsqueeze(1).broadcast_to(
                          [128, NE, TL2]),
                      op=ALU.mult, r=[h_sc, bb], w=[dtuB_g])
                S.aop("vector", "tensor_add", out=v3(y_c), in0=v3(y_c),
                      in1=vp(dtuB_g), r=[y_c, dtuB_g], w=[y_c])
        S.dma("sync", vp(h_g),
              sp_u2[:, c0:c0 + TL2].rearrange("(a p) b -> p a b", p=128))
        S.dma("sync", dt_c[:].rearrange("p (a b) -> p a b", a=NE),
              sp_z[:, c0:c0 + TL2].rearrange("(a p) b -> p a b", p=128))
        for t in range(NE):
            S.aop("scalar", "activation", h_sc[:, t * TL2:(t + 1) * TL2],
                  h_g[:, t * PITCH:t * PITCH + TL2], AF.Identity,
                  scale=Dp[:, t:t + 1], r=[h_g, Dp], w=[h_sc])
        S.aop("vector", "tensor_add", y_c[:], y_c[:], h_sc[:, 0:NE * TL2],
              r=[y_c, h_sc], w=[y_c])
        y2v = y2bf[:].bitcast(BF16)[:, 0:NE * TL2]
        S.aop("vector", "tensor_mul", y2v, y_c[:], dt_c[:], r=[y_c, dt_c],
              w=["y2bf", y2bf])
        for m in range(ND):
            bank = ps_wo[m % 2]
            pairs = [(wout_bf[t][:, m * 128:(m + 1) * 128],
                      y2v[:, t * TL2:(t + 1) * TL2]) for t in range(NE)]
            S.mm_group(bank[:], pairs, r_extra=["y2bf"])
            S.aop("vector", "scalar_tensor_tensor",
                  out=x1_c[:, m * TL2:(m + 1) * TL2], in0=bank[:],
                  scalar=rsc[:, 0:1], in1=x2_c[:, m * TL2:(m + 1) * TL2],
                  op0=ALU.mult, op1=ALU.add, r=[bank, rsc, x2_c], w=[x1_c])
        ln1.emit([x1_c[:, m * TL2:(m + 1) * TL2] for m in range(ND)],
                 [x1n_c[:, m * TL2:(m + 1) * TL2] for m in range(ND)],
                 [x1_c[:, m * TL2:(m + 1) * TL2] for m in range(ND)])
        S.dma("gpsimd", x1n_o[:, c0:c0 + TL2].rearrange("(a p) b -> p a b", p=128),
              x1n_c[:].rearrange("p (a b) -> p a b", a=ND))
        S.aop("vector", "tensor_reduce", mxtmp[:], x1n_c[:], AX, ALU.max,
              apply_absolute_value=True, r=[x1n_c], w=[mxtmp])
        S.aop("vector", "tensor_tensor", mxrun[:], mxrun[:], mxtmp[:], ALU.max,
              r=[mxrun, mxtmp], w=[mxrun])

    S.aop("tensor", "transpose", ln1.ps_tp[0:1, 0:128], mxrun[:], idn_f[:],
          r=[mxrun, idn_f], w=[ln1.ps_tp])
    S.aop("scalar", "activation", mrow[:], ln1.ps_tp[0:1, 0:128], AF.Copy,
          r=[ln1.ps_tp], w=[mrow])
    S.aop("vector", "tensor_reduce", mfin[:], mrow[:], AX, ALU.max, r=[mrow],
          w=[mfin])
    S.dma("sync", mx_o[:], mfin[:])
    S.flush_engine("sync")
    return S.build()



_CACHE = {}


def _progs():
    if "p" not in _CACHE:
        _CACHE["p"] = (build_l1(), build_l2(), build_l3(), build_l4(), build_l5())
    return _CACHE["p"]


def kernel(L0_attn_x, L0_attn_ln_g, L0_attn_ln_b, L0_attn_W_in, L0_attn_conv_w,
           L0_attn_conv_b, L0_attn_W_xp, L0_attn_W_dt, L0_attn_b_dt,
           L0_attn_A_log, L0_attn_D_param, L0_attn_W_out, L0_attn_res_scale,
           L0_ffn_w_gate, L0_ffn_w_up, L0_ffn_w_down):
    n = 8
    nc1, nc2, nc3, nc4, nc5 = _progs()
    cores = list(range(n))
    c = np.ascontiguousarray

    im1 = [{
        "x": c(L0_attn_x[i]),
        "ln_g": c(L0_attn_ln_g.reshape(1, -1)), "ln_b": c(L0_attn_ln_b.reshape(1, -1)),
        "W_in": c(L0_attn_W_in), "conv_w": c(L0_attn_conv_w),
        "conv_b": c(L0_attn_conv_b.reshape(1, -1)), "W_xp": c(L0_attn_W_xp),
        "W_dt": c(L0_attn_W_dt), "b_dt": c(L0_attn_b_dt.reshape(1, -1)),
        "A_log": c(L0_attn_A_log), "D_par": c(L0_attn_D_param.reshape(1, -1)),
        "res_sc": c(L0_attn_res_scale.reshape(1, 1)), "W_out": c(L0_attn_W_out),
    } for i in range(n)]
    r1 = run_bass_kernel_spmd(nc1, im1, core_ids=cores).results
    mx8 = np.concatenate([r1[i]["mx_o"] for i in range(n)], axis=1)

    im2 = [{"x1n": r1[i]["x1n_o"], "mx8": mx8,
            "w_up": c(L0_ffn_w_up), "w_gate": c(L0_ffn_w_gate)} for i in range(n)]
    r2 = run_bass_kernel_spmd(nc2, im2, core_ids=cores).results
    mup8 = np.concatenate([r2[i]["m_up"] for i in range(n)], axis=1)
    mg8 = np.concatenate([r2[i]["m_gate"] for i in range(n)], axis=1)

    im3 = [{"up_i": r2[i]["up_o"], "gate_i": r2[i]["gate_o"],
            "mup8": mup8, "mg8": mg8} for i in range(n)]
    r3 = run_bass_kernel_spmd(nc3, im3, core_ids=cores).results
    mg28 = np.concatenate([r3[i]["m_g2"] for i in range(n)], axis=1)

    im4 = [{"upq_i": r3[i]["upq_o"], "g2_i": r3[i]["gate2_o"], "mg28": mg28}
           for i in range(n)]
    r4 = run_bass_kernel_spmd(nc4, im4, core_ids=cores).results
    mo8 = np.concatenate([r4[i]["m_o"] for i in range(n)], axis=1)

    im5 = [{"o_i": r4[i]["o_o"], "x1n": r1[i]["x1n_o"], "mo8": mo8,
            "w_down": c(L0_ffn_w_down)} for i in range(n)]
    r5 = run_bass_kernel_spmd(nc5, im5, core_ids=cores).results

    return np.stack([r5[i]["out_o"] for i in range(n)], axis=0)


# version 11
# speedup vs baseline: 1.0001x; 1.0001x over previous
import numpy as np
from contextlib import ExitStack
import concourse.bass as bass
import concourse.mybir as mybir
from concourse.bass_utils import run_bass_kernel_spmd

F32 = mybir.dt.float32
BF16 = mybir.dt.bfloat16
AF = mybir.ActivationFunctionType
ALU = mybir.AluOpType
AX = mybir.AxisListType.X
C_RND = 12582912.0
D_MODEL = 1024
D_INNER = 2048
D_FFN = 3072
L_SEQ = 2048
ENGINES = ("sync", "scalar", "vector", "tensor", "gpsimd")


def _key(x):
    if isinstance(x, str):
        return x
    t = getattr(x, "tensor", None)
    if t is not None:
        return t.name
    return x.name


class Sched:

    def __init__(self, name):
        self.nc = bass.Bass("TRN2", target_bir_lowering=False, debug=False)
        self.es = ExitStack()
        self.q = {e: [] for e in ENGINES}
        self.sems = {}
        self.last_w = {}
        self.readers = {}
        self.waited = {e: {} for e in ENGINES}
        for e in ENGINES:
            self._mk_sem("q_" + e)
        self.dma_issued = {e: {} for e in ENGINES}
        self.nops = 0

    def _mk_sem(self, name):
        h = self.es.enter_context(self.nc.semaphore(name))
        self.sems[name] = [h, 0]

    def sb(self, name, shape, dt):
        cm = self.nc.sbuf_tensor(name, shape, dt)
        h = cm.__enter__()
        self._cms = getattr(self, "_cms", {})
        self._alloc_idx = getattr(self, "_alloc_idx", 0) + 1
        self._cms[name] = (self._alloc_idx, cm)
        return h

    def psum(self, name, shape, dt=F32):
        cm = self.nc.psum_tensor(name, shape, dt)
        h = cm.__enter__()
        self._cms = getattr(self, "_cms", {})
        self._alloc_idx = getattr(self, "_alloc_idx", 0) + 1
        self._cms[name] = (self._alloc_idx, cm)
        return h

    def release(self, *handles):
        items = sorted((self._cms[h.name][0], h.name) for h in handles)
        for _, name in reversed(items):
            self._cms.pop(name)[1].__exit__(None, None, None)

    def val(self, sem):
        return self.sems[sem][1]

    def _emit_wait(self, eng, sem_name, v):
        if v is None or v <= 0:
            return
        wd = self.waited[eng]
        if wd.get(sem_name, 0) >= v:
            return
        wd[sem_name] = v
        h = self.sems[sem_name][0]
        self.q[eng].append(lambda e, h=h, v=v: e.wait_ge(h, v))

    def _dep_waits(self, eng, r, w):
        need = {}
        for x in r:
            k = _key(x)
            lw = self.last_w.get(k)
            if lw:
                need[lw[0]] = max(need.get(lw[0], 0), lw[1])
        for x in w:
            k = _key(x)
            lw = self.last_w.get(k)
            if lw:
                need[lw[0]] = max(need.get(lw[0], 0), lw[1])
            for sn, v in self.readers.get(k, {}).items():
                need[sn] = max(need.get(sn, 0), v)
        for sn, v in need.items():
            self._emit_wait(eng, sn, v)

    def _dep_update(self, sem_name, v, r, w):
        for x in w:
            k = _key(x)
            self.last_w[k] = (sem_name, v)
            self.readers[k] = {}
        for x in r:
            k = _key(x)
            d = self.readers.setdefault(k, {})
            d[sem_name] = max(d.get(sem_name, 0), v)

    def aop(self, eng, meth, *args, r=(), w=(), sem_name=None, k=1, **kw):
        self._dep_waits(eng, r, w)
        if sem_name is None:
            sem_name = "q_" + eng
        s = self.sems[sem_name]
        s[1] += k
        v = s[1]
        h = s[0]

        def fn(e, meth=meth, args=args, kw=kw, h=h, k=k):
            getattr(e, meth)(*args, **kw).then_inc(h, k)
        self.q[eng].append(fn)
        self._dep_update(sem_name, v, r, w)
        self.nops += 1
        return sem_name, v

    def dma(self, eng, out, in_, **kw):
        sb_side = out if out.space == bass.MemorySpace.SBUF else in_
        sem_name = "dma_" + eng + "_" + _key(sb_side)
        if sem_name not in self.sems:
            self._mk_sem(sem_name)
        v = self.aop(eng, "dma_start", r=[in_], w=[out], sem_name=sem_name,
                     k=16, out=out, in_=in_, **kw)
        d = self.dma_issued[eng]
        d[sem_name] = self.sems[sem_name][1]
        return v

    def mm_group(self, psum_ap, pairs, r_extra=(), eng="tensor"):
        nk = len(pairs)
        r = ([p[0] for p in pairs] + [p[1] for p in pairs] + list(r_extra))
        w = [psum_ap]
        self._dep_waits(eng, r, w)
        sem_name = "q_" + eng
        s = self.sems[sem_name]
        s[1] += 1
        v = s[1]
        h = s[0]

        def fn(e, pairs=pairs, psum_ap=psum_ap, h=h, nk=nk):
            ins = None
            for j, (lhsT, rhs) in enumerate(pairs):
                ins = e.matmul(psum_ap, lhsT, rhs, start=(j == 0),
                               stop=(j == nk - 1))
            ins.then_inc(h, 1)
        self.q[eng].append(fn)
        self._dep_update(sem_name, v, r, w)
        self.nops += nk
        return sem_name, v

    def raw(self, eng, fn):
        self.q[eng].append(fn)

    def flush_engine(self, eng):
        self._emit_wait(eng, "q_" + eng, self.sems["q_" + eng][1])
        for sn, v in self.dma_issued[eng].items():
            self._emit_wait(eng, sn, v)

    def barrier(self, engines=ENGINES):
        if "bar" not in self.sems:
            self._mk_sem("bar")
        s = self.sems["bar"]
        tgt = s[1] + len(engines)
        for e in engines:
            self.flush_engine(e)
            h = s[0]
            self.q[e].append(lambda e_, h=h: e_.sem_inc(h, 1))
        s[1] = tgt
        for e in engines:
            self._emit_wait(e, "bar", tgt)
        self.last_w = {k: ("bar", tgt) for k in self.last_w}
        self.readers = {}

    def build(self):
        with self.nc.Block() as block:
            for eng in ENGINES:
                ops = self.q[eng]
                if not ops:
                    continue
                getattr(block, eng)(lambda e, ops=ops: [fn(e) for fn in ops])
        for _, (_, cm) in sorted(getattr(self, "_cms", {}).items(),
                                 key=lambda kv: -kv[1][0]):
            cm.__exit__(None, None, None)
        self._cms = {}
        self.es.close()
        return self.nc


def emit_identity(S, idn, fill=1.0):
    S.aop("gpsimd", "memset", idn[:], 0.0, w=[idn])
    S.aop("gpsimd", "affine_select", out=idn[:], in_=idn[:],
          compare_op=ALU.not_equal, fill=fill, base=0,
          pattern=[[-1, idn.shape[0]]], channel_multiplier=1, r=[idn], w=[idn])


def _rep(h, nseg, reps):
    return h[:].unsqueeze(2).broadcast_to([128, nseg, reps])


def emit_scale_bcast(S, mx_ap, sca, sca_bc, ones1, ps_bc, rq_cell, mx_keys=()):
    S.aop("vector", "tensor_reduce", sca[0:1, 0:1], mx_ap, AX, ALU.max,
          r=[mx_ap] + list(mx_keys), w=[sca])
    S.aop("vector", "tensor_scalar", sca[0:1, 0:1], sca[0:1, 0:1],
          rq_cell[0:1, 0:1], 1e-12, ALU.mult, ALU.max,
          r=[sca, rq_cell], w=[sca])
    S.aop("vector", "reciprocal", sca[0:1, 1:2], sca[0:1, 0:1], r=[sca], w=[sca])
    S.aop("tensor", "matmul", ps_bc[:, 0:2], ones1[0:1, :], sca[0:1, 0:2],
          start=True, stop=True, r=[ones1, sca], w=[ps_bc])
    S.aop("scalar", "activation", sca_bc[:, 0:2], ps_bc[:, 0:2], AF.Copy,
          r=[ps_bc], w=[sca_bc])


def emit_consts(S):
    ones1 = S.sb("ones1", [1, 128], F32)
    rq_cell = S.sb("rq_cell", [1, 1], F32)
    r7 = S.sb("r7", [128, 1], F32)
    S.aop("vector", "memset", ones1[:], 1.0, w=[ones1])
    S.aop("vector", "memset", rq_cell[:], 32767.0, w=[rq_cell])
    S.aop("vector", "reciprocal", rq_cell[:], rq_cell[:], r=[rq_cell], w=[rq_cell])
    S.aop("vector", "memset", r7[:], 7.0, w=[r7])
    S.aop("vector", "reciprocal", r7[:], r7[:], r=[r7], w=[r7])
    return ones1, rq_cell, r7


def emit_qdq_rounds(S, out_ap, in_ap, sca_bc, tmp_ap, r_keys, w_keys):
    S.aop("vector", "tensor_scalar", tmp_ap, in_ap, sca_bc[:, 1:2], C_RND,
          ALU.mult, ALU.add, r=list(r_keys) + [sca_bc], w=[tmp_ap])
    S.aop("vector", "tensor_scalar", out_ap, tmp_ap, C_RND, sca_bc[:, 0:1],
          ALU.subtract, ALU.mult, r=[tmp_ap, sca_bc], w=list(w_keys))


def build_l2():
    S = Sched("l2")
    nc = S.nc
    x1n = nc.dram_tensor("x1n", [D_MODEL, L_SEQ], F32, kind="ExternalInput").ap()
    mx8 = nc.dram_tensor("mx8", [1, 8], F32, kind="ExternalInput").ap()
    w_hbm = [nc.dram_tensor("w_up", [D_FFN, D_MODEL], F32, kind="ExternalInput").ap(),
             nc.dram_tensor("w_gate", [D_FFN, D_MODEL], F32, kind="ExternalInput").ap()]
    out_hbm = [nc.dram_tensor("up_o", [D_FFN, L_SEQ], F32, kind="ExternalOutput").ap(),
               nc.dram_tensor("gate_o", [D_FFN, L_SEQ], F32, kind="ExternalOutput").ap()]
    m_o = [nc.dram_tensor("m_up", [1, 1], F32, kind="ExternalOutput").ap(),
           nc.dram_tensor("m_gate", [1, 1], F32, kind="ExternalOutput").ap()]

    ND = D_MODEL // 128
    NI = D_FFN // 128

    idn_bf = S.sb("idn_bf", [128, 128], BF16)
    idn_f = S.sb("idn_f", [128, 128], F32)
    mx_sb = S.sb("mx_sb", [1, 8], F32)
    sca = S.sb("sca", [1, 2], F32)
    sca_bc = S.sb("sca_bc", [128, 2], F32)
    xq = [S.sb(f"xq{k}", [128, L_SEQ], BF16) for k in range(ND)]
    xstage = [S.sb(f"xstage{j}", [128, L_SEQ], F32) for j in range(2)]
    xtmp = S.sb("xtmp", [128, L_SEQ], F32)
    wstage = [S.sb(f"wstage{j}", [128, D_MODEL], F32) for j in range(2)]
    wq_bufs = [S.sb(f"wqb{j}", [128, D_MODEL], BF16) for j in range(2)]
    sblk = S.sb("sblk", [128, 32], F32)
    rblk = S.sb("rblk", [128, 32], F32)
    wr_t = S.sb("wr_t", [128, D_MODEL], F32)
    wqT = [[S.sb(f"wqT{w}_{k}", [128, D_FFN], BF16) for k in range(ND)]
           for w in range(2)]
    ostage = [S.sb(f"ostage{j}", [128, L_SEQ], F32) for j in range(2)]
    mrun = S.sb("mrun", [128, 2], F32)
    mtmp = S.sb("mtmp", [128, 1], F32)
    mrow = S.sb("mrow", [2, 128], F32)
    mfin = S.sb("mfin", [2, 1], F32)

    ps_bc = S.psum("ps_bc", [128, 2])
    ps_tr = [S.psum(f"ps_tr{j}", [128, 512], BF16) for j in range(2)]
    ps_mm = [S.psum(f"ps_mm{j}", [128, 512]) for j in range(2)]
    ps_fin = S.psum("ps_fin", [2, 128])

    emit_identity(S, idn_bf)
    emit_identity(S, idn_f)
    ones1, rq_cell, r7 = emit_consts(S)
    S.aop("vector", "memset", mrun[:], 0.0, w=[mrun])
    S.dma("sync", mx_sb[:], mx8[:])

    emit_scale_bcast(S, mx_sb[0:1, 0:8], sca, sca_bc, ones1, ps_bc, rq_cell)

    for k in range(ND):
        stage = xstage[k % 2]
        S.dma("sync", stage[:], x1n[k * 128:(k + 1) * 128, :])
        emit_qdq_rounds(S, xq[k][:], stage[:], sca_bc, xtmp[:],
                        r_keys=[stage], w_keys=[xq[k]])

    for w in range(2):
        for t in range(NI):
            st = wstage[t % 2]
            S.dma("sync", st[:], w_hbm[w][t * 128:(t + 1) * 128, :])
            S.aop("vector", "tensor_reduce", sblk[:],
                  st[:].rearrange("p (a b) -> p a b", a=32), AX, ALU.max,
                  apply_absolute_value=True, r=[st], w=[sblk])
            S.aop("vector", "tensor_scalar", sblk[:], sblk[:], r7[:, 0:1], 1e-12,
                  ALU.mult, ALU.max, r=[sblk, r7], w=[sblk])
            S.aop("vector", "reciprocal", rblk[:], sblk[:], r=[sblk], w=[rblk])
            S.aop("vector", "tensor_tensor",
                  out=wr_t[:].rearrange("p (a b) -> p a b", a=32),
                  in0=st[:].rearrange("p (a b) -> p a b", a=32),
                  in1=_rep(rblk, 32, 32), op=ALU.mult,
                  r=[st, rblk], w=[wr_t])
            S.aop("vector", "tensor_scalar", wr_t[:], wr_t[:], C_RND, C_RND,
                  ALU.add, ALU.subtract, r=[wr_t], w=[wr_t])
            wq = wq_bufs[t % 2]
            S.aop("vector", "tensor_tensor",
                  out=wq[:].rearrange("p (a b) -> p a b", a=32),
                  in0=wr_t[:].rearrange("p (a b) -> p a b", a=32),
                  in1=_rep(sblk, 32, 32), op=ALU.mult,
                  r=[wr_t, sblk], w=[wq])
            for half in range(2):
                bank = ps_tr[half]
                for j in range(4):
                    kblk = half * 4 + j
                    S.aop("tensor", "transpose", bank[:, j * 128:(j + 1) * 128],
                          wq[:, kblk * 128:(kblk + 1) * 128], idn_bf[:],
                          r=[wq, idn_bf], w=[bank])
                for j in range(4):
                    kblk = half * 4 + j
                    S.aop("scalar", "activation",
                          wqT[w][kblk][:, t * 128:(t + 1) * 128],
                          bank[:, j * 128:(j + 1) * 128], AF.Copy,
                          r=[bank], w=[f"wqT{w}_{kblk}:{t}"])
    for w in range(2):
        for t in range(NI):
            stg = ostage[t % 2]
            for lc in range(4):
                bank = ps_mm[lc % 2]
                pairs = [(wqT[w][k][:, t * 128:(t + 1) * 128],
                          xq[k][:, lc * 512:(lc + 1) * 512]) for k in range(ND)]
                S.mm_group(bank[:], pairs,
                           r_extra=[f"wqT{w}_{k}:{t}" for k in range(ND)])
                S.aop("scalar", "activation", stg[:, lc * 512:(lc + 1) * 512],
                      bank[:], AF.Copy, r=[bank], w=[stg])
            S.dma("sync", out_hbm[w][t * 128:(t + 1) * 128, :], stg[:])
            S.aop("vector", "tensor_reduce", mtmp[:], stg[:], AX, ALU.max,
                  apply_absolute_value=True, r=[stg], w=[mtmp])
            S.aop("vector", "tensor_tensor", mrun[:, w:w + 1], mrun[:, w:w + 1],
                  mtmp[:], ALU.max, r=[mrun, mtmp], w=[mrun])

    S.aop("tensor", "transpose", ps_fin[:], mrun[:, 0:2], idn_f[:],
          r=[mrun, idn_f], w=[ps_fin])
    S.aop("scalar", "activation", mrow[:], ps_fin[:], AF.Copy, r=[ps_fin], w=[mrow])
    S.aop("vector", "tensor_reduce", mfin[:], mrow[:], AX, ALU.max,
          r=[mrow], w=[mfin])
    S.dma("sync", m_o[0][:], mfin[0:1, :])
    S.dma("sync", m_o[1][:], mfin[1:2, :])
    S.flush_engine("sync")
    return S.build()


def host_l2_inputs(x1n_fm, mx8, w_up, w_gate):
    return {"x1n": np.ascontiguousarray(x1n_fm), "mx8": mx8,
            "w_up": w_up, "w_gate": w_gate}


def build_l3():
    S = Sched("l3")
    nc = S.nc
    up_i = nc.dram_tensor("up_i", [D_FFN, L_SEQ], F32, kind="ExternalInput").ap()
    gate_i = nc.dram_tensor("gate_i", [D_FFN, L_SEQ], F32, kind="ExternalInput").ap()
    mup8 = nc.dram_tensor("mup8", [1, 8], F32, kind="ExternalInput").ap()
    mg8 = nc.dram_tensor("mg8", [1, 8], F32, kind="ExternalInput").ap()
    upq_o = nc.dram_tensor("upq_o", [D_FFN, L_SEQ], F32, kind="ExternalOutput").ap()
    gate2_o = nc.dram_tensor("gate2_o", [D_FFN, L_SEQ], F32, kind="ExternalOutput").ap()
    m_g2 = nc.dram_tensor("m_g2", [1, 1], F32, kind="ExternalOutput").ap()
    NI = D_FFN // 128

    idn_f = S.sb("idn_f", [128, 128], F32)
    mx_sb = S.sb("mx_sb", [1, 16], F32)
    sca_u = S.sb("sca_u", [1, 2], F32)
    sca_g = S.sb("sca_g", [1, 2], F32)
    scb_u = S.sb("scb_u", [128, 2], F32)
    scb_g = S.sb("scb_g", [128, 2], F32)
    ustage = [S.sb(f"ustage{j}", [128, L_SEQ], F32) for j in range(3)]
    gstage = [S.sb(f"gstage{j}", [128, L_SEQ], F32) for j in range(3)]
    uq = [S.sb(f"uq{j}", [128, L_SEQ], F32) for j in range(3)]
    gq = [S.sb(f"gq{j}", [128, L_SEQ], F32) for j in range(3)]
    sig_t = S.sb("sig_t", [128, L_SEQ], F32)
    g2 = [S.sb(f"g2_{j}", [128, L_SEQ], F32) for j in range(3)]
    xtmp = S.sb("xtmp", [128, L_SEQ], F32)
    mrun = S.sb("mrun", [128, 1], F32)
    mtmp = S.sb("mtmp", [128, 1], F32)
    mrow = S.sb("mrow", [1, 128], F32)
    mfin = S.sb("mfin", [1, 1], F32)
    ps_bc = S.psum("ps_bc", [128, 4])
    ps_fin = S.psum("ps_fin", [1, 128])

    emit_identity(S, idn_f)
    ones1, rq_cell, r7 = emit_consts(S)
    S.aop("vector", "memset", mrun[:], 0.0, w=[mrun])
    S.dma("sync", mx_sb[0:1, 0:8], mup8[:])
    S.dma("sync", mx_sb[0:1, 8:16], mg8[:])
    emit_scale_bcast(S, mx_sb[0:1, 0:8], sca_u, scb_u, ones1, ps_bc[:, 0:2], rq_cell)
    emit_scale_bcast(S, mx_sb[0:1, 8:16], sca_g, scb_g, ones1, ps_bc[:, 2:4], rq_cell)

    for t in range(NI):
        ust = ustage[t % 3]
        gst = gstage[t % 3]
        S.dma("sync", ust[:], up_i[t * 128:(t + 1) * 128, :])
        S.dma("gpsimd", gst[:], gate_i[t * 128:(t + 1) * 128, :])
        emit_qdq_rounds(S, uq[t % 3][:], ust[:], scb_u, xtmp[:],
                        r_keys=[ust], w_keys=[uq[t % 3]])
        S.dma("scalar", upq_o[t * 128:(t + 1) * 128, :], uq[t % 3][:])
        emit_qdq_rounds(S, gq[t % 3][:], gst[:], scb_g, xtmp[:],
                        r_keys=[gst], w_keys=[gq[t % 3]])
        S.aop("scalar", "activation", sig_t[:], gq[t % 3][:], AF.Sigmoid,
              r=[gq[t % 3]], w=[sig_t])
        S.aop("vector", "tensor_scalar", sig_t[:], sig_t[:], 65536.0, C_RND,
              ALU.mult, ALU.add, r=[sig_t], w=[sig_t])
        S.aop("vector", "tensor_scalar", sig_t[:], sig_t[:], C_RND, 65535.0,
              ALU.subtract, ALU.min, r=[sig_t], w=[sig_t])
        S.aop("vector", "scalar_tensor_tensor", out=g2[t % 3][:], in0=sig_t[:],
              scalar=1.0 / 65536.0, in1=gq[t % 3][:], op0=ALU.mult, op1=ALU.mult,
              r=[sig_t, gq[t % 3]], w=[g2[t % 3]])
        S.dma("scalar", gate2_o[t * 128:(t + 1) * 128, :], g2[t % 3][:])
        S.aop("vector", "tensor_reduce", mtmp[:], g2[t % 3][:], AX, ALU.max,
              apply_absolute_value=True, r=[g2[t % 3]], w=[mtmp])
        S.aop("vector", "tensor_tensor", mrun[:], mrun[:], mtmp[:], ALU.max,
              r=[mrun, mtmp], w=[mrun])

    S.aop("tensor", "transpose", ps_fin[:], mrun[:], idn_f[:],
          r=[mrun, idn_f], w=[ps_fin])
    S.aop("scalar", "activation", mrow[:], ps_fin[:], AF.Copy, r=[ps_fin], w=[mrow])
    S.aop("vector", "tensor_reduce", mfin[:], mrow[:], AX, ALU.max, r=[mrow], w=[mfin])
    S.dma("sync", m_g2[:], mfin[:])
    S.flush_engine("sync")
    return S.build()


def build_l4():
    S = Sched("l4")
    nc = S.nc
    upq_i = nc.dram_tensor("upq_i", [D_FFN, L_SEQ], F32, kind="ExternalInput").ap()
    g2_i = nc.dram_tensor("g2_i", [D_FFN, L_SEQ], F32, kind="ExternalInput").ap()
    mg28 = nc.dram_tensor("mg28", [1, 8], F32, kind="ExternalInput").ap()
    o_o = nc.dram_tensor("o_o", [D_FFN, L_SEQ], F32, kind="ExternalOutput").ap()
    m_o = nc.dram_tensor("m_o", [1, 1], F32, kind="ExternalOutput").ap()
    NI = D_FFN // 128

    idn_f = S.sb("idn_f", [128, 128], F32)
    mx_sb = S.sb("mx_sb", [1, 8], F32)
    sca = S.sb("sca", [1, 2], F32)
    scb = S.sb("scb", [128, 2], F32)
    ustage = [S.sb(f"ustage{j}", [128, L_SEQ], F32) for j in range(3)]
    gstage = [S.sb(f"gstage{j}", [128, L_SEQ], F32) for j in range(3)]
    ot = [S.sb(f"ot{j}", [128, L_SEQ], F32) for j in range(3)]
    xtmp = S.sb("xtmp", [128, L_SEQ], F32)
    mrun = S.sb("mrun", [128, 1], F32)
    mtmp = S.sb("mtmp", [128, 1], F32)
    mrow = S.sb("mrow", [1, 128], F32)
    mfin = S.sb("mfin", [1, 1], F32)
    ps_bc = S.psum("ps_bc", [128, 2])
    ps_fin = S.psum("ps_fin", [1, 128])

    emit_identity(S, idn_f)
    ones1, rq_cell, r7 = emit_consts(S)
    S.aop("vector", "memset", mrun[:], 0.0, w=[mrun])
    S.dma("sync", mx_sb[:], mg28[:])
    emit_scale_bcast(S, mx_sb[0:1, 0:8], sca, scb, ones1, ps_bc[:, 0:2], rq_cell)

    for t in range(NI):
        ust = ustage[t % 3]
        gst = gstage[t % 3]
        S.dma("sync", ust[:], upq_i[t * 128:(t + 1) * 128, :])
        S.dma("gpsimd", gst[:], g2_i[t * 128:(t + 1) * 128, :])
        emit_qdq_rounds(S, gst[:], gst[:], scb, xtmp[:],
                        r_keys=[gst], w_keys=[gst])
        S.aop("vector", "tensor_tensor", ot[t % 3][:], gst[:], ust[:], ALU.mult,
              r=[gst, ust], w=[ot[t % 3]])
        S.dma("scalar", o_o[t * 128:(t + 1) * 128, :], ot[t % 3][:])
        S.aop("vector", "tensor_reduce", mtmp[:], ot[t % 3][:], AX, ALU.max,
              apply_absolute_value=True, r=[ot[t % 3]], w=[mtmp])
        S.aop("vector", "tensor_tensor", mrun[:], mrun[:], mtmp[:], ALU.max,
              r=[mrun, mtmp], w=[mrun])

    S.aop("tensor", "transpose", ps_fin[:], mrun[:], idn_f[:],
          r=[mrun, idn_f], w=[ps_fin])
    S.aop("scalar", "activation", mrow[:], ps_fin[:], AF.Copy, r=[ps_fin], w=[mrow])
    S.aop("vector", "tensor_reduce", mfin[:], mrow[:], AX, ALU.max, r=[mrow], w=[mfin])
    S.dma("sync", m_o[:], mfin[:])
    S.flush_engine("sync")
    return S.build()


class PlainLN:

    def __init__(self, S, nd, L, pref, ones1, ones_col, idn_f, eps=1e-5,
                 ps_banks=None):
        self.S, self.nd, self.L = S, nd, L
        self.ones1, self.ones_col, self.idn_f = ones1, ones_col, idn_f
        self.NLS = L // 128
        assert self.NLS <= 16
        self.sc = 1.0 / (nd * 128)
        self.ps_su = S.psum(pref + "ps_su", [128, self.NLS])
        self.ps_tp = S.psum(pref + "ps_tp", [1, 512])
        self.ps_bc = (ps_banks if ps_banks is not None else
                      [S.psum(pref + f"ps_bc{j}", [128, 512]) for j in range(2)])
        self.mu_cols = S.sb(pref + "mu_cols", [128, self.NLS], F32)
        self.st_row = S.sb(pref + "st_row", [1, L], F32)
        self.mu_bc = S.sb(pref + "mu_bc", [128, L], F32)
        self.rstd_bc = S.sb(pref + "rstd_bc", [128, L], F32)
        self.var_cols = S.sb(pref + "var_cols", [128, self.NLS], F32)
        self.eps_col = S.sb(pref + "eps_col", [128, 1], F32)
        S.aop("vector", "memset", self.eps_col[:], eps, w=[self.eps_col])

    def _stat_pass(self, tiles, out_cols):
        S, NLS, nd = self.S, self.NLS, self.nd
        for ls in range(NLS):
            pairs = [(tiles[k][:, ls * 128:(ls + 1) * 128], self.ones_col[:, 0:1])
                     for k in range(nd)]
            S.mm_group(self.ps_su[:, ls:ls + 1], pairs)
        S.aop("scalar", "activation", out_cols[:], self.ps_su[:], AF.Copy,
              scale=self.sc, r=[self.ps_su], w=[out_cols])

    def _bcast_cols(self, cols, dst_bc):
        S, NLS = self.S, self.NLS
        for g in range(0, NLS, 4):
            gn = min(4, NLS - g)
            for j in range(gn):
                S.aop("tensor", "transpose", self.ps_tp[0:1, j * 128:(j + 1) * 128],
                      cols[:, g + j:g + j + 1], self.idn_f[:],
                      r=[cols, self.idn_f], w=[self.ps_tp])
            S.aop("scalar", "activation", self.st_row[0:1, g * 128:(g + gn) * 128],
                  self.ps_tp[0:1, 0:gn * 128], AF.Copy, r=[self.ps_tp],
                  w=[self.st_row])
        for g in range(0, NLS, 4):
            gn = min(4, NLS - g)
            bank = self.ps_bc[(g // 4) % 2]
            S.aop("tensor", "matmul", bank[:, 0:gn * 128], self.ones1[0:1, :],
                  self.st_row[0:1, g * 128:(g + gn) * 128], start=True, stop=True,
                  r=[self.ones1, self.st_row], w=[bank])
            S.aop("scalar", "activation", dst_bc[:, g * 128:(g + gn) * 128],
                  bank[:, 0:gn * 128], AF.Copy, r=[bank], w=[dst_bc])

    def emit_from_stats(self, tiles_in, tiles_out, mu_cols_ap, var_cols_ap,
                        extra_r=()):
        S, nd = self.S, self.nd
        S.aop("scalar", "activation", var_cols_ap, var_cols_ap, AF.Ln,
              bias=self.eps_col[:, 0:1],
              r=[var_cols_ap, self.eps_col] + list(extra_r), w=[var_cols_ap])
        S.aop("scalar", "activation", var_cols_ap, var_cols_ap, AF.Exp,
              scale=-0.5, r=[var_cols_ap], w=[var_cols_ap])
        self._bcast_cols(mu_cols_ap, self.mu_bc)
        self._bcast_cols(var_cols_ap, self.rstd_bc)
        for k in range(nd):
            S.aop("vector", "tensor_sub", tiles_in[k], tiles_in[k], self.mu_bc[:],
                  r=[tiles_in[k], self.mu_bc], w=[tiles_in[k]])
            S.aop("vector", "tensor_mul", tiles_out[k], tiles_in[k],
                  self.rstd_bc[:], r=[tiles_in[k], self.rstd_bc],
                  w=[tiles_out[k]])

    def emit(self, tiles_in, tiles_out, tiles_tmp):
        S, nd = self.S, self.nd
        self._stat_pass(tiles_in, self.mu_cols)
        self._bcast_cols(self.mu_cols, self.mu_bc)
        for k in range(nd):
            S.aop("vector", "tensor_sub", tiles_out[k], tiles_in[k], self.mu_bc[:],
                  r=[tiles_in[k], self.mu_bc], w=[tiles_out[k]])
            S.aop("scalar", "activation", tiles_tmp[k], tiles_out[k], AF.Square,
                  r=[tiles_out[k]], w=[tiles_tmp[k]])
        self._stat_pass(tiles_tmp, self.var_cols)
        S.aop("scalar", "activation", self.var_cols[:], self.var_cols[:], AF.Ln,
              bias=self.eps_col[:, 0:1], r=[self.var_cols, self.eps_col],
              w=[self.var_cols])
        S.aop("scalar", "activation", self.var_cols[:], self.var_cols[:], AF.Exp,
              scale=-0.5, r=[self.var_cols], w=[self.var_cols])
        self._bcast_cols(self.var_cols, self.rstd_bc)
        for k in range(nd):
            S.aop("vector", "tensor_mul", tiles_out[k], tiles_out[k],
                  self.rstd_bc[:], r=[tiles_out[k], self.rstd_bc],
                  w=[tiles_out[k]])


def build_l5():
    S = Sched("l5")
    nc = S.nc
    o_i = nc.dram_tensor("o_i", [D_FFN, L_SEQ], F32, kind="ExternalInput").ap()
    x1n = nc.dram_tensor("x1n", [D_MODEL, L_SEQ], F32, kind="ExternalInput").ap()
    mo8 = nc.dram_tensor("mo8", [1, 8], F32, kind="ExternalInput").ap()
    w_down = nc.dram_tensor("w_down", [D_MODEL, D_FFN], F32, kind="ExternalInput").ap()
    out_o = nc.dram_tensor("out_o", [L_SEQ, D_MODEL], F32, kind="ExternalOutput").ap()
    NI = D_FFN // 128
    ND = D_MODEL // 128

    idn_f = S.sb("idn_f", [128, 128], F32)
    idn_bf = S.sb("idn_bf", [128, 128], BF16)
    ones_col = S.sb("ones_col", [128, 1], F32)
    mx_sb = S.sb("mx_sb", [1, 8], F32)
    sca = S.sb("sca", [1, 2], F32)
    scb = S.sb("scb", [128, 2], F32)
    ps_bc = S.psum("ps_bc", [128, 2])
    emit_identity(S, idn_f)
    emit_identity(S, idn_bf)
    ones1, rq_cell, r7 = emit_consts(S)
    S.aop("vector", "memset", ones_col[:], 1.0, w=[ones_col])
    S.dma("sync", mx_sb[:], mo8[:])
    emit_scale_bcast(S, mx_sb[0:1, 0:8], sca, scb, ones1, ps_bc[:, 0:2], rq_cell)
    x2 = [S.sb(f"x2_{k}", [128, L_SEQ], F32) for k in range(ND)]
    wdT = [S.sb(f"wdT{k}", [128, D_MODEL], BF16) for k in range(NI)]

    wstage = [S.sb("wstage0", [128, D_FFN], F32)] * 2
    wq_bufs = [S.sb(f"wqb{j}", [128, D_FFN], BF16) for j in range(2)]
    sblk = S.sb("sblk", [128, 96], F32)
    rblk = S.sb("rblk", [128, 96], F32)
    ps_tr = [S.psum(f"ps_tr{j}", [128, 512], BF16) for j in range(2)]
    oq = [S.sb(f"oq{k}", [128, 1024], BF16) for k in range(NI)]
    ostage = [S.sb(f"ostage{j}", [128, 1024], F32) for j in range(2)]
    xtmp = S.sb("xtmp", [128, 1024], F32)
    x1st = [S.sb(f"x1st{j}", [128, 512], F32) for j in range(2)]
    ps_mm = [S.psum(f"ps_mm{j}", [128, 512]) for j in range(2)]
    for lh in range(2):
        co = lh * 1024
        for t in range(NI):
            st = ostage[t % 2]
            S.dma("sync", st[:], o_i[t * 128:(t + 1) * 128, co:co + 1024])
            emit_qdq_rounds(S, oq[t][:], st[:], scb, xtmp[:],
                            r_keys=[st], w_keys=[oq[t]])
        for t in range(ND):
            if lh == 0:
                st = wstage[t % 2]
                S.dma("sync", st[:], w_down[t * 128:(t + 1) * 128, :])
                S.aop("vector", "tensor_reduce", sblk[:],
                      st[:].rearrange("p (a b) -> p a b", a=96), AX, ALU.max,
                      apply_absolute_value=True, r=[st], w=[sblk])
                S.aop("vector", "tensor_scalar", sblk[:], sblk[:], r7[:, 0:1],
                      1e-12, ALU.mult, ALU.max, r=[sblk, r7], w=[sblk])
                S.aop("vector", "reciprocal", rblk[:], sblk[:], r=[sblk], w=[rblk])
                S.aop("vector", "tensor_tensor",
                      out=st[:].rearrange("p (a b) -> p a b", a=96),
                      in0=st[:].rearrange("p (a b) -> p a b", a=96),
                      in1=_rep(rblk, 96, 32), op=ALU.mult, r=[st, rblk], w=[st])
                S.aop("vector", "tensor_scalar", st[:], st[:], C_RND, C_RND,
                      ALU.add, ALU.subtract, r=[st], w=[st])
                wq = wq_bufs[t % 2]
                S.aop("vector", "tensor_tensor",
                      out=wq[:].rearrange("p (a b) -> p a b", a=96),
                      in0=st[:].rearrange("p (a b) -> p a b", a=96),
                      in1=_rep(sblk, 96, 32), op=ALU.mult, r=[st, sblk], w=[wq])
                for half in range(6):
                    bank = ps_tr[half % 2]
                    for j in range(4):
                        kblk = half * 4 + j
                        S.aop("tensor", "transpose",
                              bank[:, j * 128:(j + 1) * 128],
                              wq[:, kblk * 128:(kblk + 1) * 128], idn_bf[:],
                              r=[wq, idn_bf], w=[bank])
                    for j in range(4):
                        kblk = half * 4 + j
                        S.aop("scalar", "activation",
                              wdT[kblk][:, t * 128:(t + 1) * 128],
                              bank[:, j * 128:(j + 1) * 128], AF.Copy,
                              r=[bank], w=[f"wdT{kblk}:{t}"])
            for lc in range(2):
                sl = co + lc * 512
                xs = x1st[lc % 2]
                S.dma("sync", xs[:], x1n[t * 128:(t + 1) * 128, sl:sl + 512])
                bank = ps_mm[lc % 2]
                pairs = [(wdT[k][:, t * 128:(t + 1) * 128],
                          oq[k][:, lc * 512:(lc + 1) * 512]) for k in range(NI)]
                S.mm_group(bank[:], pairs,
                           r_extra=[f"wdT{k}:{t}" for k in range(NI)])
                S.aop("vector", "tensor_add", x2[t][:, sl:sl + 512],
                      bank[:], xs[:], r=[bank, xs], w=[x2[t]])
    S.barrier()
    S.release(*oq, ostage[0], ostage[1], xtmp, x1st[0], x1st[1],
              ps_mm[0], ps_mm[1], ps_tr[0], ps_tr[1], wstage[0],
              wq_bufs[0], wq_bufs[1], sblk, rblk)
    S.release(*wdT)

    x2t = [S.sb(f"x2t_{k}", [128, L_SEQ], F32) for k in range(ND)]
    ln2 = PlainLN(S, ND, L_SEQ, "ln2", ones1, ones_col, idn_f)
    ln2.emit([t[:] for t in x2], [t[:] for t in x2t], [t[:] for t in x2])
    otm = [S.sb(f"otm{j}", [128, D_MODEL], F32) for j in range(2)]
    ps_otr = [S.psum(f"ps_otr{j}", [128, 512]) for j in range(2)]
    for ls in range(L_SEQ // 128):
        buf = otm[ls % 2]
        for half in range(2):
            bank = ps_otr[half]
            for j in range(4):
                k = half * 4 + j
                S.aop("tensor", "transpose", bank[:, j * 128:(j + 1) * 128],
                      x2t[k][:, ls * 128:(ls + 1) * 128], idn_f[:],
                      r=[x2t[k], idn_f], w=[bank])
            S.aop("scalar", "activation", buf[:, half * 512:(half + 1) * 512],
                  bank[:], AF.Copy, r=[bank], w=[buf])
        S.dma("sync", out_o[ls * 128:(ls + 1) * 128, :], buf[:])
    S.flush_engine("sync")
    return S.build()


def build_l1():
    S = Sched("l1")
    nc = S.nc
    x_i = nc.dram_tensor("x", [L_SEQ, D_MODEL], F32, kind="ExternalInput").ap()
    ln_g = nc.dram_tensor("ln_g", [1, D_MODEL], F32, kind="ExternalInput").ap()
    ln_b = nc.dram_tensor("ln_b", [1, D_MODEL], F32, kind="ExternalInput").ap()
    W_in = nc.dram_tensor("W_in", [D_MODEL, 2 * D_INNER], F32, kind="ExternalInput").ap()
    conv_w = nc.dram_tensor("conv_w", [D_INNER, 4], F32, kind="ExternalInput").ap()
    conv_b = nc.dram_tensor("conv_b", [1, D_INNER], F32, kind="ExternalInput").ap()
    W_xp = nc.dram_tensor("W_xp", [D_INNER, 96], F32, kind="ExternalInput").ap()
    W_dt = nc.dram_tensor("W_dt", [64, D_INNER], F32, kind="ExternalInput").ap()
    b_dt = nc.dram_tensor("b_dt", [1, D_INNER], F32, kind="ExternalInput").ap()
    A_log = nc.dram_tensor("A_log", [D_INNER, 16], F32, kind="ExternalInput").ap()
    D_par = nc.dram_tensor("D_par", [1, D_INNER], F32, kind="ExternalInput").ap()
    res_sc = nc.dram_tensor("res_sc", [1, 1], F32, kind="ExternalInput").ap()
    W_out_h = nc.dram_tensor("W_out", [D_INNER, D_MODEL], F32, kind="ExternalInput").ap()
    x1n_o = nc.dram_tensor("x1n_o", [D_MODEL, L_SEQ], F32, kind="ExternalOutput").ap()
    mx_o = nc.dram_tensor("mx_o", [1, 1], F32, kind="ExternalOutput").ap()
    sp_u2 = nc.dram_tensor("sp_u2", [D_INNER, L_SEQ], F32, kind="Internal").ap()
    sp_dt = nc.dram_tensor("sp_dt", [D_INNER, L_SEQ], F32, kind="Internal").ap()
    sp_z = nc.dram_tensor("sp_z", [D_INNER, L_SEQ], F32, kind="Internal").ap()
    sp_x2 = nc.dram_tensor("sp_x2", [D_MODEL, L_SEQ], F32, kind="Internal").ap()

    NE = D_INNER // 128
    ND = D_MODEL // 128
    TL1 = 512
    NCH1 = L_SEQ // TL1
    TL2 = 256
    NCH2 = L_SEQ // TL2
    NST = 16

    idn_f = S.sb("idn_f", [128, 128], F32)
    ones1 = S.sb("ones1", [1, 128], F32)
    ones_col = S.sb("ones_col", [128, 1], F32)
    gcol = S.sb("gcol", [128, ND], F32)
    bcol = S.sb("bcol", [128, ND], F32)
    cw = S.sb("cw", [128, 4 * NE], F32)
    cb = S.sb("cb", [128, NE], F32)
    bdt = S.sb("bdt", [128, NE], F32)
    Dp = S.sb("Dp", [128, NE], F32)
    A_sb = S.sb("A_sb", [128, NST * NE], F32)
    rsc = S.sb("rsc", [128, 1], F32)
    rsc_st = S.sb("rsc_st", [1, 1], F32)
    bu_acc = S.sb("bu_acc", [128, 32], F32)
    butmp = S.sb("butmp", [128, 32], F32)
    wxp_bf = S.sb("wxp_bf", [128, 96 * NE], BF16)
    wdt_bf = S.sb("wdt_bf", [64, D_INNER], BF16)
    bc_sb = S.sb("bc_sb", [32, L_SEQ], F32)
    sel_sb = S.sb("sel_sb", [32, 32 * 128], F32)
    mxrun = S.sb("mxrun", [128, 1], F32)
    mxtmp = S.sb("mxtmp", [128, 1], F32)
    mrow = S.sb("mrow", [1, 128], F32)
    mfin = S.sb("mfin", [1, 1], F32)

    emit_identity(S, idn_f)
    S.aop("vector", "memset", ones1[:], 1.0, w=[ones1])
    S.aop("vector", "memset", ones_col[:], 1.0, w=[ones_col])
    S.aop("vector", "memset", mxrun[:], 0.0, w=[mxrun])
    S.aop("gpsimd", "memset", sel_sb[:], 0.0, w=[sel_sb])
    S.aop("gpsimd", "affine_select",
          out=sel_sb[:].rearrange("p (a b) -> p a b", a=32),
          in_=sel_sb[:].rearrange("p (a b) -> p a b", a=32),
          compare_op=ALU.not_equal, fill=1.0, base=0,
          pattern=[[-1, 32], [0, 128]], channel_multiplier=1,
          r=[sel_sb], w=[sel_sb])
    for k in range(ND):
        S.dma("sync", gcol[:, k:k + 1], ln_g[0:1, k * 128:(k + 1) * 128])
        S.dma("sync", bcol[:, k:k + 1], ln_b[0:1, k * 128:(k + 1) * 128])
    for t in range(NE):
        S.dma("sync", cw[:, 4 * t:4 * t + 4], conv_w[t * 128:(t + 1) * 128, :])
        S.dma("sync", cb[:, t:t + 1], conv_b[0:1, t * 128:(t + 1) * 128])
        S.dma("sync", bdt[:, t:t + 1], b_dt[0:1, t * 128:(t + 1) * 128])
        S.dma("sync", Dp[:, t:t + 1], D_par[0:1, t * 128:(t + 1) * 128])
        S.dma("sync", A_sb[:, NST * t:NST * (t + 1)],
              A_log[t * 128:(t + 1) * 128, :])
        S.dma("gpsimd", wxp_bf[:, 96 * t:96 * (t + 1)],
              W_xp[t * 128:(t + 1) * 128, :])
    S.dma("gpsimd", wdt_bf[:], W_dt[:])
    S.dma("sync", rsc_st[:], res_sc[:])
    S.aop("scalar", "activation", A_sb[:], A_sb[:], AF.Exp, r=[A_sb], w=[A_sb])
    S.aop("vector", "tensor_scalar", A_sb[:], A_sb[:], -1.0, None, ALU.mult,
          r=[A_sb], w=[A_sb])

    win_bf = [S.sb(f"winbf{k}", [128, 2 * D_INNER], BF16) for k in range(ND)]
    winst2 = [S.sb("winst0", [128, 2 * D_INNER], F32)] * 2
    xdbl_dt = S.sb("xdbl_dt", [64, L_SEQ], BF16)
    ps_a = S.psum("ps_a", [128, 512])
    ps_b = S.psum("ps_b", [128, 512])
    ps_c = S.psum("ps_c", [128, 512])
    ps_d = S.psum("ps_d", [128, 512])
    ps_e = S.psum("ps_e", [128, 512])
    ps_f = S.psum("ps_f", [128, 512])
    ps_bu = ps_e
    ps4 = [ps_a, ps_b, ps_c, ps_d]

    S.aop("tensor", "matmul", ps_bu[:, 0:1], ones1[0:1, :], rsc_st[0:1, 0:1],
          start=True, stop=True, r=[ones1, rsc_st], w=["psbu0"])
    S.aop("scalar", "activation", rsc[:], ps_bu[:, 0:1], AF.Copy,
          r=["psbu0"], w=[rsc])

    S.aop("vector", "memset", bu_acc[:], 0.0, w=[bu_acc])
    for k in range(ND):
        winst = winst2[k % 2]
        S.dma("gpsimd", winst[:], W_in[k * 128:(k + 1) * 128, :])
        for s in range(32):
            S.aop("tensor", "matmul", ps_bu[:, s:s + 1],
                  winst[:, s * 128:(s + 1) * 128], bcol[:, k:k + 1],
                  start=True, stop=True, r=[winst, bcol], w=[f"psbu{s}"])
        S.aop("scalar", "activation", butmp[:], ps_bu[:, 0:32], AF.Copy,
              r=[f"psbu{s}" for s in range(32)], w=[butmp])
        S.aop("vector", "tensor_add", bu_acc[:], bu_acc[:], butmp[:],
              r=[bu_acc, butmp], w=[bu_acc])
        S.aop("scalar", "activation", win_bf[k][:], winst[:], AF.Identity,
              scale=gcol[:, k:k + 1], r=[winst, gcol], w=[win_bf[k]])

    xtm = [S.sb(f"xtm{j}", [128, D_MODEL], F32) for j in range(3)]
    x_fm = [S.sb(f"xfm{k}", [128, TL1], F32) for k in range(ND)]
    xn_bf = [S.sb(f"xnbf{k}", [128, TL1], BF16) for k in range(ND)]
    bnraw = S.sb("bnraw", [128, 12], F32)
    bnagg = S.sb("bnagg", [128, 8], F32)
    x2st = S.sb("x2st", [128, TL1], F32)
    u_sb = [S.sb(f"usb{j}", [128, TL1 + 3], F32) for j in range(2)]
    u_tail = S.sb("u_tail", [128, 3 * NE], F32)
    cacc = S.sb("cacc", [128, TL1], F32)
    u2f = S.sb("u2f", [128, TL1], F32)
    u2bf = [S.sb(f"u2bf{t}", [128, TL1], BF16) for t in range(NE)]
    zs_sb = S.sb("zs_sb", [128, TL1], F32)
    dtf = S.sb("dtf", [128, TL1], F32)
    sp_t1 = S.sb("sp_t1", [128, TL1], F32)
    sp_t2 = S.sb("sp_t2", [128, TL1], F32)
    ln0 = PlainLN(S, ND, TL1, "ln0", ones1, ones_col, idn_f,
                  ps_banks=[ps_c, ps_d])
    S.aop("vector", "memset", u_tail[:], 0.0, w=[u_tail])

    for c in range(NCH1):
        c0 = c * TL1
        for j in range(4):
            stg = xtm[j % 3]
            S.dma("sync", stg[:], x_i[c0 + j * 128: c0 + (j + 1) * 128, :])
            S.aop("vector", "bn_stats", bnraw[:, 0:6], stg[:, 0:512],
                  r=[stg], w=[bnraw])
            S.aop("vector", "bn_stats", bnraw[:, 6:12], stg[:, 512:1024],
                  r=[stg, bnraw], w=[bnraw])
            S.aop("vector", "bn_aggr", bnagg[:, j:j + 5:4], bnraw[:],
                  r=[bnraw], w=[bnagg])
            for half in range(2):
                bank = ps_c if half == 0 else ps_d
                for kk in range(4):
                    k = half * 4 + kk
                    S.aop("tensor", "transpose", bank[:, kk * 128:(kk + 1) * 128],
                          stg[:, k * 128:(k + 1) * 128], idn_f[:],
                          r=[stg, idn_f], w=[bank])
                for kk in range(4):
                    k = half * 4 + kk
                    S.aop("vector", "tensor_copy",
                          x_fm[k][:, j * 128:(j + 1) * 128],
                          bank[:, kk * 128:(kk + 1) * 128],
                          r=[bank], w=[x_fm[k]])
        for k in range(ND):
            S.aop("vector", "tensor_scalar", x2st[:], x_fm[k][:], 2.0, None,
                  ALU.mult, r=[x_fm[k]], w=[x2st])
            S.dma("gpsimd", sp_x2[k * 128:(k + 1) * 128, c0:c0 + TL1], x2st[:])
        ln0.emit_from_stats([t[:] for t in x_fm], [t[:] for t in xn_bf],
                            bnagg[:, 0:4], bnagg[:, 4:8])
        for t in range(NE):
            bank = ps4[t % 4]
            pairs = [(win_bf[k][:, t * 128:(t + 1) * 128], xn_bf[k][:])
                     for k in range(ND)]
            S.mm_group(bank[:], pairs)
            usb = u_sb[t % 2]
            S.aop("vector", "tensor_copy", usb[:, 0:3], u_tail[:, 3 * t:3 * t + 3],
                  r=[u_tail], w=[usb])
            S.aop("scalar", "activation", usb[:, 3:3 + TL1], bank[:],
                  AF.Identity, bias=bu_acc[:, t:t + 1], r=[bank, bu_acc],
                  w=[usb])
            S.aop("vector", "tensor_copy", u_tail[:, 3 * t:3 * t + 3],
                  usb[:, TL1:TL1 + 3], r=[usb], w=[u_tail])
            S.aop("vector", "tensor_scalar", cacc[:], usb[:, 0:TL1],
                  cw[:, 4 * t:4 * t + 1], None, ALU.mult,
                  r=[usb, cw], w=[cacc])
            for q in range(1, 4):
                S.aop("vector", "scalar_tensor_tensor", out=cacc[:],
                      in0=usb[:, q:q + TL1], scalar=cw[:, 4 * t + q:4 * t + q + 1],
                      in1=cacc[:], op0=ALU.mult, op1=ALU.add,
                      r=[usb, cw, cacc], w=[cacc])
            S.aop("scalar", "activation", u2f[:], cacc[:], AF.Silu,
                  bias=cb[:, t:t + 1], r=[cacc, cb], w=[u2f])
            S.aop("vector", "tensor_copy", u2bf[t][:], u2f[:],
                  r=[u2f], w=[u2bf[t]])
            S.dma("gpsimd", sp_u2[t * 128:(t + 1) * 128, c0:c0 + TL1], u2f[:])
        for t in range(NE):
            bank = ps4[t % 4]
            pairs = [(win_bf[k][:, 2048 + t * 128:2048 + (t + 1) * 128],
                      xn_bf[k][:]) for k in range(ND)]
            S.mm_group(bank[:], pairs)
            S.aop("scalar", "activation", zs_sb[:], bank[:], AF.Silu,
                  bias=bu_acc[:, 16 + t:16 + t + 1], r=[bank, bu_acc], w=[zs_sb])
            S.dma("gpsimd", sp_z[t * 128:(t + 1) * 128, c0:c0 + TL1], zs_sb[:])
        pairs_dt = [(wxp_bf[:, 96 * t:96 * t + 64], u2bf[t][:]) for t in range(NE)]
        S.mm_group(ps_a[0:64, :], pairs_dt)
        S.aop("scalar", "activation", xdbl_dt[:, c0:c0 + TL1], ps_a[0:64, :],
              AF.Copy, r=[ps_a], w=[xdbl_dt])
        pairs_bc = [(wxp_bf[:, 96 * t + 64:96 * (t + 1)], u2bf[t][:])
                    for t in range(NE)]
        S.mm_group(ps_b[0:32, :], pairs_bc)
        S.aop("scalar", "activation", bc_sb[:, c0:c0 + TL1], ps_b[0:32, :],
              AF.Copy, r=[ps_b], w=[bc_sb])
        for t in range(NE):
            bank = ps_e if t % 2 == 0 else ps_f
            S.mm_group(bank[:], [(wdt_bf[:, t * 128:(t + 1) * 128],
                                  xdbl_dt[:, c0:c0 + TL1])])
            S.aop("scalar", "activation", sp_t1[:], bank[:], AF.Abs,
                  bias=bdt[:, t:t + 1], r=[bank, bdt], w=[sp_t1])
            S.aop("scalar", "activation", sp_t1[:], sp_t1[:], AF.Exp, scale=-1.0,
                  r=[sp_t1], w=[sp_t1])
            S.aop("scalar", "activation", sp_t1[:], sp_t1[:], AF.Ln, bias=1.0,
                  r=[sp_t1], w=[sp_t1])
            S.aop("scalar", "activation", sp_t2[:], bank[:], AF.Relu,
                  bias=bdt[:, t:t + 1], r=[bank, bdt], w=[sp_t2])
            S.aop("vector", "tensor_add", dtf[:], sp_t1[:], sp_t2[:],
                  r=[sp_t1, sp_t2], w=[dtf])
            S.dma("gpsimd", sp_dt[t * 128:(t + 1) * 128, c0:c0 + TL1], dtf[:])

    S.barrier()
    S.release(*win_bf, winst2[0], xdbl_dt, ps_a, ps_b, ps_c, ps_d,
              ps_e, ps_f,
              *xtm, *x_fm, *xn_bf, x2st, bnraw, bnagg,
              u_sb[0], u_sb[1], u_tail, cacc, u2f, *u2bf, zs_sb, dtf,
              sp_t1, sp_t2,
              ln0.ps_su, ln0.ps_tp, ln0.mu_cols,
              ln0.st_row, ln0.mu_bc, ln0.rstd_bc, ln0.var_cols, ln0.eps_col)

    wout_bf = [S.sb(f"wout{t}", [128, D_MODEL], BF16) for t in range(NE)]
    for t in range(NE):
        S.dma("gpsimd", wout_bf[t][:], W_out_h[t * 128:(t + 1) * 128, :])
    dt_c = S.sb("dt_c", [128, NE * TL2], F32)
    dtu_c = S.sb("dtu_c", [128, NE * TL2], F32)
    y_c = S.sb("y_c", [128, NE * TL2], F32)
    x2_c = S.sb("x2_c", [128, ND * TL2], F32)
    PITCH = TL2 + 1
    dA_g = [S.sb(f"dAg{j}", [128, NE * PITCH], F32) for j in range(2)]
    dtuB_g = S.sb("dtuB_g", [128, NE * PITCH], F32)
    h_sc = S.sb("h_sc", [128, NE * PITCH], F32)
    h_g = dtuB_g
    carry = S.sb("carry", [128, NST * NE], F32)
    bcn = [S.sb(f"bcn{j}", [128, 2 * TL2], F32) for j in range(4)]
    x1_c = S.sb("x1_c", [128, ND * TL2], F32)
    x1n_c = x2_c
    ps_bcn = [S.psum(f"ps_bcn{j}", [128, 2 * TL2]) for j in range(2)]
    ps_wo = [S.psum(f"ps_wo{j}", [128, TL2]) for j in range(2)]
    ln1 = PlainLN(S, ND, TL2, "ln1", ones1, ones_col, idn_f,
                  ps_banks=[ps_bcn[0], ps_bcn[1]])
    S.aop("vector", "memset", carry[:], 0.0, w=[carry])
    S.aop("vector", "memset", dA_g[0][:], 0.0, w=[dA_g[0]])
    S.aop("vector", "memset", dA_g[1][:], 0.0, w=[dA_g[1]])
    S.aop("vector", "memset", dtuB_g[:, NE * PITCH - 1:NE * PITCH], 0.0,
          w=[dtuB_g])

    def vp(t3, width=TL2):
        return t3[:].rearrange("p (a b) -> p a b", a=NE)[:, :, 0:width]
    y2bf = dtu_c

    def v3(t3):
        return t3[:].rearrange("p (a b) -> p a b", a=NE)

    for c in range(NCH2):
        c0 = c * TL2
        S.dma("sync", dt_c[:].rearrange("p (a b) -> p a b", a=NE),
              sp_dt[:, c0:c0 + TL2].rearrange("(a p) b -> p a b", p=128))
        S.dma("scalar", vp(h_g),
              sp_u2[:, c0:c0 + TL2].rearrange("(a p) b -> p a b", p=128))
        S.dma("gpsimd", x2_c[:].rearrange("p (a b) -> p a b", a=ND),
              sp_x2[:, c0:c0 + TL2].rearrange("(a p) b -> p a b", p=128))
        S.aop("vector", "tensor_mul", out=v3(dtu_c), in0=v3(dt_c),
              in1=vp(h_g), op=None, r=[dt_c, h_g], w=[dtu_c]) if False else \
        S.aop("vector", "tensor_tensor", out=v3(dtu_c), in0=v3(dt_c),
              in1=vp(h_g), op=ALU.mult, r=[dt_c, h_g], w=[dtu_c])
        for n in range(NST):
            if n % 4 == 0:
                for nn in range(n, n + 4):
                    pb = ps_bcn[nn % 2]
                    S.aop("tensor", "matmul", pb[:, 0:TL2],
                          sel_sb[:, nn * 128:(nn + 1) * 128],
                          bc_sb[:, c0:c0 + TL2], start=True, stop=True,
                          r=[sel_sb, bc_sb], w=[pb])
                    S.aop("tensor", "matmul", pb[:, TL2:2 * TL2],
                          sel_sb[:, (16 + nn) * 128:(17 + nn) * 128],
                          bc_sb[:, c0:c0 + TL2], start=True, stop=True,
                          r=[sel_sb, bc_sb], w=[pb])
                    S.aop("scalar", "activation", bcn[nn % 4][:], pb[:], AF.Copy,
                          r=[pb], w=[bcn[nn % 4]])
            bb = bcn[n % 4]
            dag = dA_g[n % 2]
            for t in range(NE):
                S.aop("scalar", "activation",
                      dag[:, t * PITCH:t * PITCH + TL2],
                      dt_c[:, t * TL2:(t + 1) * TL2], AF.Exp,
                      scale=A_sb[:, NST * t + n:NST * t + n + 1],
                      r=[dt_c, A_sb], w=[dag])
            S.aop("vector", "tensor_tensor", out=vp(dtuB_g), in0=v3(dtu_c),
                  in1=bb[:, 0:TL2].unsqueeze(1).broadcast_to([128, NE, TL2]),
                  op=ALU.mult, r=[dtu_c, bb], w=[dtuB_g])
            S.aop("vector", "tensor_copy",
                  dtuB_g[:].rearrange("p (a b) -> p a b", a=NE)[:, 0:NE - 1,
                                                               TL2:TL2 + 1],
                  carry[:, n * NE + 1:(n + 1) * NE].unsqueeze(2),
                  r=[carry], w=[dtuB_g])
            S.aop("vector", "tensor_tensor_scan",
                  h_sc[:], dag[:], dtuB_g[:],
                  carry[:, n * NE:n * NE + 1], ALU.mult, ALU.add,
                  r=[dag, dtuB_g, carry], w=[h_sc])
            S.aop("vector", "tensor_copy",
                  carry[:, n * NE:(n + 1) * NE],
                  h_sc[:].rearrange("p (a b) -> p a b", a=NE)[:, :,
                      TL2 - 1:TL2].rearrange("p a b -> p (a b)"),
                  r=[h_sc], w=[carry])
            if n == 0:
                S.aop("vector", "tensor_tensor", out=v3(y_c), in0=vp(h_sc),
                      in1=bb[:, TL2:2 * TL2].unsqueeze(1).broadcast_to(
                          [128, NE, TL2]),
                      op=ALU.mult, r=[h_sc, bb], w=[y_c])
            else:
                S.aop("vector", "tensor_tensor", out=vp(dtuB_g), in0=vp(h_sc),
                      in1=bb[:, TL2:2 * TL2].unsqueeze(1).broadcast_to(
                          [128, NE, TL2]),
                      op=ALU.mult, r=[h_sc, bb], w=[dtuB_g])
                S.aop("vector", "tensor_add", out=v3(y_c), in0=v3(y_c),
                      in1=vp(dtuB_g), r=[y_c, dtuB_g], w=[y_c])
        S.dma("scalar", vp(h_g),
              sp_u2[:, c0:c0 + TL2].rearrange("(a p) b -> p a b", p=128))
        S.dma("sync", dt_c[:].rearrange("p (a b) -> p a b", a=NE),
              sp_z[:, c0:c0 + TL2].rearrange("(a p) b -> p a b", p=128))
        for t in range(NE):
            S.aop("scalar", "activation", h_sc[:, t * TL2:(t + 1) * TL2],
                  h_g[:, t * PITCH:t * PITCH + TL2], AF.Identity,
                  scale=Dp[:, t:t + 1], r=[h_g, Dp], w=[h_sc])
        S.aop("vector", "tensor_add", y_c[:], y_c[:], h_sc[:, 0:NE * TL2],
              r=[y_c, h_sc], w=[y_c])
        y2v = y2bf[:].bitcast(BF16)[:, 0:NE * TL2]
        S.aop("vector", "tensor_mul", y2v, y_c[:], dt_c[:], r=[y_c, dt_c],
              w=["y2bf", y2bf])
        for m in range(ND):
            bank = ps_wo[m % 2]
            pairs = [(wout_bf[t][:, m * 128:(m + 1) * 128],
                      y2v[:, t * TL2:(t + 1) * TL2]) for t in range(NE)]
            S.mm_group(bank[:], pairs, r_extra=["y2bf"])
            S.aop("vector", "scalar_tensor_tensor",
                  out=x1_c[:, m * TL2:(m + 1) * TL2], in0=bank[:],
                  scalar=rsc[:, 0:1], in1=x2_c[:, m * TL2:(m + 1) * TL2],
                  op0=ALU.mult, op1=ALU.add, r=[bank, rsc, x2_c], w=[x1_c])
        ln1.emit([x1_c[:, m * TL2:(m + 1) * TL2] for m in range(ND)],
                 [x1n_c[:, m * TL2:(m + 1) * TL2] for m in range(ND)],
                 [x1_c[:, m * TL2:(m + 1) * TL2] for m in range(ND)])
        S.dma("gpsimd", x1n_o[:, c0:c0 + TL2].rearrange("(a p) b -> p a b", p=128),
              x1n_c[:].rearrange("p (a b) -> p a b", a=ND))
        S.aop("vector", "tensor_reduce", mxtmp[:], x1n_c[:], AX, ALU.max,
              apply_absolute_value=True, r=[x1n_c], w=[mxtmp])
        S.aop("vector", "tensor_tensor", mxrun[:], mxrun[:], mxtmp[:], ALU.max,
              r=[mxrun, mxtmp], w=[mxrun])

    S.aop("tensor", "transpose", ln1.ps_tp[0:1, 0:128], mxrun[:], idn_f[:],
          r=[mxrun, idn_f], w=[ln1.ps_tp])
    S.aop("scalar", "activation", mrow[:], ln1.ps_tp[0:1, 0:128], AF.Copy,
          r=[ln1.ps_tp], w=[mrow])
    S.aop("vector", "tensor_reduce", mfin[:], mrow[:], AX, ALU.max, r=[mrow],
          w=[mfin])
    S.dma("sync", mx_o[:], mfin[:])
    S.flush_engine("sync")
    return S.build()



_CACHE = {}


def _progs():
    if "p" not in _CACHE:
        _CACHE["p"] = (build_l1(), build_l2(), build_l3(), build_l4(), build_l5())
    return _CACHE["p"]


def kernel(L0_attn_x, L0_attn_ln_g, L0_attn_ln_b, L0_attn_W_in, L0_attn_conv_w,
           L0_attn_conv_b, L0_attn_W_xp, L0_attn_W_dt, L0_attn_b_dt,
           L0_attn_A_log, L0_attn_D_param, L0_attn_W_out, L0_attn_res_scale,
           L0_ffn_w_gate, L0_ffn_w_up, L0_ffn_w_down):
    n = 8
    nc1, nc2, nc3, nc4, nc5 = _progs()
    cores = list(range(n))
    c = np.ascontiguousarray

    im1 = [{
        "x": c(L0_attn_x[i]),
        "ln_g": c(L0_attn_ln_g.reshape(1, -1)), "ln_b": c(L0_attn_ln_b.reshape(1, -1)),
        "W_in": c(L0_attn_W_in), "conv_w": c(L0_attn_conv_w),
        "conv_b": c(L0_attn_conv_b.reshape(1, -1)), "W_xp": c(L0_attn_W_xp),
        "W_dt": c(L0_attn_W_dt), "b_dt": c(L0_attn_b_dt.reshape(1, -1)),
        "A_log": c(L0_attn_A_log), "D_par": c(L0_attn_D_param.reshape(1, -1)),
        "res_sc": c(L0_attn_res_scale.reshape(1, 1)), "W_out": c(L0_attn_W_out),
    } for i in range(n)]
    r1 = run_bass_kernel_spmd(nc1, im1, core_ids=cores).results
    mx8 = np.concatenate([r1[i]["mx_o"] for i in range(n)], axis=1)

    im2 = [{"x1n": r1[i]["x1n_o"], "mx8": mx8,
            "w_up": c(L0_ffn_w_up), "w_gate": c(L0_ffn_w_gate)} for i in range(n)]
    r2 = run_bass_kernel_spmd(nc2, im2, core_ids=cores).results
    mup8 = np.concatenate([r2[i]["m_up"] for i in range(n)], axis=1)
    mg8 = np.concatenate([r2[i]["m_gate"] for i in range(n)], axis=1)

    im3 = [{"up_i": r2[i]["up_o"], "gate_i": r2[i]["gate_o"],
            "mup8": mup8, "mg8": mg8} for i in range(n)]
    r3 = run_bass_kernel_spmd(nc3, im3, core_ids=cores).results
    mg28 = np.concatenate([r3[i]["m_g2"] for i in range(n)], axis=1)

    im4 = [{"upq_i": r3[i]["upq_o"], "g2_i": r3[i]["gate2_o"], "mg28": mg28}
           for i in range(n)]
    r4 = run_bass_kernel_spmd(nc4, im4, core_ids=cores).results
    mo8 = np.concatenate([r4[i]["m_o"] for i in range(n)], axis=1)

    im5 = [{"o_i": r4[i]["o_o"], "x1n": r1[i]["x1n_o"], "mo8": mo8,
            "w_down": c(L0_ffn_w_down)} for i in range(n)]
    r5 = run_bass_kernel_spmd(nc5, im5, core_ids=cores).results

    return np.stack([r5[i]["out_o"] for i in range(n)], axis=0)


# version 13
# speedup vs baseline: 1.0058x; 1.0057x over previous
import numpy as np
from contextlib import ExitStack
import concourse.bass as bass
import concourse.mybir as mybir
from concourse.bass_utils import run_bass_kernel_spmd

F32 = mybir.dt.float32
BF16 = mybir.dt.bfloat16
AF = mybir.ActivationFunctionType
ALU = mybir.AluOpType
AX = mybir.AxisListType.X
C_RND = 12582912.0
D_MODEL = 1024
D_INNER = 2048
D_FFN = 3072
L_SEQ = 2048
ENGINES = ("sync", "scalar", "vector", "tensor", "gpsimd")


def _key(x):
    if isinstance(x, str):
        return x
    t = getattr(x, "tensor", None)
    if t is not None:
        return t.name
    return x.name


class Sched:

    def __init__(self, name):
        self.nc = bass.Bass("TRN2", target_bir_lowering=False, debug=False)
        self.es = ExitStack()
        self.q = {e: [] for e in ENGINES}
        self.sems = {}
        self.last_w = {}
        self.readers = {}
        self.waited = {e: {} for e in ENGINES}
        for e in ENGINES:
            self._mk_sem("q_" + e)
        self.dma_issued = {e: {} for e in ENGINES}
        self.nops = 0

    def _mk_sem(self, name):
        h = self.es.enter_context(self.nc.semaphore(name))
        self.sems[name] = [h, 0]

    def sb(self, name, shape, dt):
        cm = self.nc.sbuf_tensor(name, shape, dt)
        h = cm.__enter__()
        self._cms = getattr(self, "_cms", {})
        self._alloc_idx = getattr(self, "_alloc_idx", 0) + 1
        self._cms[name] = (self._alloc_idx, cm)
        return h

    def psum(self, name, shape, dt=F32):
        cm = self.nc.psum_tensor(name, shape, dt)
        h = cm.__enter__()
        self._cms = getattr(self, "_cms", {})
        self._alloc_idx = getattr(self, "_alloc_idx", 0) + 1
        self._cms[name] = (self._alloc_idx, cm)
        return h

    def release(self, *handles):
        items = sorted((self._cms[h.name][0], h.name) for h in handles)
        for _, name in reversed(items):
            self._cms.pop(name)[1].__exit__(None, None, None)

    def val(self, sem):
        return self.sems[sem][1]

    def _emit_wait(self, eng, sem_name, v):
        if v is None or v <= 0:
            return
        wd = self.waited[eng]
        if wd.get(sem_name, 0) >= v:
            return
        wd[sem_name] = v
        h = self.sems[sem_name][0]
        self.q[eng].append(lambda e, h=h, v=v: e.wait_ge(h, v))

    def _dep_waits(self, eng, r, w):
        need = {}
        for x in r:
            k = _key(x)
            lw = self.last_w.get(k)
            if lw:
                need[lw[0]] = max(need.get(lw[0], 0), lw[1])
        for x in w:
            k = _key(x)
            lw = self.last_w.get(k)
            if lw:
                need[lw[0]] = max(need.get(lw[0], 0), lw[1])
            for sn, v in self.readers.get(k, {}).items():
                need[sn] = max(need.get(sn, 0), v)
        for sn, v in need.items():
            self._emit_wait(eng, sn, v)

    def _dep_update(self, sem_name, v, r, w):
        for x in w:
            k = _key(x)
            self.last_w[k] = (sem_name, v)
            self.readers[k] = {}
        for x in r:
            k = _key(x)
            d = self.readers.setdefault(k, {})
            d[sem_name] = max(d.get(sem_name, 0), v)

    def aop(self, eng, meth, *args, r=(), w=(), sem_name=None, k=1, **kw):
        self._dep_waits(eng, r, w)
        if sem_name is None:
            sem_name = "q_" + eng
        s = self.sems[sem_name]
        s[1] += k
        v = s[1]
        h = s[0]

        def fn(e, meth=meth, args=args, kw=kw, h=h, k=k):
            getattr(e, meth)(*args, **kw).then_inc(h, k)
        self.q[eng].append(fn)
        self._dep_update(sem_name, v, r, w)
        self.nops += 1
        return sem_name, v

    def dma(self, eng, out, in_, **kw):
        sb_side = out if out.space == bass.MemorySpace.SBUF else in_
        sem_name = "dma_" + eng + "_" + _key(sb_side)
        if sem_name not in self.sems:
            self._mk_sem(sem_name)
        v = self.aop(eng, "dma_start", r=[in_], w=[out], sem_name=sem_name,
                     k=16, out=out, in_=in_, **kw)
        d = self.dma_issued[eng]
        d[sem_name] = self.sems[sem_name][1]
        return v

    def mm_group(self, psum_ap, pairs, r_extra=(), eng="tensor"):
        nk = len(pairs)
        r = ([p[0] for p in pairs] + [p[1] for p in pairs] + list(r_extra))
        w = [psum_ap]
        self._dep_waits(eng, r, w)
        sem_name = "q_" + eng
        s = self.sems[sem_name]
        s[1] += 1
        v = s[1]
        h = s[0]

        def fn(e, pairs=pairs, psum_ap=psum_ap, h=h, nk=nk):
            ins = None
            for j, (lhsT, rhs) in enumerate(pairs):
                ins = e.matmul(psum_ap, lhsT, rhs, start=(j == 0),
                               stop=(j == nk - 1))
            ins.then_inc(h, 1)
        self.q[eng].append(fn)
        self._dep_update(sem_name, v, r, w)
        self.nops += nk
        return sem_name, v

    def raw(self, eng, fn):
        self.q[eng].append(fn)

    def flush_engine(self, eng):
        self._emit_wait(eng, "q_" + eng, self.sems["q_" + eng][1])
        for sn, v in self.dma_issued[eng].items():
            self._emit_wait(eng, sn, v)

    def barrier(self, engines=ENGINES):
        if "bar" not in self.sems:
            self._mk_sem("bar")
        s = self.sems["bar"]
        tgt = s[1] + len(engines)
        for e in engines:
            self.flush_engine(e)
            h = s[0]
            self.q[e].append(lambda e_, h=h: e_.sem_inc(h, 1))
        s[1] = tgt
        for e in engines:
            self._emit_wait(e, "bar", tgt)
        self.last_w = {k: ("bar", tgt) for k in self.last_w}
        self.readers = {}

    def build(self):
        with self.nc.Block() as block:
            for eng in ENGINES:
                ops = self.q[eng]
                if not ops:
                    continue
                getattr(block, eng)(lambda e, ops=ops: [fn(e) for fn in ops])
        for _, (_, cm) in sorted(getattr(self, "_cms", {}).items(),
                                 key=lambda kv: -kv[1][0]):
            cm.__exit__(None, None, None)
        self._cms = {}
        self.es.close()
        return self.nc


def emit_identity(S, idn, fill=1.0):
    S.aop("gpsimd", "memset", idn[:], 0.0, w=[idn])
    S.aop("gpsimd", "affine_select", out=idn[:], in_=idn[:],
          compare_op=ALU.not_equal, fill=fill, base=0,
          pattern=[[-1, idn.shape[0]]], channel_multiplier=1, r=[idn], w=[idn])


def _rep(h, nseg, reps):
    return h[:].unsqueeze(2).broadcast_to([128, nseg, reps])


def emit_scale_bcast(S, mx_ap, sca, sca_bc, ones1, ps_bc, rq_cell, mx_keys=()):
    S.aop("vector", "tensor_reduce", sca[0:1, 0:1], mx_ap, AX, ALU.max,
          r=[mx_ap] + list(mx_keys), w=[sca])
    S.aop("vector", "tensor_scalar", sca[0:1, 0:1], sca[0:1, 0:1],
          rq_cell[0:1, 0:1], 1e-12, ALU.mult, ALU.max,
          r=[sca, rq_cell], w=[sca])
    S.aop("vector", "reciprocal", sca[0:1, 1:2], sca[0:1, 0:1], r=[sca], w=[sca])
    S.aop("tensor", "matmul", ps_bc[:, 0:2], ones1[0:1, :], sca[0:1, 0:2],
          start=True, stop=True, r=[ones1, sca], w=[ps_bc])
    S.aop("scalar", "activation", sca_bc[:, 0:2], ps_bc[:, 0:2], AF.Copy,
          r=[ps_bc], w=[sca_bc])


def emit_consts(S):
    ones1 = S.sb("ones1", [1, 128], F32)
    rq_cell = S.sb("rq_cell", [1, 1], F32)
    r7 = S.sb("r7", [128, 1], F32)
    S.aop("vector", "memset", ones1[:], 1.0, w=[ones1])
    S.aop("vector", "memset", rq_cell[:], 32767.0, w=[rq_cell])
    S.aop("vector", "reciprocal", rq_cell[:], rq_cell[:], r=[rq_cell], w=[rq_cell])
    S.aop("vector", "memset", r7[:], 7.0, w=[r7])
    S.aop("vector", "reciprocal", r7[:], r7[:], r=[r7], w=[r7])
    return ones1, rq_cell, r7


def emit_qdq_rounds(S, out_ap, in_ap, sca_bc, tmp_ap, r_keys, w_keys):
    S.aop("vector", "tensor_scalar", tmp_ap, in_ap, sca_bc[:, 1:2], C_RND,
          ALU.mult, ALU.add, r=list(r_keys) + [sca_bc], w=[tmp_ap])
    S.aop("vector", "tensor_scalar", out_ap, tmp_ap, C_RND, sca_bc[:, 0:1],
          ALU.subtract, ALU.mult, r=[tmp_ap, sca_bc], w=list(w_keys))


def build_l2():
    S = Sched("l2")
    nc = S.nc
    x1n = nc.dram_tensor("x1n", [D_MODEL, L_SEQ], F32, kind="ExternalInput").ap()
    mx8 = nc.dram_tensor("mx8", [1, 8], F32, kind="ExternalInput").ap()
    w_hbm = [nc.dram_tensor("w_up", [D_FFN, D_MODEL], F32, kind="ExternalInput").ap(),
             nc.dram_tensor("w_gate", [D_FFN, D_MODEL], F32, kind="ExternalInput").ap()]
    out_hbm = [nc.dram_tensor("up_o", [D_FFN, L_SEQ], F32, kind="ExternalOutput").ap(),
               nc.dram_tensor("gate_o", [D_FFN, L_SEQ], F32, kind="ExternalOutput").ap()]
    m_o = [nc.dram_tensor("m_up", [1, 1], F32, kind="ExternalOutput").ap(),
           nc.dram_tensor("m_gate", [1, 1], F32, kind="ExternalOutput").ap()]

    ND = D_MODEL // 128
    NI = D_FFN // 128

    idn_bf = S.sb("idn_bf", [128, 128], BF16)
    idn_f = S.sb("idn_f", [128, 128], F32)
    mx_sb = S.sb("mx_sb", [1, 8], F32)
    sca = S.sb("sca", [1, 2], F32)
    sca_bc = S.sb("sca_bc", [128, 2], F32)
    xq = [S.sb(f"xq{k}", [128, L_SEQ], BF16) for k in range(ND)]
    xstage = [S.sb(f"xstage{j}", [128, L_SEQ], F32) for j in range(2)]
    xtmp = S.sb("xtmp", [128, L_SEQ], F32)
    wstage = [S.sb(f"wstage{j}", [128, D_MODEL], F32) for j in range(2)]
    wq_bufs = [S.sb(f"wqb{j}", [128, D_MODEL], BF16) for j in range(2)]
    sblk = S.sb("sblk", [128, 32], F32)
    rblk = S.sb("rblk", [128, 32], F32)
    wr_t = S.sb("wr_t", [128, D_MODEL], F32)
    wqT = [[S.sb(f"wqT{w}_{k}", [128, D_FFN], BF16) for k in range(ND)]
           for w in range(2)]
    ostage = [S.sb(f"ostage{j}", [128, L_SEQ], F32) for j in range(2)]
    mrun = S.sb("mrun", [128, 2], F32)
    mtmp = S.sb("mtmp", [128, 1], F32)
    mrow = S.sb("mrow", [2, 128], F32)
    mfin = S.sb("mfin", [2, 1], F32)

    ps_bc = S.psum("ps_bc", [128, 2])
    ps_tr = [S.psum(f"ps_tr{j}", [128, 512], BF16) for j in range(2)]
    ps_mm = [S.psum(f"ps_mm{j}", [128, 512]) for j in range(2)]
    ps_fin = S.psum("ps_fin", [2, 128])

    emit_identity(S, idn_bf)
    emit_identity(S, idn_f)
    ones1, rq_cell, r7 = emit_consts(S)
    S.aop("vector", "memset", mrun[:], 0.0, w=[mrun])
    S.dma("sync", mx_sb[:], mx8[:])

    emit_scale_bcast(S, mx_sb[0:1, 0:8], sca, sca_bc, ones1, ps_bc, rq_cell)

    for k in range(ND):
        stage = xstage[k % 2]
        S.dma("sync", stage[:], x1n[k * 128:(k + 1) * 128, :])
        emit_qdq_rounds(S, xq[k][:], stage[:], sca_bc, xtmp[:],
                        r_keys=[stage], w_keys=[xq[k]])

    for w in range(2):
        for t in range(NI):
            st = wstage[t % 2]
            S.dma("sync", st[:], w_hbm[w][t * 128:(t + 1) * 128, :])
            S.aop("vector", "tensor_reduce", sblk[:],
                  st[:].rearrange("p (a b) -> p a b", a=32), AX, ALU.max,
                  apply_absolute_value=True, r=[st], w=[sblk])
            S.aop("vector", "tensor_scalar", sblk[:], sblk[:], r7[:, 0:1], 1e-12,
                  ALU.mult, ALU.max, r=[sblk, r7], w=[sblk])
            S.aop("vector", "reciprocal", rblk[:], sblk[:], r=[sblk], w=[rblk])
            S.aop("vector", "tensor_tensor",
                  out=wr_t[:].rearrange("p (a b) -> p a b", a=32),
                  in0=st[:].rearrange("p (a b) -> p a b", a=32),
                  in1=_rep(rblk, 32, 32), op=ALU.mult,
                  r=[st, rblk], w=[wr_t])
            S.aop("vector", "tensor_scalar", wr_t[:], wr_t[:], C_RND, C_RND,
                  ALU.add, ALU.subtract, r=[wr_t], w=[wr_t])
            wq = wq_bufs[t % 2]
            S.aop("vector", "tensor_tensor",
                  out=wq[:].rearrange("p (a b) -> p a b", a=32),
                  in0=wr_t[:].rearrange("p (a b) -> p a b", a=32),
                  in1=_rep(sblk, 32, 32), op=ALU.mult,
                  r=[wr_t, sblk], w=[wq])
            for half in range(2):
                bank = ps_tr[half]
                for j in range(4):
                    kblk = half * 4 + j
                    S.aop("tensor", "transpose", bank[:, j * 128:(j + 1) * 128],
                          wq[:, kblk * 128:(kblk + 1) * 128], idn_bf[:],
                          r=[wq, idn_bf], w=[bank])
                for j in range(4):
                    kblk = half * 4 + j
                    S.aop("scalar", "activation",
                          wqT[w][kblk][:, t * 128:(t + 1) * 128],
                          bank[:, j * 128:(j + 1) * 128], AF.Copy,
                          r=[bank], w=[f"wqT{w}_{kblk}:{t}"])
    for w in range(2):
        for t in range(NI):
            stg = ostage[t % 2]
            for lc in range(4):
                bank = ps_mm[lc % 2]
                pairs = [(wqT[w][k][:, t * 128:(t + 1) * 128],
                          xq[k][:, lc * 512:(lc + 1) * 512]) for k in range(ND)]
                S.mm_group(bank[:], pairs,
                           r_extra=[f"wqT{w}_{k}:{t}" for k in range(ND)])
                S.aop("scalar", "activation", stg[:, lc * 512:(lc + 1) * 512],
                      bank[:], AF.Copy, r=[bank], w=[stg])
            S.dma("sync", out_hbm[w][t * 128:(t + 1) * 128, :], stg[:])
            S.aop("vector", "tensor_reduce", mtmp[:], stg[:], AX, ALU.max,
                  apply_absolute_value=True, r=[stg], w=[mtmp])
            S.aop("vector", "tensor_tensor", mrun[:, w:w + 1], mrun[:, w:w + 1],
                  mtmp[:], ALU.max, r=[mrun, mtmp], w=[mrun])

    S.aop("tensor", "transpose", ps_fin[:], mrun[:, 0:2], idn_f[:],
          r=[mrun, idn_f], w=[ps_fin])
    S.aop("scalar", "activation", mrow[:], ps_fin[:], AF.Copy, r=[ps_fin], w=[mrow])
    S.aop("vector", "tensor_reduce", mfin[:], mrow[:], AX, ALU.max,
          r=[mrow], w=[mfin])
    S.dma("sync", m_o[0][:], mfin[0:1, :])
    S.dma("sync", m_o[1][:], mfin[1:2, :])
    S.flush_engine("sync")
    return S.build()


def host_l2_inputs(x1n_fm, mx8, w_up, w_gate):
    return {"x1n": np.ascontiguousarray(x1n_fm), "mx8": mx8,
            "w_up": w_up, "w_gate": w_gate}


def build_l3():
    S = Sched("l3")
    nc = S.nc
    up_i = nc.dram_tensor("up_i", [D_FFN, L_SEQ], F32, kind="ExternalInput").ap()
    gate_i = nc.dram_tensor("gate_i", [D_FFN, L_SEQ], F32, kind="ExternalInput").ap()
    mup8 = nc.dram_tensor("mup8", [1, 8], F32, kind="ExternalInput").ap()
    mg8 = nc.dram_tensor("mg8", [1, 8], F32, kind="ExternalInput").ap()
    upq_o = nc.dram_tensor("upq_o", [D_FFN, L_SEQ], F32, kind="ExternalOutput").ap()
    gate2_o = nc.dram_tensor("gate2_o", [D_FFN, L_SEQ], F32, kind="ExternalOutput").ap()
    m_g2 = nc.dram_tensor("m_g2", [1, 1], F32, kind="ExternalOutput").ap()
    NI = D_FFN // 128

    idn_f = S.sb("idn_f", [128, 128], F32)
    mx_sb = S.sb("mx_sb", [1, 16], F32)
    sca_u = S.sb("sca_u", [1, 2], F32)
    sca_g = S.sb("sca_g", [1, 2], F32)
    scb_u = S.sb("scb_u", [128, 2], F32)
    scb_g = S.sb("scb_g", [128, 2], F32)
    ustage = [S.sb(f"ustage{j}", [128, L_SEQ], F32) for j in range(3)]
    gstage = [S.sb(f"gstage{j}", [128, L_SEQ], F32) for j in range(3)]
    uq = [S.sb(f"uq{j}", [128, L_SEQ], F32) for j in range(3)]
    gq = [S.sb(f"gq{j}", [128, L_SEQ], F32) for j in range(3)]
    sig_t = S.sb("sig_t", [128, L_SEQ], F32)
    g2 = [S.sb(f"g2_{j}", [128, L_SEQ], F32) for j in range(3)]
    xtmp = S.sb("xtmp", [128, L_SEQ], F32)
    mrun = S.sb("mrun", [128, 1], F32)
    mtmp = S.sb("mtmp", [128, 1], F32)
    mrow = S.sb("mrow", [1, 128], F32)
    mfin = S.sb("mfin", [1, 1], F32)
    ps_bc = S.psum("ps_bc", [128, 4])
    ps_fin = S.psum("ps_fin", [1, 128])

    emit_identity(S, idn_f)
    ones1, rq_cell, r7 = emit_consts(S)
    S.aop("vector", "memset", mrun[:], 0.0, w=[mrun])
    S.dma("sync", mx_sb[0:1, 0:8], mup8[:])
    S.dma("sync", mx_sb[0:1, 8:16], mg8[:])
    emit_scale_bcast(S, mx_sb[0:1, 0:8], sca_u, scb_u, ones1, ps_bc[:, 0:2], rq_cell)
    emit_scale_bcast(S, mx_sb[0:1, 8:16], sca_g, scb_g, ones1, ps_bc[:, 2:4], rq_cell)

    for t in range(NI):
        ust = ustage[t % 3]
        gst = gstage[t % 3]
        S.dma("sync", ust[:], up_i[t * 128:(t + 1) * 128, :])
        S.dma("gpsimd", gst[:], gate_i[t * 128:(t + 1) * 128, :])
        emit_qdq_rounds(S, uq[t % 3][:], ust[:], scb_u, xtmp[:],
                        r_keys=[ust], w_keys=[uq[t % 3]])
        S.dma("scalar", upq_o[t * 128:(t + 1) * 128, :], uq[t % 3][:])
        emit_qdq_rounds(S, gq[t % 3][:], gst[:], scb_g, xtmp[:],
                        r_keys=[gst], w_keys=[gq[t % 3]])
        S.aop("scalar", "activation", sig_t[:], gq[t % 3][:], AF.Sigmoid,
              r=[gq[t % 3]], w=[sig_t])
        S.aop("vector", "tensor_scalar", sig_t[:], sig_t[:], 65536.0, C_RND,
              ALU.mult, ALU.add, r=[sig_t], w=[sig_t])
        S.aop("vector", "tensor_scalar", sig_t[:], sig_t[:], C_RND, 65535.0,
              ALU.subtract, ALU.min, r=[sig_t], w=[sig_t])
        S.aop("vector", "scalar_tensor_tensor", out=g2[t % 3][:], in0=sig_t[:],
              scalar=1.0 / 65536.0, in1=gq[t % 3][:], op0=ALU.mult, op1=ALU.mult,
              r=[sig_t, gq[t % 3]], w=[g2[t % 3]])
        S.dma("scalar", gate2_o[t * 128:(t + 1) * 128, :], g2[t % 3][:])
        S.aop("vector", "tensor_reduce", mtmp[:], g2[t % 3][:], AX, ALU.max,
              apply_absolute_value=True, r=[g2[t % 3]], w=[mtmp])
        S.aop("vector", "tensor_tensor", mrun[:], mrun[:], mtmp[:], ALU.max,
              r=[mrun, mtmp], w=[mrun])

    S.aop("tensor", "transpose", ps_fin[:], mrun[:], idn_f[:],
          r=[mrun, idn_f], w=[ps_fin])
    S.aop("scalar", "activation", mrow[:], ps_fin[:], AF.Copy, r=[ps_fin], w=[mrow])
    S.aop("vector", "tensor_reduce", mfin[:], mrow[:], AX, ALU.max, r=[mrow], w=[mfin])
    S.dma("sync", m_g2[:], mfin[:])
    S.flush_engine("sync")
    return S.build()


def build_l4():
    S = Sched("l4")
    nc = S.nc
    upq_i = nc.dram_tensor("upq_i", [D_FFN, L_SEQ], F32, kind="ExternalInput").ap()
    g2_i = nc.dram_tensor("g2_i", [D_FFN, L_SEQ], F32, kind="ExternalInput").ap()
    mg28 = nc.dram_tensor("mg28", [1, 8], F32, kind="ExternalInput").ap()
    o_o = nc.dram_tensor("o_o", [D_FFN, L_SEQ], F32, kind="ExternalOutput").ap()
    m_o = nc.dram_tensor("m_o", [1, 1], F32, kind="ExternalOutput").ap()
    NI = D_FFN // 128

    idn_f = S.sb("idn_f", [128, 128], F32)
    mx_sb = S.sb("mx_sb", [1, 8], F32)
    sca = S.sb("sca", [1, 2], F32)
    scb = S.sb("scb", [128, 2], F32)
    ustage = [S.sb(f"ustage{j}", [128, L_SEQ], F32) for j in range(3)]
    gstage = [S.sb(f"gstage{j}", [128, L_SEQ], F32) for j in range(3)]
    ot = [S.sb(f"ot{j}", [128, L_SEQ], F32) for j in range(3)]
    xtmp = S.sb("xtmp", [128, L_SEQ], F32)
    mrun = S.sb("mrun", [128, 1], F32)
    mtmp = S.sb("mtmp", [128, 1], F32)
    mrow = S.sb("mrow", [1, 128], F32)
    mfin = S.sb("mfin", [1, 1], F32)
    ps_bc = S.psum("ps_bc", [128, 2])
    ps_fin = S.psum("ps_fin", [1, 128])

    emit_identity(S, idn_f)
    ones1, rq_cell, r7 = emit_consts(S)
    S.aop("vector", "memset", mrun[:], 0.0, w=[mrun])
    S.dma("sync", mx_sb[:], mg28[:])
    emit_scale_bcast(S, mx_sb[0:1, 0:8], sca, scb, ones1, ps_bc[:, 0:2], rq_cell)

    for t in range(NI):
        ust = ustage[t % 3]
        gst = gstage[t % 3]
        S.dma("sync", ust[:], upq_i[t * 128:(t + 1) * 128, :])
        S.dma("gpsimd", gst[:], g2_i[t * 128:(t + 1) * 128, :])
        emit_qdq_rounds(S, gst[:], gst[:], scb, xtmp[:],
                        r_keys=[gst], w_keys=[gst])
        S.aop("vector", "tensor_tensor", ot[t % 3][:], gst[:], ust[:], ALU.mult,
              r=[gst, ust], w=[ot[t % 3]])
        S.dma("scalar", o_o[t * 128:(t + 1) * 128, :], ot[t % 3][:])
        S.aop("vector", "tensor_reduce", mtmp[:], ot[t % 3][:], AX, ALU.max,
              apply_absolute_value=True, r=[ot[t % 3]], w=[mtmp])
        S.aop("vector", "tensor_tensor", mrun[:], mrun[:], mtmp[:], ALU.max,
              r=[mrun, mtmp], w=[mrun])

    S.aop("tensor", "transpose", ps_fin[:], mrun[:], idn_f[:],
          r=[mrun, idn_f], w=[ps_fin])
    S.aop("scalar", "activation", mrow[:], ps_fin[:], AF.Copy, r=[ps_fin], w=[mrow])
    S.aop("vector", "tensor_reduce", mfin[:], mrow[:], AX, ALU.max, r=[mrow], w=[mfin])
    S.dma("sync", m_o[:], mfin[:])
    S.flush_engine("sync")
    return S.build()


class PlainLN:

    def __init__(self, S, nd, L, pref, ones1, ones_col, idn_f, eps=1e-5,
                 ps_banks=None):
        self.S, self.nd, self.L = S, nd, L
        self.ones1, self.ones_col, self.idn_f = ones1, ones_col, idn_f
        self.NLS = L // 128
        assert self.NLS <= 16
        self.sc = 1.0 / (nd * 128)
        self.ps_su = S.psum(pref + "ps_su", [128, self.NLS])
        self.ps_tp = S.psum(pref + "ps_tp", [1, 512])
        self.ps_bc = (ps_banks if ps_banks is not None else
                      [S.psum(pref + f"ps_bc{j}", [128, 512]) for j in range(2)])
        self.mu_cols = S.sb(pref + "mu_cols", [128, self.NLS], F32)
        self.st_row = S.sb(pref + "st_row", [1, L], F32)
        self.mu_bc = S.sb(pref + "mu_bc", [128, L], F32)
        self.rstd_bc = S.sb(pref + "rstd_bc", [128, L], F32)
        self.var_cols = S.sb(pref + "var_cols", [128, self.NLS], F32)
        self.eps_col = S.sb(pref + "eps_col", [128, 1], F32)
        S.aop("vector", "memset", self.eps_col[:], eps, w=[self.eps_col])

    def _stat_pass(self, tiles, out_cols):
        S, NLS, nd = self.S, self.NLS, self.nd
        for ls in range(NLS):
            pairs = [(tiles[k][:, ls * 128:(ls + 1) * 128], self.ones_col[:, 0:1])
                     for k in range(nd)]
            S.mm_group(self.ps_su[:, ls:ls + 1], pairs)
        S.aop("scalar", "activation", out_cols[:], self.ps_su[:], AF.Copy,
              scale=self.sc, r=[self.ps_su], w=[out_cols])

    def _bcast_cols(self, cols, dst_bc):
        S, NLS = self.S, self.NLS
        for g in range(0, NLS, 4):
            gn = min(4, NLS - g)
            for j in range(gn):
                S.aop("tensor", "transpose", self.ps_tp[0:1, j * 128:(j + 1) * 128],
                      cols[:, g + j:g + j + 1], self.idn_f[:],
                      r=[cols, self.idn_f], w=[self.ps_tp])
            S.aop("scalar", "activation", self.st_row[0:1, g * 128:(g + gn) * 128],
                  self.ps_tp[0:1, 0:gn * 128], AF.Copy, r=[self.ps_tp],
                  w=[self.st_row])
        for g in range(0, NLS, 4):
            gn = min(4, NLS - g)
            bank = self.ps_bc[(g // 4) % 2]
            S.aop("tensor", "matmul", bank[:, 0:gn * 128], self.ones1[0:1, :],
                  self.st_row[0:1, g * 128:(g + gn) * 128], start=True, stop=True,
                  r=[self.ones1, self.st_row], w=[bank])
            S.aop("scalar", "activation", dst_bc[:, g * 128:(g + gn) * 128],
                  bank[:, 0:gn * 128], AF.Copy, r=[bank], w=[dst_bc])

    def emit_from_stats(self, tiles_in, tiles_out, mu_cols_ap, var_cols_ap,
                        extra_r=()):
        S, nd = self.S, self.nd
        S.aop("scalar", "activation", var_cols_ap, var_cols_ap, AF.Ln,
              bias=self.eps_col[:, 0:1],
              r=[var_cols_ap, self.eps_col] + list(extra_r), w=[var_cols_ap])
        S.aop("scalar", "activation", var_cols_ap, var_cols_ap, AF.Exp,
              scale=-0.5, r=[var_cols_ap], w=[var_cols_ap])
        self._bcast_cols(mu_cols_ap, self.mu_bc)
        self._bcast_cols(var_cols_ap, self.rstd_bc)
        for k in range(nd):
            S.aop("vector", "tensor_sub", tiles_in[k], tiles_in[k], self.mu_bc[:],
                  r=[tiles_in[k], self.mu_bc], w=[tiles_in[k]])
            S.aop("vector", "tensor_mul", tiles_out[k], tiles_in[k],
                  self.rstd_bc[:], r=[tiles_in[k], self.rstd_bc],
                  w=[tiles_out[k]])

    def emit(self, tiles_in, tiles_out, tiles_tmp, tiles_final=None):
        S, nd = self.S, self.nd
        fin = tiles_final if tiles_final is not None else tiles_out
        self._stat_pass(tiles_in, self.mu_cols)
        self._bcast_cols(self.mu_cols, self.mu_bc)
        for k in range(nd):
            S.aop("vector", "tensor_sub", tiles_out[k], tiles_in[k], self.mu_bc[:],
                  r=[tiles_in[k], self.mu_bc], w=[tiles_out[k]])
            S.aop("scalar", "activation", tiles_tmp[k], tiles_out[k], AF.Square,
                  r=[tiles_out[k]], w=[tiles_tmp[k]])
        self._stat_pass(tiles_tmp, self.var_cols)
        S.aop("scalar", "activation", self.var_cols[:], self.var_cols[:], AF.Ln,
              bias=self.eps_col[:, 0:1], r=[self.var_cols, self.eps_col],
              w=[self.var_cols])
        S.aop("scalar", "activation", self.var_cols[:], self.var_cols[:], AF.Exp,
              scale=-0.5, r=[self.var_cols], w=[self.var_cols])
        self._bcast_cols(self.var_cols, self.rstd_bc)
        for k in range(nd):
            S.aop("vector", "tensor_mul", fin[k], tiles_out[k],
                  self.rstd_bc[:], r=[tiles_out[k], self.rstd_bc],
                  w=[fin[k]])


def build_l5():
    S = Sched("l5")
    nc = S.nc
    o_i = nc.dram_tensor("o_i", [D_FFN, L_SEQ], F32, kind="ExternalInput").ap()
    x1n = nc.dram_tensor("x1n", [D_MODEL, L_SEQ], F32, kind="ExternalInput").ap()
    mo8 = nc.dram_tensor("mo8", [1, 8], F32, kind="ExternalInput").ap()
    w_down = nc.dram_tensor("w_down", [D_MODEL, D_FFN], F32, kind="ExternalInput").ap()
    out_o = nc.dram_tensor("out_o", [D_MODEL, L_SEQ], F32, kind="ExternalOutput").ap()
    NI = D_FFN // 128
    ND = D_MODEL // 128

    idn_f = S.sb("idn_f", [128, 128], F32)
    idn_bf = S.sb("idn_bf", [128, 128], BF16)
    ones_col = S.sb("ones_col", [128, 1], F32)
    mx_sb = S.sb("mx_sb", [1, 8], F32)
    sca = S.sb("sca", [1, 2], F32)
    scb = S.sb("scb", [128, 2], F32)
    ps_bc = S.psum("ps_bc", [128, 2])
    emit_identity(S, idn_f)
    emit_identity(S, idn_bf)
    ones1, rq_cell, r7 = emit_consts(S)
    S.aop("vector", "memset", ones_col[:], 1.0, w=[ones_col])
    S.dma("sync", mx_sb[:], mo8[:])
    emit_scale_bcast(S, mx_sb[0:1, 0:8], sca, scb, ones1, ps_bc[:, 0:2], rq_cell)
    x2 = [S.sb(f"x2_{k}", [128, L_SEQ], F32) for k in range(ND)]
    wdT = [S.sb(f"wdT{k}", [128, D_MODEL], BF16) for k in range(NI)]

    wstage = [S.sb("wstage0", [128, D_FFN], F32)] * 2
    wq_bufs = [S.sb(f"wqb{j}", [128, D_FFN], BF16) for j in range(2)]
    sblk = S.sb("sblk", [128, 96], F32)
    rblk = S.sb("rblk", [128, 96], F32)
    ps_tr = [S.psum(f"ps_tr{j}", [128, 512], BF16) for j in range(2)]
    oq = [S.sb(f"oq{k}", [128, 1024], BF16) for k in range(NI)]
    ostage = [S.sb(f"ostage{j}", [128, 1024], F32) for j in range(2)]
    xtmp = S.sb("xtmp", [128, 1024], F32)
    x1st = [S.sb(f"x1st{j}", [128, 512], F32) for j in range(2)]
    ps_mm = [S.psum(f"ps_mm{j}", [128, 512]) for j in range(2)]
    for lh in range(2):
        co = lh * 1024
        for t in range(NI):
            st = ostage[t % 2]
            S.dma("sync", st[:], o_i[t * 128:(t + 1) * 128, co:co + 1024])
            emit_qdq_rounds(S, oq[t][:], st[:], scb, xtmp[:],
                            r_keys=[st], w_keys=[oq[t]])
        for t in range(ND):
            if lh == 0:
                st = wstage[t % 2]
                S.dma("sync", st[:], w_down[t * 128:(t + 1) * 128, :])
                S.aop("vector", "tensor_reduce", sblk[:],
                      st[:].rearrange("p (a b) -> p a b", a=96), AX, ALU.max,
                      apply_absolute_value=True, r=[st], w=[sblk])
                S.aop("vector", "tensor_scalar", sblk[:], sblk[:], r7[:, 0:1],
                      1e-12, ALU.mult, ALU.max, r=[sblk, r7], w=[sblk])
                S.aop("vector", "reciprocal", rblk[:], sblk[:], r=[sblk], w=[rblk])
                S.aop("vector", "tensor_tensor",
                      out=st[:].rearrange("p (a b) -> p a b", a=96),
                      in0=st[:].rearrange("p (a b) -> p a b", a=96),
                      in1=_rep(rblk, 96, 32), op=ALU.mult, r=[st, rblk], w=[st])
                S.aop("vector", "tensor_scalar", st[:], st[:], C_RND, C_RND,
                      ALU.add, ALU.subtract, r=[st], w=[st])
                wq = wq_bufs[t % 2]
                S.aop("vector", "tensor_tensor",
                      out=wq[:].rearrange("p (a b) -> p a b", a=96),
                      in0=st[:].rearrange("p (a b) -> p a b", a=96),
                      in1=_rep(sblk, 96, 32), op=ALU.mult, r=[st, sblk], w=[wq])
                for half in range(6):
                    bank = ps_tr[half % 2]
                    for j in range(4):
                        kblk = half * 4 + j
                        S.aop("tensor", "transpose",
                              bank[:, j * 128:(j + 1) * 128],
                              wq[:, kblk * 128:(kblk + 1) * 128], idn_bf[:],
                              r=[wq, idn_bf], w=[bank])
                    for j in range(4):
                        kblk = half * 4 + j
                        S.aop("scalar", "activation",
                              wdT[kblk][:, t * 128:(t + 1) * 128],
                              bank[:, j * 128:(j + 1) * 128], AF.Copy,
                              r=[bank], w=[f"wdT{kblk}:{t}"])
            for lc in range(2):
                sl = co + lc * 512
                xs = x1st[lc % 2]
                S.dma("sync", xs[:], x1n[t * 128:(t + 1) * 128, sl:sl + 512])
                bank = ps_mm[lc % 2]
                pairs = [(wdT[k][:, t * 128:(t + 1) * 128],
                          oq[k][:, lc * 512:(lc + 1) * 512]) for k in range(NI)]
                S.mm_group(bank[:], pairs,
                           r_extra=[f"wdT{k}:{t}" for k in range(NI)])
                S.aop("vector", "tensor_add", x2[t][:, sl:sl + 512],
                      bank[:], xs[:], r=[bank, xs], w=[x2[t]])
    S.barrier()
    S.release(*oq, ostage[0], ostage[1], xtmp, x1st[0], x1st[1],
              ps_mm[0], ps_mm[1], ps_tr[0], ps_tr[1], wstage[0],
              wq_bufs[0], wq_bufs[1], sblk, rblk)
    S.release(*wdT)

    x2t = [S.sb(f"x2t_{k}", [128, L_SEQ], F32) for k in range(ND)]
    ln2 = PlainLN(S, ND, L_SEQ, "ln2", ones1, ones_col, idn_f)
    ln2.emit([t[:] for t in x2], [t[:] for t in x2t], [t[:] for t in x2])
    for k in range(ND):
        S.dma("sync" if k % 2 == 0 else "scalar",
              out_o[k * 128:(k + 1) * 128, :], x2t[k][:])
    S.flush_engine("sync")
    S.flush_engine("scalar")
    return S.build()


def build_l1():
    S = Sched("l1")
    nc = S.nc
    x_i = nc.dram_tensor("x", [D_MODEL, L_SEQ], F32, kind="ExternalInput").ap()
    ln_g = nc.dram_tensor("ln_g", [1, D_MODEL], F32, kind="ExternalInput").ap()
    ln_b = nc.dram_tensor("ln_b", [1, D_MODEL], F32, kind="ExternalInput").ap()
    W_in = nc.dram_tensor("W_in", [D_MODEL, 2 * D_INNER], F32, kind="ExternalInput").ap()
    conv_w = nc.dram_tensor("conv_w", [D_INNER, 4], F32, kind="ExternalInput").ap()
    conv_b = nc.dram_tensor("conv_b", [1, D_INNER], F32, kind="ExternalInput").ap()
    W_xp = nc.dram_tensor("W_xp", [D_INNER, 96], F32, kind="ExternalInput").ap()
    W_dt = nc.dram_tensor("W_dt", [64, D_INNER], F32, kind="ExternalInput").ap()
    b_dt = nc.dram_tensor("b_dt", [1, D_INNER], F32, kind="ExternalInput").ap()
    A_log = nc.dram_tensor("A_log", [D_INNER, 16], F32, kind="ExternalInput").ap()
    D_par = nc.dram_tensor("D_par", [1, D_INNER], F32, kind="ExternalInput").ap()
    res_sc = nc.dram_tensor("res_sc", [1, 1], F32, kind="ExternalInput").ap()
    W_out_h = nc.dram_tensor("W_out", [D_INNER, D_MODEL], F32, kind="ExternalInput").ap()
    x1n_o = nc.dram_tensor("x1n_o", [D_MODEL, L_SEQ], F32, kind="ExternalOutput").ap()
    mx_o = nc.dram_tensor("mx_o", [1, 1], F32, kind="ExternalOutput").ap()
    sp_u2 = nc.dram_tensor("sp_u2", [D_INNER, L_SEQ], F32, kind="Internal").ap()
    sp_dt = nc.dram_tensor("sp_dt", [D_INNER, L_SEQ], F32, kind="Internal").ap()
    sp_z = nc.dram_tensor("sp_z", [D_INNER, L_SEQ], F32, kind="Internal").ap()
    sp_x2 = nc.dram_tensor("sp_x2", [D_MODEL, L_SEQ], F32, kind="Internal").ap()

    NE = D_INNER // 128
    ND = D_MODEL // 128
    TL1 = 512
    NCH1 = L_SEQ // TL1
    TL2 = 256
    NCH2 = L_SEQ // TL2
    NST = 16

    idn_f = S.sb("idn_f", [128, 128], F32)
    ones1 = S.sb("ones1", [1, 128], F32)
    ones_col = S.sb("ones_col", [128, 1], F32)
    gcol = S.sb("gcol", [128, ND], F32)
    bcol = S.sb("bcol", [128, ND], F32)
    cw = S.sb("cw", [128, 4 * NE], F32)
    cb = S.sb("cb", [128, NE], F32)
    bdt = S.sb("bdt", [128, NE], F32)
    Dp = S.sb("Dp", [128, NE], F32)
    A_sb = S.sb("A_sb", [128, NST * NE], F32)
    rsc = S.sb("rsc", [128, 1], F32)
    rsc_st = S.sb("rsc_st", [1, 1], F32)
    bu_acc = S.sb("bu_acc", [128, 32], F32)
    butmp = S.sb("butmp", [128, 32], F32)
    wxp_bf = S.sb("wxp_bf", [128, 96 * NE], BF16)
    wdt_bf = S.sb("wdt_bf", [64, D_INNER], BF16)
    bc_sb = S.sb("bc_sb", [32, L_SEQ], F32)
    sel_sb = S.sb("sel_sb", [32, 32 * 128], F32)
    mxrun = S.sb("mxrun", [128, 1], F32)
    mxtmp = S.sb("mxtmp", [128, 1], F32)
    mrow = S.sb("mrow", [1, 128], F32)
    mfin = S.sb("mfin", [1, 1], F32)

    emit_identity(S, idn_f)
    S.aop("vector", "memset", ones1[:], 1.0, w=[ones1])
    S.aop("vector", "memset", ones_col[:], 1.0, w=[ones_col])
    S.aop("vector", "memset", mxrun[:], 0.0, w=[mxrun])
    S.aop("gpsimd", "memset", sel_sb[:], 0.0, w=[sel_sb])
    S.aop("gpsimd", "affine_select",
          out=sel_sb[:].rearrange("p (a b) -> p a b", a=32),
          in_=sel_sb[:].rearrange("p (a b) -> p a b", a=32),
          compare_op=ALU.not_equal, fill=1.0, base=0,
          pattern=[[-1, 32], [0, 128]], channel_multiplier=1,
          r=[sel_sb], w=[sel_sb])
    for k in range(ND):
        S.dma("sync", gcol[:, k:k + 1], ln_g[0:1, k * 128:(k + 1) * 128])
        S.dma("sync", bcol[:, k:k + 1], ln_b[0:1, k * 128:(k + 1) * 128])
    for t in range(NE):
        S.dma("sync", cw[:, 4 * t:4 * t + 4], conv_w[t * 128:(t + 1) * 128, :])
        S.dma("sync", cb[:, t:t + 1], conv_b[0:1, t * 128:(t + 1) * 128])
        S.dma("sync", bdt[:, t:t + 1], b_dt[0:1, t * 128:(t + 1) * 128])
        S.dma("sync", Dp[:, t:t + 1], D_par[0:1, t * 128:(t + 1) * 128])
        S.dma("sync", A_sb[:, NST * t:NST * (t + 1)],
              A_log[t * 128:(t + 1) * 128, :])
        S.dma("gpsimd", wxp_bf[:, 96 * t:96 * (t + 1)],
              W_xp[t * 128:(t + 1) * 128, :])
    S.dma("gpsimd", wdt_bf[:], W_dt[:])
    S.dma("sync", rsc_st[:], res_sc[:])
    S.aop("scalar", "activation", A_sb[:], A_sb[:], AF.Exp, r=[A_sb], w=[A_sb])
    S.aop("vector", "tensor_scalar", A_sb[:], A_sb[:], -1.0, None, ALU.mult,
          r=[A_sb], w=[A_sb])

    win_bf = [S.sb(f"winbf{k}", [128, 2 * D_INNER], BF16) for k in range(ND)]
    winst2 = [S.sb("winst0", [128, 2 * D_INNER], F32)] * 2
    xdbl_dt = S.sb("xdbl_dt", [64, L_SEQ], BF16)
    ps_a = S.psum("ps_a", [128, 512])
    ps_b = S.psum("ps_b", [128, 512])
    ps_c = S.psum("ps_c", [128, 512])
    ps_d = S.psum("ps_d", [128, 512])
    ps_e = S.psum("ps_e", [128, 512])
    ps_f = S.psum("ps_f", [128, 512])
    ps_bu = ps_e
    ps4 = [ps_a, ps_b, ps_c, ps_d]

    S.aop("tensor", "matmul", ps_bu[:, 0:1], ones1[0:1, :], rsc_st[0:1, 0:1],
          start=True, stop=True, r=[ones1, rsc_st], w=["psbu0"])
    S.aop("scalar", "activation", rsc[:], ps_bu[:, 0:1], AF.Copy,
          r=["psbu0"], w=[rsc])

    S.aop("vector", "memset", bu_acc[:], 0.0, w=[bu_acc])
    for k in range(ND):
        winst = winst2[k % 2]
        S.dma("gpsimd", winst[:], W_in[k * 128:(k + 1) * 128, :])
        for s in range(32):
            S.aop("tensor", "matmul", ps_bu[:, s:s + 1],
                  winst[:, s * 128:(s + 1) * 128], bcol[:, k:k + 1],
                  start=True, stop=True, r=[winst, bcol], w=[f"psbu{s}"])
        S.aop("scalar", "activation", butmp[:], ps_bu[:, 0:32], AF.Copy,
              r=[f"psbu{s}" for s in range(32)], w=[butmp])
        S.aop("vector", "tensor_add", bu_acc[:], bu_acc[:], butmp[:],
              r=[bu_acc, butmp], w=[bu_acc])
        S.aop("scalar", "activation", win_bf[k][:], winst[:], AF.Identity,
              scale=gcol[:, k:k + 1], r=[winst, gcol], w=[win_bf[k]])

    x_fm = [S.sb(f"xfm{k}", [128, TL1], F32) for k in range(ND)]
    xn_bf = [S.sb(f"xnbf{k}", [128, TL1], BF16) for k in range(ND)]
    ln0tmp = [S.sb(f"ln0tmp{k}", [128, TL1], F32) for k in range(ND)]
    x2st = S.sb("x2st", [128, TL1], F32)
    u_sb = [S.sb(f"usb{j}", [128, TL1 + 3], F32) for j in range(2)]
    u_tail = S.sb("u_tail", [128, 3 * NE], F32)
    cacc = S.sb("cacc", [128, TL1], F32)
    u2f = S.sb("u2f", [128, TL1], F32)
    u2bf = [S.sb(f"u2bf{t}", [128, TL1], BF16) for t in range(NE)]
    zs_sb = S.sb("zs_sb", [128, TL1], F32)
    dtf = S.sb("dtf", [128, TL1], F32)
    sp_t1 = S.sb("sp_t1", [128, TL1], F32)
    sp_t2 = S.sb("sp_t2", [128, TL1], F32)
    ln0 = PlainLN(S, ND, TL1, "ln0", ones1, ones_col, idn_f,
                  ps_banks=[ps_c, ps_d])
    S.aop("vector", "memset", u_tail[:], 0.0, w=[u_tail])

    for c in range(NCH1):
        c0 = c * TL1
        for k in range(ND):
            S.dma("sync", x_fm[k][:], x_i[k * 128:(k + 1) * 128, c0:c0 + TL1])
        for k in range(ND):
            S.aop("vector", "tensor_scalar", x2st[:], x_fm[k][:], 2.0, None,
                  ALU.mult, r=[x_fm[k]], w=[x2st])
            S.dma("gpsimd", sp_x2[k * 128:(k + 1) * 128, c0:c0 + TL1], x2st[:])
        ln0.emit([t[:] for t in x_fm], [t[:] for t in x_fm],
                 [t[:] for t in ln0tmp], tiles_final=[t[:] for t in xn_bf])
        for t in range(NE):
            bank = ps4[t % 4]
            pairs = [(win_bf[k][:, t * 128:(t + 1) * 128], xn_bf[k][:])
                     for k in range(ND)]
            S.mm_group(bank[:], pairs)
            usb = u_sb[t % 2]
            S.aop("vector", "tensor_copy", usb[:, 0:3], u_tail[:, 3 * t:3 * t + 3],
                  r=[u_tail], w=[usb])
            S.aop("scalar", "activation", usb[:, 3:3 + TL1], bank[:],
                  AF.Identity, bias=bu_acc[:, t:t + 1], r=[bank, bu_acc],
                  w=[usb])
            S.aop("vector", "tensor_copy", u_tail[:, 3 * t:3 * t + 3],
                  usb[:, TL1:TL1 + 3], r=[usb], w=[u_tail])
            S.aop("vector", "tensor_scalar", cacc[:], usb[:, 0:TL1],
                  cw[:, 4 * t:4 * t + 1], None, ALU.mult,
                  r=[usb, cw], w=[cacc])
            for q in range(1, 4):
                S.aop("vector", "scalar_tensor_tensor", out=cacc[:],
                      in0=usb[:, q:q + TL1], scalar=cw[:, 4 * t + q:4 * t + q + 1],
                      in1=cacc[:], op0=ALU.mult, op1=ALU.add,
                      r=[usb, cw, cacc], w=[cacc])
            S.aop("scalar", "activation", u2f[:], cacc[:], AF.Silu,
                  bias=cb[:, t:t + 1], r=[cacc, cb], w=[u2f])
            S.aop("vector", "tensor_copy", u2bf[t][:], u2f[:],
                  r=[u2f], w=[u2bf[t]])
            S.dma("gpsimd", sp_u2[t * 128:(t + 1) * 128, c0:c0 + TL1], u2f[:])
        for t in range(NE):
            bank = ps4[t % 4]
            pairs = [(win_bf[k][:, 2048 + t * 128:2048 + (t + 1) * 128],
                      xn_bf[k][:]) for k in range(ND)]
            S.mm_group(bank[:], pairs)
            S.aop("scalar", "activation", zs_sb[:], bank[:], AF.Silu,
                  bias=bu_acc[:, 16 + t:16 + t + 1], r=[bank, bu_acc], w=[zs_sb])
            S.dma("gpsimd", sp_z[t * 128:(t + 1) * 128, c0:c0 + TL1], zs_sb[:])
        pairs_dt = [(wxp_bf[:, 96 * t:96 * t + 64], u2bf[t][:]) for t in range(NE)]
        S.mm_group(ps_a[0:64, :], pairs_dt)
        S.aop("scalar", "activation", xdbl_dt[:, c0:c0 + TL1], ps_a[0:64, :],
              AF.Copy, r=[ps_a], w=[xdbl_dt])
        pairs_bc = [(wxp_bf[:, 96 * t + 64:96 * (t + 1)], u2bf[t][:])
                    for t in range(NE)]
        S.mm_group(ps_b[0:32, :], pairs_bc)
        S.aop("scalar", "activation", bc_sb[:, c0:c0 + TL1], ps_b[0:32, :],
              AF.Copy, r=[ps_b], w=[bc_sb])
        for t in range(NE):
            bank = ps_e if t % 2 == 0 else ps_f
            S.mm_group(bank[:], [(wdt_bf[:, t * 128:(t + 1) * 128],
                                  xdbl_dt[:, c0:c0 + TL1])])
            S.aop("scalar", "activation", sp_t1[:], bank[:], AF.Abs,
                  bias=bdt[:, t:t + 1], r=[bank, bdt], w=[sp_t1])
            S.aop("scalar", "activation", sp_t1[:], sp_t1[:], AF.Exp, scale=-1.0,
                  r=[sp_t1], w=[sp_t1])
            S.aop("scalar", "activation", sp_t1[:], sp_t1[:], AF.Ln, bias=1.0,
                  r=[sp_t1], w=[sp_t1])
            S.aop("scalar", "activation", sp_t2[:], bank[:], AF.Relu,
                  bias=bdt[:, t:t + 1], r=[bank, bdt], w=[sp_t2])
            S.aop("vector", "tensor_add", dtf[:], sp_t1[:], sp_t2[:],
                  r=[sp_t1, sp_t2], w=[dtf])
            S.dma("gpsimd", sp_dt[t * 128:(t + 1) * 128, c0:c0 + TL1], dtf[:])

    S.barrier()
    S.release(*win_bf, winst2[0], xdbl_dt, ps_a, ps_b, ps_c, ps_d,
              ps_e, ps_f,
              *x_fm, *xn_bf, *ln0tmp, x2st,
              u_sb[0], u_sb[1], u_tail, cacc, u2f, *u2bf, zs_sb, dtf,
              sp_t1, sp_t2,
              ln0.ps_su, ln0.ps_tp, ln0.mu_cols,
              ln0.st_row, ln0.mu_bc, ln0.rstd_bc, ln0.var_cols, ln0.eps_col)

    wout_bf = [S.sb(f"wout{t}", [128, D_MODEL], BF16) for t in range(NE)]
    for t in range(NE):
        S.dma("gpsimd", wout_bf[t][:], W_out_h[t * 128:(t + 1) * 128, :])
    dt_c = S.sb("dt_c", [128, NE * TL2], F32)
    dtu_c = S.sb("dtu_c", [128, NE * TL2], F32)
    y_c = S.sb("y_c", [128, NE * TL2], F32)
    x2_c = S.sb("x2_c", [128, ND * TL2], F32)
    PITCH = TL2 + 1
    dA_g = [S.sb(f"dAg{j}", [128, NE * PITCH], F32) for j in range(2)]
    dtuB_g = S.sb("dtuB_g", [128, NE * PITCH], F32)
    h_sc = S.sb("h_sc", [128, NE * PITCH], F32)
    h_g = dtuB_g
    carry = S.sb("carry", [128, NST * NE], F32)
    bcn = [S.sb(f"bcn{j}", [128, 2 * TL2], F32) for j in range(4)]
    x1_c = S.sb("x1_c", [128, ND * TL2], F32)
    x1n_c = x2_c
    ps_bcn = [S.psum(f"ps_bcn{j}", [128, 2 * TL2]) for j in range(2)]
    ps_wo = [S.psum(f"ps_wo{j}", [128, TL2]) for j in range(2)]
    ln1 = PlainLN(S, ND, TL2, "ln1", ones1, ones_col, idn_f,
                  ps_banks=[ps_bcn[0], ps_bcn[1]])
    S.aop("vector", "memset", carry[:], 0.0, w=[carry])
    S.aop("vector", "memset", dA_g[0][:], 0.0, w=[dA_g[0]])
    S.aop("vector", "memset", dA_g[1][:], 0.0, w=[dA_g[1]])
    S.aop("vector", "memset", dtuB_g[:, NE * PITCH - 1:NE * PITCH], 0.0,
          w=[dtuB_g])

    def vp(t3, width=TL2):
        return t3[:].rearrange("p (a b) -> p a b", a=NE)[:, :, 0:width]
    y2bf = dtu_c

    def v3(t3):
        return t3[:].rearrange("p (a b) -> p a b", a=NE)

    for c in range(NCH2):
        c0 = c * TL2
        S.dma("sync", dt_c[:].rearrange("p (a b) -> p a b", a=NE),
              sp_dt[:, c0:c0 + TL2].rearrange("(a p) b -> p a b", p=128))
        S.dma("scalar", vp(h_g),
              sp_u2[:, c0:c0 + TL2].rearrange("(a p) b -> p a b", p=128))
        S.dma("gpsimd", x2_c[:].rearrange("p (a b) -> p a b", a=ND),
              sp_x2[:, c0:c0 + TL2].rearrange("(a p) b -> p a b", p=128))
        S.aop("vector", "tensor_mul", out=v3(dtu_c), in0=v3(dt_c),
              in1=vp(h_g), op=None, r=[dt_c, h_g], w=[dtu_c]) if False else \
        S.aop("vector", "tensor_tensor", out=v3(dtu_c), in0=v3(dt_c),
              in1=vp(h_g), op=ALU.mult, r=[dt_c, h_g], w=[dtu_c])
        for n in range(NST):
            if n % 4 == 0:
                for nn in range(n, n + 4):
                    pb = ps_bcn[nn % 2]
                    S.aop("tensor", "matmul", pb[:, 0:TL2],
                          sel_sb[:, nn * 128:(nn + 1) * 128],
                          bc_sb[:, c0:c0 + TL2], start=True, stop=True,
                          r=[sel_sb, bc_sb], w=[pb])
                    S.aop("tensor", "matmul", pb[:, TL2:2 * TL2],
                          sel_sb[:, (16 + nn) * 128:(17 + nn) * 128],
                          bc_sb[:, c0:c0 + TL2], start=True, stop=True,
                          r=[sel_sb, bc_sb], w=[pb])
                    S.aop("scalar", "activation", bcn[nn % 4][:], pb[:], AF.Copy,
                          r=[pb], w=[bcn[nn % 4]])
            bb = bcn[n % 4]
            dag = dA_g[n % 2]
            for t in range(NE):
                S.aop("scalar", "activation",
                      dag[:, t * PITCH:t * PITCH + TL2],
                      dt_c[:, t * TL2:(t + 1) * TL2], AF.Exp,
                      scale=A_sb[:, NST * t + n:NST * t + n + 1],
                      r=[dt_c, A_sb], w=[dag])
            S.aop("vector", "tensor_tensor", out=vp(dtuB_g), in0=v3(dtu_c),
                  in1=bb[:, 0:TL2].unsqueeze(1).broadcast_to([128, NE, TL2]),
                  op=ALU.mult, r=[dtu_c, bb], w=[dtuB_g])
            S.aop("vector", "tensor_copy",
                  dtuB_g[:].rearrange("p (a b) -> p a b", a=NE)[:, 0:NE - 1,
                                                               TL2:TL2 + 1],
                  carry[:, n * NE + 1:(n + 1) * NE].unsqueeze(2),
                  r=[carry], w=[dtuB_g])
            S.aop("vector", "tensor_tensor_scan",
                  h_sc[:], dag[:], dtuB_g[:],
                  carry[:, n * NE:n * NE + 1], ALU.mult, ALU.add,
                  r=[dag, dtuB_g, carry], w=[h_sc])
            S.aop("vector", "tensor_copy",
                  carry[:, n * NE:(n + 1) * NE],
                  h_sc[:].rearrange("p (a b) -> p a b", a=NE)[:, :,
                      TL2 - 1:TL2].rearrange("p a b -> p (a b)"),
                  r=[h_sc], w=[carry])
            if n == 0:
                S.aop("vector", "tensor_tensor", out=v3(y_c), in0=vp(h_sc),
                      in1=bb[:, TL2:2 * TL2].unsqueeze(1).broadcast_to(
                          [128, NE, TL2]),
                      op=ALU.mult, r=[h_sc, bb], w=[y_c])
            else:
                S.aop("vector", "tensor_tensor", out=vp(dtuB_g), in0=vp(h_sc),
                      in1=bb[:, TL2:2 * TL2].unsqueeze(1).broadcast_to(
                          [128, NE, TL2]),
                      op=ALU.mult, r=[h_sc, bb], w=[dtuB_g])
                S.aop("vector", "tensor_add", out=v3(y_c), in0=v3(y_c),
                      in1=vp(dtuB_g), r=[y_c, dtuB_g], w=[y_c])
        S.dma("scalar", vp(h_g),
              sp_u2[:, c0:c0 + TL2].rearrange("(a p) b -> p a b", p=128))
        S.dma("sync", dt_c[:].rearrange("p (a b) -> p a b", a=NE),
              sp_z[:, c0:c0 + TL2].rearrange("(a p) b -> p a b", p=128))
        for t in range(NE):
            S.aop("scalar", "activation", h_sc[:, t * TL2:(t + 1) * TL2],
                  h_g[:, t * PITCH:t * PITCH + TL2], AF.Identity,
                  scale=Dp[:, t:t + 1], r=[h_g, Dp], w=[h_sc])
        S.aop("vector", "tensor_add", y_c[:], y_c[:], h_sc[:, 0:NE * TL2],
              r=[y_c, h_sc], w=[y_c])
        y2v = y2bf[:].bitcast(BF16)[:, 0:NE * TL2]
        S.aop("vector", "tensor_mul", y2v, y_c[:], dt_c[:], r=[y_c, dt_c],
              w=["y2bf", y2bf])
        for m in range(ND):
            bank = ps_wo[m % 2]
            pairs = [(wout_bf[t][:, m * 128:(m + 1) * 128],
                      y2v[:, t * TL2:(t + 1) * TL2]) for t in range(NE)]
            S.mm_group(bank[:], pairs, r_extra=["y2bf"])
            S.aop("vector", "scalar_tensor_tensor",
                  out=x1_c[:, m * TL2:(m + 1) * TL2], in0=bank[:],
                  scalar=rsc[:, 0:1], in1=x2_c[:, m * TL2:(m + 1) * TL2],
                  op0=ALU.mult, op1=ALU.add, r=[bank, rsc, x2_c], w=[x1_c])
        ln1.emit([x1_c[:, m * TL2:(m + 1) * TL2] for m in range(ND)],
                 [x1n_c[:, m * TL2:(m + 1) * TL2] for m in range(ND)],
                 [x1_c[:, m * TL2:(m + 1) * TL2] for m in range(ND)])
        S.dma("gpsimd", x1n_o[:, c0:c0 + TL2].rearrange("(a p) b -> p a b", p=128),
              x1n_c[:].rearrange("p (a b) -> p a b", a=ND))
        S.aop("vector", "tensor_reduce", mxtmp[:], x1n_c[:], AX, ALU.max,
              apply_absolute_value=True, r=[x1n_c], w=[mxtmp])
        S.aop("vector", "tensor_tensor", mxrun[:], mxrun[:], mxtmp[:], ALU.max,
              r=[mxrun, mxtmp], w=[mxrun])

    S.aop("tensor", "transpose", ln1.ps_tp[0:1, 0:128], mxrun[:], idn_f[:],
          r=[mxrun, idn_f], w=[ln1.ps_tp])
    S.aop("scalar", "activation", mrow[:], ln1.ps_tp[0:1, 0:128], AF.Copy,
          r=[ln1.ps_tp], w=[mrow])
    S.aop("vector", "tensor_reduce", mfin[:], mrow[:], AX, ALU.max, r=[mrow],
          w=[mfin])
    S.dma("sync", mx_o[:], mfin[:])
    S.flush_engine("sync")
    return S.build()



_CACHE = {}


def _progs():
    if "p" not in _CACHE:
        _CACHE["p"] = (build_l1(), build_l2(), build_l3(), build_l4(), build_l5())
    return _CACHE["p"]


def kernel(L0_attn_x, L0_attn_ln_g, L0_attn_ln_b, L0_attn_W_in, L0_attn_conv_w,
           L0_attn_conv_b, L0_attn_W_xp, L0_attn_W_dt, L0_attn_b_dt,
           L0_attn_A_log, L0_attn_D_param, L0_attn_W_out, L0_attn_res_scale,
           L0_ffn_w_gate, L0_ffn_w_up, L0_ffn_w_down):
    n = 8
    nc1, nc2, nc3, nc4, nc5 = _progs()
    cores = list(range(n))
    c = np.ascontiguousarray

    im1 = [{
        "x": c(L0_attn_x[i].T),
        "ln_g": c(L0_attn_ln_g.reshape(1, -1)), "ln_b": c(L0_attn_ln_b.reshape(1, -1)),
        "W_in": c(L0_attn_W_in), "conv_w": c(L0_attn_conv_w),
        "conv_b": c(L0_attn_conv_b.reshape(1, -1)), "W_xp": c(L0_attn_W_xp),
        "W_dt": c(L0_attn_W_dt), "b_dt": c(L0_attn_b_dt.reshape(1, -1)),
        "A_log": c(L0_attn_A_log), "D_par": c(L0_attn_D_param.reshape(1, -1)),
        "res_sc": c(L0_attn_res_scale.reshape(1, 1)), "W_out": c(L0_attn_W_out),
    } for i in range(n)]
    r1 = run_bass_kernel_spmd(nc1, im1, core_ids=cores).results
    mx8 = np.concatenate([r1[i]["mx_o"] for i in range(n)], axis=1)

    im2 = [{"x1n": r1[i]["x1n_o"], "mx8": mx8,
            "w_up": c(L0_ffn_w_up), "w_gate": c(L0_ffn_w_gate)} for i in range(n)]
    r2 = run_bass_kernel_spmd(nc2, im2, core_ids=cores).results
    mup8 = np.concatenate([r2[i]["m_up"] for i in range(n)], axis=1)
    mg8 = np.concatenate([r2[i]["m_gate"] for i in range(n)], axis=1)

    im3 = [{"up_i": r2[i]["up_o"], "gate_i": r2[i]["gate_o"],
            "mup8": mup8, "mg8": mg8} for i in range(n)]
    r3 = run_bass_kernel_spmd(nc3, im3, core_ids=cores).results
    mg28 = np.concatenate([r3[i]["m_g2"] for i in range(n)], axis=1)

    im4 = [{"upq_i": r3[i]["upq_o"], "g2_i": r3[i]["gate2_o"], "mg28": mg28}
           for i in range(n)]
    r4 = run_bass_kernel_spmd(nc4, im4, core_ids=cores).results
    mo8 = np.concatenate([r4[i]["m_o"] for i in range(n)], axis=1)

    im5 = [{"o_i": r4[i]["o_o"], "x1n": r1[i]["x1n_o"], "mo8": mo8,
            "w_down": c(L0_ffn_w_down)} for i in range(n)]
    r5 = run_bass_kernel_spmd(nc5, im5, core_ids=cores).results

    return np.stack([r5[i]["out_o"].T for i in range(n)], axis=0)


# version 14 (final)
# speedup vs baseline: 1.0222x; 1.0163x over previous
import numpy as np
from contextlib import ExitStack
import concourse.bass as bass
import concourse.mybir as mybir
from concourse.bass_utils import run_bass_kernel_spmd

F32 = mybir.dt.float32
BF16 = mybir.dt.bfloat16
AF = mybir.ActivationFunctionType
ALU = mybir.AluOpType
AX = mybir.AxisListType.X
C_RND = 12582912.0
D_MODEL = 1024
D_INNER = 2048
D_FFN = 3072
L_SEQ = 2048
ENGINES = ("sync", "scalar", "vector", "tensor", "gpsimd")


def _key(x):
    if isinstance(x, str):
        return x
    t = getattr(x, "tensor", None)
    if t is not None:
        return t.name
    return x.name


class Sched:

    def __init__(self, name):
        self.nc = bass.Bass("TRN2", target_bir_lowering=False, debug=False)
        self.es = ExitStack()
        self.q = {e: [] for e in ENGINES}
        self.sems = {}
        self.last_w = {}
        self.readers = {}
        self.waited = {e: {} for e in ENGINES}
        for e in ENGINES:
            self._mk_sem("q_" + e)
        self.dma_issued = {e: {} for e in ENGINES}
        self.nops = 0

    def _mk_sem(self, name):
        h = self.es.enter_context(self.nc.semaphore(name))
        self.sems[name] = [h, 0]

    def sb(self, name, shape, dt):
        cm = self.nc.sbuf_tensor(name, shape, dt)
        h = cm.__enter__()
        self._cms = getattr(self, "_cms", {})
        self._alloc_idx = getattr(self, "_alloc_idx", 0) + 1
        self._cms[name] = (self._alloc_idx, cm)
        return h

    def psum(self, name, shape, dt=F32):
        cm = self.nc.psum_tensor(name, shape, dt)
        h = cm.__enter__()
        self._cms = getattr(self, "_cms", {})
        self._alloc_idx = getattr(self, "_alloc_idx", 0) + 1
        self._cms[name] = (self._alloc_idx, cm)
        return h

    def release(self, *handles):
        items = sorted((self._cms[h.name][0], h.name) for h in handles)
        for _, name in reversed(items):
            self._cms.pop(name)[1].__exit__(None, None, None)

    def val(self, sem):
        return self.sems[sem][1]

    def _emit_wait(self, eng, sem_name, v):
        if v is None or v <= 0:
            return
        wd = self.waited[eng]
        if wd.get(sem_name, 0) >= v:
            return
        wd[sem_name] = v
        h = self.sems[sem_name][0]
        self.q[eng].append(lambda e, h=h, v=v: e.wait_ge(h, v))

    def _dep_waits(self, eng, r, w):
        need = {}
        for x in r:
            k = _key(x)
            lw = self.last_w.get(k)
            if lw:
                need[lw[0]] = max(need.get(lw[0], 0), lw[1])
        for x in w:
            k = _key(x)
            lw = self.last_w.get(k)
            if lw:
                need[lw[0]] = max(need.get(lw[0], 0), lw[1])
            for sn, v in self.readers.get(k, {}).items():
                need[sn] = max(need.get(sn, 0), v)
        for sn, v in need.items():
            self._emit_wait(eng, sn, v)

    def _dep_update(self, sem_name, v, r, w):
        for x in w:
            k = _key(x)
            self.last_w[k] = (sem_name, v)
            self.readers[k] = {}
        for x in r:
            k = _key(x)
            d = self.readers.setdefault(k, {})
            d[sem_name] = max(d.get(sem_name, 0), v)

    def aop(self, eng, meth, *args, r=(), w=(), sem_name=None, k=1, **kw):
        self._dep_waits(eng, r, w)
        if sem_name is None:
            sem_name = "q_" + eng
        s = self.sems[sem_name]
        s[1] += k
        v = s[1]
        h = s[0]

        def fn(e, meth=meth, args=args, kw=kw, h=h, k=k):
            getattr(e, meth)(*args, **kw).then_inc(h, k)
        self.q[eng].append(fn)
        self._dep_update(sem_name, v, r, w)
        self.nops += 1
        return sem_name, v

    def dma(self, eng, out, in_, **kw):
        sb_side = out if out.space == bass.MemorySpace.SBUF else in_
        sem_name = "dma_" + eng + "_" + _key(sb_side)
        if sem_name not in self.sems:
            self._mk_sem(sem_name)
        v = self.aop(eng, "dma_start", r=[in_], w=[out], sem_name=sem_name,
                     k=16, out=out, in_=in_, **kw)
        d = self.dma_issued[eng]
        d[sem_name] = self.sems[sem_name][1]
        return v

    def mm_group(self, psum_ap, pairs, r_extra=(), eng="tensor"):
        nk = len(pairs)
        r = ([p[0] for p in pairs] + [p[1] for p in pairs] + list(r_extra))
        w = [psum_ap]
        self._dep_waits(eng, r, w)
        sem_name = "q_" + eng
        s = self.sems[sem_name]
        s[1] += 1
        v = s[1]
        h = s[0]

        def fn(e, pairs=pairs, psum_ap=psum_ap, h=h, nk=nk):
            ins = None
            for j, (lhsT, rhs) in enumerate(pairs):
                ins = e.matmul(psum_ap, lhsT, rhs, start=(j == 0),
                               stop=(j == nk - 1))
            ins.then_inc(h, 1)
        self.q[eng].append(fn)
        self._dep_update(sem_name, v, r, w)
        self.nops += nk
        return sem_name, v

    def raw(self, eng, fn):
        self.q[eng].append(fn)

    def flush_engine(self, eng):
        self._emit_wait(eng, "q_" + eng, self.sems["q_" + eng][1])
        for sn, v in self.dma_issued[eng].items():
            self._emit_wait(eng, sn, v)

    def barrier(self, engines=ENGINES):
        if "bar" not in self.sems:
            self._mk_sem("bar")
        s = self.sems["bar"]
        tgt = s[1] + len(engines)
        for e in engines:
            self.flush_engine(e)
            h = s[0]
            self.q[e].append(lambda e_, h=h: e_.sem_inc(h, 1))
        s[1] = tgt
        for e in engines:
            self._emit_wait(e, "bar", tgt)
        self.last_w = {k: ("bar", tgt) for k in self.last_w}
        self.readers = {}

    def build(self):
        with self.nc.Block() as block:
            for eng in ENGINES:
                ops = self.q[eng]
                if not ops:
                    continue
                getattr(block, eng)(lambda e, ops=ops: [fn(e) for fn in ops])
        for _, (_, cm) in sorted(getattr(self, "_cms", {}).items(),
                                 key=lambda kv: -kv[1][0]):
            cm.__exit__(None, None, None)
        self._cms = {}
        self.es.close()
        return self.nc


def emit_identity(S, idn, fill=1.0):
    S.aop("gpsimd", "memset", idn[:], 0.0, w=[idn])
    S.aop("gpsimd", "affine_select", out=idn[:], in_=idn[:],
          compare_op=ALU.not_equal, fill=fill, base=0,
          pattern=[[-1, idn.shape[0]]], channel_multiplier=1, r=[idn], w=[idn])


def _rep(h, nseg, reps):
    return h[:].unsqueeze(2).broadcast_to([128, nseg, reps])


def emit_scale_bcast(S, mx_ap, sca, sca_bc, ones1, ps_bc, rq_cell, mx_keys=()):
    S.aop("vector", "tensor_reduce", sca[0:1, 0:1], mx_ap, AX, ALU.max,
          r=[mx_ap] + list(mx_keys), w=[sca])
    S.aop("vector", "tensor_scalar", sca[0:1, 0:1], sca[0:1, 0:1],
          rq_cell[0:1, 0:1], 1e-12, ALU.mult, ALU.max,
          r=[sca, rq_cell], w=[sca])
    S.aop("vector", "reciprocal", sca[0:1, 1:2], sca[0:1, 0:1], r=[sca], w=[sca])
    S.aop("tensor", "matmul", ps_bc[:, 0:2], ones1[0:1, :], sca[0:1, 0:2],
          start=True, stop=True, r=[ones1, sca], w=[ps_bc])
    S.aop("scalar", "activation", sca_bc[:, 0:2], ps_bc[:, 0:2], AF.Copy,
          r=[ps_bc], w=[sca_bc])


def emit_consts(S):
    ones1 = S.sb("ones1", [1, 128], F32)
    rq_cell = S.sb("rq_cell", [1, 1], F32)
    r7 = S.sb("r7", [128, 1], F32)
    S.aop("vector", "memset", ones1[:], 1.0, w=[ones1])
    S.aop("vector", "memset", rq_cell[:], 32767.0, w=[rq_cell])
    S.aop("vector", "reciprocal", rq_cell[:], rq_cell[:], r=[rq_cell], w=[rq_cell])
    S.aop("vector", "memset", r7[:], 7.0, w=[r7])
    S.aop("vector", "reciprocal", r7[:], r7[:], r=[r7], w=[r7])
    return ones1, rq_cell, r7


def emit_qdq_rounds(S, out_ap, in_ap, sca_bc, tmp_ap, r_keys, w_keys):
    S.aop("vector", "tensor_scalar", tmp_ap, in_ap, sca_bc[:, 1:2], C_RND,
          ALU.mult, ALU.add, r=list(r_keys) + [sca_bc], w=[tmp_ap])
    S.aop("vector", "tensor_scalar", out_ap, tmp_ap, C_RND, sca_bc[:, 0:1],
          ALU.subtract, ALU.mult, r=[tmp_ap, sca_bc], w=list(w_keys))


def build_l2():
    S = Sched("l2")
    nc = S.nc
    x1n = nc.dram_tensor("x1n", [D_MODEL, L_SEQ], F32, kind="ExternalInput").ap()
    mx8 = nc.dram_tensor("mx8", [1, 8], F32, kind="ExternalInput").ap()
    w_hbm = [nc.dram_tensor("w_up", [D_FFN, D_MODEL], F32, kind="ExternalInput").ap(),
             nc.dram_tensor("w_gate", [D_FFN, D_MODEL], F32, kind="ExternalInput").ap()]
    out_hbm = [nc.dram_tensor("up_o", [D_FFN, L_SEQ], F32, kind="ExternalOutput").ap(),
               nc.dram_tensor("gate_o", [D_FFN, L_SEQ], F32, kind="ExternalOutput").ap()]
    m_o = [nc.dram_tensor("m_up", [1, 1], F32, kind="ExternalOutput").ap(),
           nc.dram_tensor("m_gate", [1, 1], F32, kind="ExternalOutput").ap()]

    ND = D_MODEL // 128
    NI = D_FFN // 128

    idn_bf = S.sb("idn_bf", [128, 128], BF16)
    idn_f = S.sb("idn_f", [128, 128], F32)
    mx_sb = S.sb("mx_sb", [1, 8], F32)
    sca = S.sb("sca", [1, 2], F32)
    sca_bc = S.sb("sca_bc", [128, 2], F32)
    xq = [S.sb(f"xq{k}", [128, L_SEQ], BF16) for k in range(ND)]
    xstage = [S.sb(f"xstage{j}", [128, L_SEQ], F32) for j in range(2)]
    xtmp = S.sb("xtmp", [128, L_SEQ], F32)
    wstage = [S.sb(f"wstage{j}", [128, D_MODEL], F32) for j in range(2)]
    wq_bufs = [S.sb(f"wqb{j}", [128, D_MODEL], BF16) for j in range(2)]
    sblk = S.sb("sblk", [128, 32], F32)
    rblk = S.sb("rblk", [128, 32], F32)
    wr_t = S.sb("wr_t", [128, D_MODEL], F32)
    wqT = [[S.sb(f"wqT{w}_{k}", [128, D_FFN], BF16) for k in range(ND)]
           for w in range(2)]
    ostage = [S.sb(f"ostage{j}", [128, L_SEQ], F32) for j in range(2)]
    mrun = S.sb("mrun", [128, 2], F32)
    mtmp = S.sb("mtmp", [128, 1], F32)
    mrow = S.sb("mrow", [2, 128], F32)
    mfin = S.sb("mfin", [2, 1], F32)

    ps_bc = S.psum("ps_bc", [128, 2])
    ps_tr = [S.psum(f"ps_tr{j}", [128, 512], BF16) for j in range(2)]
    ps_mm = [S.psum(f"ps_mm{j}", [128, 512]) for j in range(2)]
    ps_fin = S.psum("ps_fin", [2, 128])

    emit_identity(S, idn_bf)
    emit_identity(S, idn_f)
    ones1, rq_cell, r7 = emit_consts(S)
    S.aop("vector", "memset", mrun[:], 0.0, w=[mrun])
    S.dma("sync", mx_sb[:], mx8[:])

    emit_scale_bcast(S, mx_sb[0:1, 0:8], sca, sca_bc, ones1, ps_bc, rq_cell)

    for k in range(ND):
        stage = xstage[k % 2]
        S.dma("sync", stage[:], x1n[k * 128:(k + 1) * 128, :])
        emit_qdq_rounds(S, xq[k][:], stage[:], sca_bc, xtmp[:],
                        r_keys=[stage], w_keys=[xq[k]])

    for w in range(2):
        for t in range(NI):
            st = wstage[t % 2]
            S.dma("sync", st[:], w_hbm[w][t * 128:(t + 1) * 128, :])
            S.aop("vector", "tensor_reduce", sblk[:],
                  st[:].rearrange("p (a b) -> p a b", a=32), AX, ALU.max,
                  apply_absolute_value=True, r=[st], w=[sblk])
            S.aop("vector", "tensor_scalar", sblk[:], sblk[:], r7[:, 0:1], 1e-12,
                  ALU.mult, ALU.max, r=[sblk, r7], w=[sblk])
            S.aop("vector", "reciprocal", rblk[:], sblk[:], r=[sblk], w=[rblk])
            S.aop("vector", "tensor_tensor",
                  out=wr_t[:].rearrange("p (a b) -> p a b", a=32),
                  in0=st[:].rearrange("p (a b) -> p a b", a=32),
                  in1=_rep(rblk, 32, 32), op=ALU.mult,
                  r=[st, rblk], w=[wr_t])
            S.aop("vector", "tensor_scalar", wr_t[:], wr_t[:], C_RND, C_RND,
                  ALU.add, ALU.subtract, r=[wr_t], w=[wr_t])
            wq = wq_bufs[t % 2]
            S.aop("vector", "tensor_tensor",
                  out=wq[:].rearrange("p (a b) -> p a b", a=32),
                  in0=wr_t[:].rearrange("p (a b) -> p a b", a=32),
                  in1=_rep(sblk, 32, 32), op=ALU.mult,
                  r=[wr_t, sblk], w=[wq])
            for half in range(2):
                bank = ps_tr[half]
                for j in range(4):
                    kblk = half * 4 + j
                    S.aop("tensor", "transpose", bank[:, j * 128:(j + 1) * 128],
                          wq[:, kblk * 128:(kblk + 1) * 128], idn_bf[:],
                          r=[wq, idn_bf], w=[bank])
                for j in range(4):
                    kblk = half * 4 + j
                    S.aop("scalar", "activation",
                          wqT[w][kblk][:, t * 128:(t + 1) * 128],
                          bank[:, j * 128:(j + 1) * 128], AF.Copy,
                          r=[bank], w=[f"wqT{w}_{kblk}:{t}"])
    for w in range(2):
        for t in range(NI):
            stg = ostage[t % 2]
            for lc in range(4):
                bank = ps_mm[lc % 2]
                pairs = [(wqT[w][k][:, t * 128:(t + 1) * 128],
                          xq[k][:, lc * 512:(lc + 1) * 512]) for k in range(ND)]
                S.mm_group(bank[:], pairs,
                           r_extra=[f"wqT{w}_{k}:{t}" for k in range(ND)])
                S.aop("scalar", "activation", stg[:, lc * 512:(lc + 1) * 512],
                      bank[:], AF.Copy, r=[bank], w=[stg])
            S.dma("sync", out_hbm[w][t * 128:(t + 1) * 128, :], stg[:])
            S.aop("vector", "tensor_reduce", mtmp[:], stg[:], AX, ALU.max,
                  apply_absolute_value=True, r=[stg], w=[mtmp])
            S.aop("vector", "tensor_tensor", mrun[:, w:w + 1], mrun[:, w:w + 1],
                  mtmp[:], ALU.max, r=[mrun, mtmp], w=[mrun])

    S.aop("tensor", "transpose", ps_fin[:], mrun[:, 0:2], idn_f[:],
          r=[mrun, idn_f], w=[ps_fin])
    S.aop("scalar", "activation", mrow[:], ps_fin[:], AF.Copy, r=[ps_fin], w=[mrow])
    S.aop("vector", "tensor_reduce", mfin[:], mrow[:], AX, ALU.max,
          r=[mrow], w=[mfin])
    S.dma("sync", m_o[0][:], mfin[0:1, :])
    S.dma("sync", m_o[1][:], mfin[1:2, :])
    S.flush_engine("sync")
    return S.build()


def host_l2_inputs(x1n_fm, mx8, w_up, w_gate):
    return {"x1n": np.ascontiguousarray(x1n_fm), "mx8": mx8,
            "w_up": w_up, "w_gate": w_gate}


def build_l3():
    S = Sched("l3")
    nc = S.nc
    up_i = nc.dram_tensor("up_i", [D_FFN, L_SEQ], F32, kind="ExternalInput").ap()
    gate_i = nc.dram_tensor("gate_i", [D_FFN, L_SEQ], F32, kind="ExternalInput").ap()
    mup8 = nc.dram_tensor("mup8", [1, 8], F32, kind="ExternalInput").ap()
    mg8 = nc.dram_tensor("mg8", [1, 8], F32, kind="ExternalInput").ap()
    upq_o = nc.dram_tensor("upq_o", [D_FFN, L_SEQ], F32, kind="ExternalOutput").ap()
    gate2_o = nc.dram_tensor("gate2_o", [D_FFN, L_SEQ], F32, kind="ExternalOutput").ap()
    m_g2 = nc.dram_tensor("m_g2", [1, 1], F32, kind="ExternalOutput").ap()
    NI = D_FFN // 128

    idn_f = S.sb("idn_f", [128, 128], F32)
    mx_sb = S.sb("mx_sb", [1, 16], F32)
    sca_u = S.sb("sca_u", [1, 2], F32)
    sca_g = S.sb("sca_g", [1, 2], F32)
    scb_u = S.sb("scb_u", [128, 2], F32)
    scb_g = S.sb("scb_g", [128, 2], F32)
    ustage = [S.sb(f"ustage{j}", [128, L_SEQ], F32) for j in range(3)]
    gstage = [S.sb(f"gstage{j}", [128, L_SEQ], F32) for j in range(3)]
    uq = [S.sb(f"uq{j}", [128, L_SEQ], F32) for j in range(3)]
    gq = [S.sb(f"gq{j}", [128, L_SEQ], F32) for j in range(3)]
    sig_t = S.sb("sig_t", [128, L_SEQ], F32)
    g2 = [S.sb(f"g2_{j}", [128, L_SEQ], F32) for j in range(3)]
    xtmp = S.sb("xtmp", [128, L_SEQ], F32)
    mrun = S.sb("mrun", [128, 1], F32)
    mtmp = S.sb("mtmp", [128, 1], F32)
    mrow = S.sb("mrow", [1, 128], F32)
    mfin = S.sb("mfin", [1, 1], F32)
    ps_bc = S.psum("ps_bc", [128, 4])
    ps_fin = S.psum("ps_fin", [1, 128])

    emit_identity(S, idn_f)
    ones1, rq_cell, r7 = emit_consts(S)
    S.aop("vector", "memset", mrun[:], 0.0, w=[mrun])
    S.dma("sync", mx_sb[0:1, 0:8], mup8[:])
    S.dma("sync", mx_sb[0:1, 8:16], mg8[:])
    emit_scale_bcast(S, mx_sb[0:1, 0:8], sca_u, scb_u, ones1, ps_bc[:, 0:2], rq_cell)
    emit_scale_bcast(S, mx_sb[0:1, 8:16], sca_g, scb_g, ones1, ps_bc[:, 2:4], rq_cell)

    for t in range(NI):
        ust = ustage[t % 3]
        gst = gstage[t % 3]
        S.dma("sync", ust[:], up_i[t * 128:(t + 1) * 128, :])
        S.dma("gpsimd", gst[:], gate_i[t * 128:(t + 1) * 128, :])
        emit_qdq_rounds(S, uq[t % 3][:], ust[:], scb_u, xtmp[:],
                        r_keys=[ust], w_keys=[uq[t % 3]])
        S.dma("scalar", upq_o[t * 128:(t + 1) * 128, :], uq[t % 3][:])
        emit_qdq_rounds(S, gq[t % 3][:], gst[:], scb_g, xtmp[:],
                        r_keys=[gst], w_keys=[gq[t % 3]])
        S.aop("scalar", "activation", sig_t[:], gq[t % 3][:], AF.Sigmoid,
              r=[gq[t % 3]], w=[sig_t])
        S.aop("vector", "tensor_scalar", sig_t[:], sig_t[:], 65536.0, C_RND,
              ALU.mult, ALU.add, r=[sig_t], w=[sig_t])
        S.aop("vector", "tensor_scalar", sig_t[:], sig_t[:], C_RND, 65535.0,
              ALU.subtract, ALU.min, r=[sig_t], w=[sig_t])
        S.aop("vector", "scalar_tensor_tensor", out=g2[t % 3][:], in0=sig_t[:],
              scalar=1.0 / 65536.0, in1=gq[t % 3][:], op0=ALU.mult, op1=ALU.mult,
              r=[sig_t, gq[t % 3]], w=[g2[t % 3]])
        S.dma("scalar", gate2_o[t * 128:(t + 1) * 128, :], g2[t % 3][:])
        S.aop("vector", "tensor_reduce", mtmp[:], g2[t % 3][:], AX, ALU.max,
              apply_absolute_value=True, r=[g2[t % 3]], w=[mtmp])
        S.aop("vector", "tensor_tensor", mrun[:], mrun[:], mtmp[:], ALU.max,
              r=[mrun, mtmp], w=[mrun])

    S.aop("tensor", "transpose", ps_fin[:], mrun[:], idn_f[:],
          r=[mrun, idn_f], w=[ps_fin])
    S.aop("scalar", "activation", mrow[:], ps_fin[:], AF.Copy, r=[ps_fin], w=[mrow])
    S.aop("vector", "tensor_reduce", mfin[:], mrow[:], AX, ALU.max, r=[mrow], w=[mfin])
    S.dma("sync", m_g2[:], mfin[:])
    S.flush_engine("sync")
    return S.build()


def build_l4():
    S = Sched("l4")
    nc = S.nc
    upq_i = nc.dram_tensor("upq_i", [D_FFN, L_SEQ], F32, kind="ExternalInput").ap()
    g2_i = nc.dram_tensor("g2_i", [D_FFN, L_SEQ], F32, kind="ExternalInput").ap()
    mg28 = nc.dram_tensor("mg28", [1, 8], F32, kind="ExternalInput").ap()
    o_o = nc.dram_tensor("o_o", [D_FFN, L_SEQ], F32, kind="ExternalOutput").ap()
    m_o = nc.dram_tensor("m_o", [1, 1], F32, kind="ExternalOutput").ap()
    NI = D_FFN // 128

    idn_f = S.sb("idn_f", [128, 128], F32)
    mx_sb = S.sb("mx_sb", [1, 8], F32)
    sca = S.sb("sca", [1, 2], F32)
    scb = S.sb("scb", [128, 2], F32)
    ustage = [S.sb(f"ustage{j}", [128, L_SEQ], F32) for j in range(3)]
    gstage = [S.sb(f"gstage{j}", [128, L_SEQ], F32) for j in range(3)]
    ot = [S.sb(f"ot{j}", [128, L_SEQ], F32) for j in range(3)]
    xtmp = S.sb("xtmp", [128, L_SEQ], F32)
    mrun = S.sb("mrun", [128, 1], F32)
    mtmp = S.sb("mtmp", [128, 1], F32)
    mrow = S.sb("mrow", [1, 128], F32)
    mfin = S.sb("mfin", [1, 1], F32)
    ps_bc = S.psum("ps_bc", [128, 2])
    ps_fin = S.psum("ps_fin", [1, 128])

    emit_identity(S, idn_f)
    ones1, rq_cell, r7 = emit_consts(S)
    S.aop("vector", "memset", mrun[:], 0.0, w=[mrun])
    S.dma("sync", mx_sb[:], mg28[:])
    emit_scale_bcast(S, mx_sb[0:1, 0:8], sca, scb, ones1, ps_bc[:, 0:2], rq_cell)

    for t in range(NI):
        ust = ustage[t % 3]
        gst = gstage[t % 3]
        S.dma("sync", ust[:], upq_i[t * 128:(t + 1) * 128, :])
        S.dma("gpsimd", gst[:], g2_i[t * 128:(t + 1) * 128, :])
        emit_qdq_rounds(S, gst[:], gst[:], scb, xtmp[:],
                        r_keys=[gst], w_keys=[gst])
        S.aop("vector", "tensor_tensor", ot[t % 3][:], gst[:], ust[:], ALU.mult,
              r=[gst, ust], w=[ot[t % 3]])
        S.dma("scalar", o_o[t * 128:(t + 1) * 128, :], ot[t % 3][:])
        S.aop("vector", "tensor_reduce", mtmp[:], ot[t % 3][:], AX, ALU.max,
              apply_absolute_value=True, r=[ot[t % 3]], w=[mtmp])
        S.aop("vector", "tensor_tensor", mrun[:], mrun[:], mtmp[:], ALU.max,
              r=[mrun, mtmp], w=[mrun])

    S.aop("tensor", "transpose", ps_fin[:], mrun[:], idn_f[:],
          r=[mrun, idn_f], w=[ps_fin])
    S.aop("scalar", "activation", mrow[:], ps_fin[:], AF.Copy, r=[ps_fin], w=[mrow])
    S.aop("vector", "tensor_reduce", mfin[:], mrow[:], AX, ALU.max, r=[mrow], w=[mfin])
    S.dma("sync", m_o[:], mfin[:])
    S.flush_engine("sync")
    return S.build()


class PlainLN:

    def __init__(self, S, nd, L, pref, ones1, ones_col, idn_f, eps=1e-5,
                 ps_banks=None):
        self.S, self.nd, self.L = S, nd, L
        self.ones1, self.ones_col, self.idn_f = ones1, ones_col, idn_f
        self.NLS = L // 128
        assert self.NLS <= 16
        self.sc = 1.0 / (nd * 128)
        self.ps_su = S.psum(pref + "ps_su", [128, self.NLS])
        self.ps_tp = S.psum(pref + "ps_tp", [1, 512])
        self.ps_bc = (ps_banks if ps_banks is not None else
                      [S.psum(pref + f"ps_bc{j}", [128, 512]) for j in range(2)])
        self.mu_cols = S.sb(pref + "mu_cols", [128, self.NLS], F32)
        self.st_row = S.sb(pref + "st_row", [1, L], F32)
        self.mu_bc = S.sb(pref + "mu_bc", [128, L], F32)
        self.rstd_bc = S.sb(pref + "rstd_bc", [128, L], F32)
        self.var_cols = S.sb(pref + "var_cols", [128, self.NLS], F32)
        self.eps_col = S.sb(pref + "eps_col", [128, 1], F32)
        S.aop("vector", "memset", self.eps_col[:], eps, w=[self.eps_col])

    def _stat_pass(self, tiles, out_cols):
        S, NLS, nd = self.S, self.NLS, self.nd
        for ls in range(NLS):
            pairs = [(tiles[k][:, ls * 128:(ls + 1) * 128], self.ones_col[:, 0:1])
                     for k in range(nd)]
            S.mm_group(self.ps_su[:, ls:ls + 1], pairs)
        S.aop("scalar", "activation", out_cols[:], self.ps_su[:], AF.Copy,
              scale=self.sc, r=[self.ps_su], w=[out_cols])

    def _bcast_cols(self, cols, dst_bc):
        S, NLS = self.S, self.NLS
        for g in range(0, NLS, 4):
            gn = min(4, NLS - g)
            for j in range(gn):
                S.aop("tensor", "transpose", self.ps_tp[0:1, j * 128:(j + 1) * 128],
                      cols[:, g + j:g + j + 1], self.idn_f[:],
                      r=[cols, self.idn_f], w=[self.ps_tp])
            S.aop("scalar", "activation", self.st_row[0:1, g * 128:(g + gn) * 128],
                  self.ps_tp[0:1, 0:gn * 128], AF.Copy, r=[self.ps_tp],
                  w=[self.st_row])
        for g in range(0, NLS, 4):
            gn = min(4, NLS - g)
            bank = self.ps_bc[(g // 4) % 2]
            S.aop("tensor", "matmul", bank[:, 0:gn * 128], self.ones1[0:1, :],
                  self.st_row[0:1, g * 128:(g + gn) * 128], start=True, stop=True,
                  r=[self.ones1, self.st_row], w=[bank])
            S.aop("scalar", "activation", dst_bc[:, g * 128:(g + gn) * 128],
                  bank[:, 0:gn * 128], AF.Copy, r=[bank], w=[dst_bc])

    def emit_from_stats(self, tiles_in, tiles_out, mu_cols_ap, var_cols_ap,
                        extra_r=()):
        S, nd = self.S, self.nd
        S.aop("scalar", "activation", var_cols_ap, var_cols_ap, AF.Ln,
              bias=self.eps_col[:, 0:1],
              r=[var_cols_ap, self.eps_col] + list(extra_r), w=[var_cols_ap])
        S.aop("scalar", "activation", var_cols_ap, var_cols_ap, AF.Exp,
              scale=-0.5, r=[var_cols_ap], w=[var_cols_ap])
        self._bcast_cols(mu_cols_ap, self.mu_bc)
        self._bcast_cols(var_cols_ap, self.rstd_bc)
        for k in range(nd):
            S.aop("vector", "tensor_sub", tiles_in[k], tiles_in[k], self.mu_bc[:],
                  r=[tiles_in[k], self.mu_bc], w=[tiles_in[k]])
            S.aop("vector", "tensor_mul", tiles_out[k], tiles_in[k],
                  self.rstd_bc[:], r=[tiles_in[k], self.rstd_bc],
                  w=[tiles_out[k]])

    def emit(self, tiles_in, tiles_out, tiles_tmp, tiles_final=None):
        S, nd = self.S, self.nd
        fin = tiles_final if tiles_final is not None else tiles_out
        self._stat_pass(tiles_in, self.mu_cols)
        self._bcast_cols(self.mu_cols, self.mu_bc)
        for k in range(nd):
            S.aop("vector", "tensor_sub", tiles_out[k], tiles_in[k], self.mu_bc[:],
                  r=[tiles_in[k], self.mu_bc], w=[tiles_out[k]])
            S.aop("scalar", "activation", tiles_tmp[k], tiles_out[k], AF.Square,
                  r=[tiles_out[k]], w=[tiles_tmp[k]])
        self._stat_pass(tiles_tmp, self.var_cols)
        S.aop("scalar", "activation", self.var_cols[:], self.var_cols[:], AF.Ln,
              bias=self.eps_col[:, 0:1], r=[self.var_cols, self.eps_col],
              w=[self.var_cols])
        S.aop("scalar", "activation", self.var_cols[:], self.var_cols[:], AF.Exp,
              scale=-0.5, r=[self.var_cols], w=[self.var_cols])
        self._bcast_cols(self.var_cols, self.rstd_bc)
        for k in range(nd):
            S.aop("vector", "tensor_mul", fin[k], tiles_out[k],
                  self.rstd_bc[:], r=[tiles_out[k], self.rstd_bc],
                  w=[fin[k]])


def build_l5():
    S = Sched("l5")
    nc = S.nc
    o_i = nc.dram_tensor("o_i", [D_FFN, L_SEQ], F32, kind="ExternalInput").ap()
    x1n = nc.dram_tensor("x1n", [D_MODEL, L_SEQ], F32, kind="ExternalInput").ap()
    mo8 = nc.dram_tensor("mo8", [1, 8], F32, kind="ExternalInput").ap()
    w_down = nc.dram_tensor("w_down", [D_MODEL, D_FFN], F32, kind="ExternalInput").ap()
    out_o = nc.dram_tensor("out_o", [D_MODEL, L_SEQ], F32, kind="ExternalOutput").ap()
    NI = D_FFN // 128
    ND = D_MODEL // 128

    idn_f = S.sb("idn_f", [128, 128], F32)
    idn_bf = S.sb("idn_bf", [128, 128], BF16)
    ones_col = S.sb("ones_col", [128, 1], F32)
    mx_sb = S.sb("mx_sb", [1, 8], F32)
    sca = S.sb("sca", [1, 2], F32)
    scb = S.sb("scb", [128, 2], F32)
    ps_bc = S.psum("ps_bc", [128, 2])
    emit_identity(S, idn_f)
    emit_identity(S, idn_bf)
    ones1, rq_cell, r7 = emit_consts(S)
    S.aop("vector", "memset", ones_col[:], 1.0, w=[ones_col])
    S.dma("sync", mx_sb[:], mo8[:])
    emit_scale_bcast(S, mx_sb[0:1, 0:8], sca, scb, ones1, ps_bc[:, 0:2], rq_cell)
    x2 = [S.sb(f"x2_{k}", [128, L_SEQ], F32) for k in range(ND)]
    wdT = [S.sb(f"wdT{k}", [128, D_MODEL], BF16) for k in range(NI)]

    wstage = [S.sb("wstage0", [128, D_FFN], F32)] * 2
    wq_bufs = [S.sb(f"wqb{j}", [128, D_FFN], BF16) for j in range(2)]
    sblk = S.sb("sblk", [128, 96], F32)
    rblk = S.sb("rblk", [128, 96], F32)
    ps_tr = [S.psum(f"ps_tr{j}", [128, 512], BF16) for j in range(2)]
    oq = [S.sb(f"oq{k}", [128, 1024], BF16) for k in range(NI)]
    ostage = [S.sb(f"ostage{j}", [128, 1024], F32) for j in range(2)]
    xtmp = S.sb("xtmp", [128, 1024], F32)
    x1st = [S.sb(f"x1st{j}", [128, 512], F32) for j in range(2)]
    ps_mm = [S.psum(f"ps_mm{j}", [128, 512]) for j in range(2)]
    for lh in range(2):
        co = lh * 1024
        for t in range(NI):
            st = ostage[t % 2]
            S.dma("sync", st[:], o_i[t * 128:(t + 1) * 128, co:co + 1024])
            emit_qdq_rounds(S, oq[t][:], st[:], scb, xtmp[:],
                            r_keys=[st], w_keys=[oq[t]])
        for t in range(ND):
            if lh == 0:
                st = wstage[t % 2]
                S.dma("sync", st[:], w_down[t * 128:(t + 1) * 128, :])
                S.aop("vector", "tensor_reduce", sblk[:],
                      st[:].rearrange("p (a b) -> p a b", a=96), AX, ALU.max,
                      apply_absolute_value=True, r=[st], w=[sblk])
                S.aop("vector", "tensor_scalar", sblk[:], sblk[:], r7[:, 0:1],
                      1e-12, ALU.mult, ALU.max, r=[sblk, r7], w=[sblk])
                S.aop("vector", "reciprocal", rblk[:], sblk[:], r=[sblk], w=[rblk])
                S.aop("vector", "tensor_tensor",
                      out=st[:].rearrange("p (a b) -> p a b", a=96),
                      in0=st[:].rearrange("p (a b) -> p a b", a=96),
                      in1=_rep(rblk, 96, 32), op=ALU.mult, r=[st, rblk], w=[st])
                S.aop("vector", "tensor_scalar", st[:], st[:], C_RND, C_RND,
                      ALU.add, ALU.subtract, r=[st], w=[st])
                wq = wq_bufs[t % 2]
                S.aop("vector", "tensor_tensor",
                      out=wq[:].rearrange("p (a b) -> p a b", a=96),
                      in0=st[:].rearrange("p (a b) -> p a b", a=96),
                      in1=_rep(sblk, 96, 32), op=ALU.mult, r=[st, sblk], w=[wq])
                for half in range(6):
                    bank = ps_tr[half % 2]
                    for j in range(4):
                        kblk = half * 4 + j
                        S.aop("tensor", "transpose",
                              bank[:, j * 128:(j + 1) * 128],
                              wq[:, kblk * 128:(kblk + 1) * 128], idn_bf[:],
                              r=[wq, idn_bf], w=[bank])
                    for j in range(4):
                        kblk = half * 4 + j
                        S.aop("scalar", "activation",
                              wdT[kblk][:, t * 128:(t + 1) * 128],
                              bank[:, j * 128:(j + 1) * 128], AF.Copy,
                              r=[bank], w=[f"wdT{kblk}:{t}"])
            for lc in range(2):
                sl = co + lc * 512
                xs = x1st[lc % 2]
                S.dma("sync", xs[:], x1n[t * 128:(t + 1) * 128, sl:sl + 512])
                bank = ps_mm[lc % 2]
                pairs = [(wdT[k][:, t * 128:(t + 1) * 128],
                          oq[k][:, lc * 512:(lc + 1) * 512]) for k in range(NI)]
                S.mm_group(bank[:], pairs,
                           r_extra=[f"wdT{k}:{t}" for k in range(NI)])
                S.aop("vector", "tensor_add", x2[t][:, sl:sl + 512],
                      bank[:], xs[:], r=[bank, xs], w=[x2[t]])
    S.barrier()
    S.release(*oq, ostage[0], ostage[1], xtmp, x1st[0], x1st[1],
              ps_mm[0], ps_mm[1], ps_tr[0], ps_tr[1], wstage[0],
              wq_bufs[0], wq_bufs[1], sblk, rblk)
    S.release(*wdT)

    x2t = [S.sb(f"x2t_{k}", [128, L_SEQ], F32) for k in range(ND)]
    ln2 = PlainLN(S, ND, L_SEQ, "ln2", ones1, ones_col, idn_f)
    ln2.emit([t[:] for t in x2], [t[:] for t in x2t], [t[:] for t in x2])
    for k in range(ND):
        S.dma("sync" if k % 2 == 0 else "scalar",
              out_o[k * 128:(k + 1) * 128, :], x2t[k][:])
    S.flush_engine("sync")
    S.flush_engine("scalar")
    return S.build()


def build_l1():
    S = Sched("l1")
    nc = S.nc
    x_i = nc.dram_tensor("x", [D_MODEL, L_SEQ], F32, kind="ExternalInput").ap()
    ln_g = nc.dram_tensor("ln_g", [1, D_MODEL], F32, kind="ExternalInput").ap()
    ln_b = nc.dram_tensor("ln_b", [1, D_MODEL], F32, kind="ExternalInput").ap()
    W_in = nc.dram_tensor("W_in", [D_MODEL, 2 * D_INNER], F32, kind="ExternalInput").ap()
    conv_w = nc.dram_tensor("conv_w", [D_INNER, 4], F32, kind="ExternalInput").ap()
    conv_b = nc.dram_tensor("conv_b", [1, D_INNER], F32, kind="ExternalInput").ap()
    W_xp = nc.dram_tensor("W_xp", [D_INNER, 96], F32, kind="ExternalInput").ap()
    W_dt = nc.dram_tensor("W_dt", [64, D_INNER], F32, kind="ExternalInput").ap()
    b_dt = nc.dram_tensor("b_dt", [1, D_INNER], F32, kind="ExternalInput").ap()
    A_log = nc.dram_tensor("A_log", [D_INNER, 16], F32, kind="ExternalInput").ap()
    D_par = nc.dram_tensor("D_par", [1, D_INNER], F32, kind="ExternalInput").ap()
    res_sc = nc.dram_tensor("res_sc", [1, 1], F32, kind="ExternalInput").ap()
    W_out_h = nc.dram_tensor("W_out", [D_INNER, D_MODEL], F32, kind="ExternalInput").ap()
    x1n_o = nc.dram_tensor("x1n_o", [D_MODEL, L_SEQ], F32, kind="ExternalOutput").ap()
    mx_o = nc.dram_tensor("mx_o", [1, 1], F32, kind="ExternalOutput").ap()
    sp_u2 = nc.dram_tensor("sp_u2", [D_INNER, L_SEQ], F32, kind="Internal").ap()
    sp_dt = nc.dram_tensor("sp_dt", [D_INNER, L_SEQ], F32, kind="Internal").ap()
    sp_z = nc.dram_tensor("sp_z", [D_INNER, L_SEQ], F32, kind="Internal").ap()
    sp_x2 = nc.dram_tensor("sp_x2", [D_MODEL, L_SEQ], F32, kind="Internal").ap()

    NE = D_INNER // 128
    ND = D_MODEL // 128
    TL1 = 512
    NCH1 = L_SEQ // TL1
    TL2 = 256
    NCH2 = L_SEQ // TL2
    NST = 16

    idn_f = S.sb("idn_f", [128, 128], F32)
    ones1 = S.sb("ones1", [1, 128], F32)
    ones_col = S.sb("ones_col", [128, 1], F32)
    gcol = S.sb("gcol", [128, ND], F32)
    bcol = S.sb("bcol", [128, ND], F32)
    cw = S.sb("cw", [128, 4 * NE], F32)
    cb = S.sb("cb", [128, NE], F32)
    bdt = S.sb("bdt", [128, NE], F32)
    Dp = S.sb("Dp", [128, NE], F32)
    A_sb = S.sb("A_sb", [128, NST * NE], F32)
    rsc = S.sb("rsc", [128, 1], F32)
    rsc_st = S.sb("rsc_st", [1, 1], F32)
    bu_acc = S.sb("bu_acc", [128, 32], F32)
    butmp = S.sb("butmp", [128, 32], F32)
    wxp_bf = S.sb("wxp_bf", [128, 96 * NE], BF16)
    wdt_bf = S.sb("wdt_bf", [64, D_INNER], BF16)
    bc_sb = S.sb("bc_sb", [32, L_SEQ], F32)
    sel_sb = S.sb("sel_sb", [32, 32 * 128], F32)
    mxrun = S.sb("mxrun", [128, 1], F32)
    mxtmp = S.sb("mxtmp", [128, 1], F32)
    mrow = S.sb("mrow", [1, 128], F32)
    mfin = S.sb("mfin", [1, 1], F32)

    emit_identity(S, idn_f)
    S.aop("vector", "memset", ones1[:], 1.0, w=[ones1])
    S.aop("vector", "memset", ones_col[:], 1.0, w=[ones_col])
    S.aop("vector", "memset", mxrun[:], 0.0, w=[mxrun])
    S.aop("gpsimd", "memset", sel_sb[:], 0.0, w=[sel_sb])
    S.aop("gpsimd", "affine_select",
          out=sel_sb[:].rearrange("p (a b) -> p a b", a=32),
          in_=sel_sb[:].rearrange("p (a b) -> p a b", a=32),
          compare_op=ALU.not_equal, fill=1.0, base=0,
          pattern=[[-1, 32], [0, 128]], channel_multiplier=1,
          r=[sel_sb], w=[sel_sb])
    for k in range(ND):
        S.dma("sync", gcol[:, k:k + 1], ln_g[0:1, k * 128:(k + 1) * 128])
        S.dma("sync", bcol[:, k:k + 1], ln_b[0:1, k * 128:(k + 1) * 128])
    for t in range(NE):
        S.dma("sync", cw[:, 4 * t:4 * t + 4], conv_w[t * 128:(t + 1) * 128, :])
        S.dma("sync", cb[:, t:t + 1], conv_b[0:1, t * 128:(t + 1) * 128])
        S.dma("sync", bdt[:, t:t + 1], b_dt[0:1, t * 128:(t + 1) * 128])
        S.dma("sync", Dp[:, t:t + 1], D_par[0:1, t * 128:(t + 1) * 128])
        S.dma("sync", A_sb[:, NST * t:NST * (t + 1)],
              A_log[t * 128:(t + 1) * 128, :])
        S.dma("gpsimd", wxp_bf[:, 96 * t:96 * (t + 1)],
              W_xp[t * 128:(t + 1) * 128, :])
    S.dma("gpsimd", wdt_bf[:], W_dt[:])
    S.dma("sync", rsc_st[:], res_sc[:])
    S.aop("scalar", "activation", A_sb[:], A_sb[:], AF.Exp, r=[A_sb], w=[A_sb])
    S.aop("vector", "tensor_scalar", A_sb[:], A_sb[:], -1.0, None, ALU.mult,
          r=[A_sb], w=[A_sb])

    win_bf = [S.sb(f"winbf{k}", [128, 2 * D_INNER], BF16) for k in range(ND)]
    winst2 = [S.sb("winst0", [128, 2 * D_INNER], F32)] * 2
    xdbl_dt = S.sb("xdbl_dt", [64, L_SEQ], BF16)
    ps_a = S.psum("ps_a", [128, 512])
    ps_b = S.psum("ps_b", [128, 512])
    ps_c = S.psum("ps_c", [128, 512])
    ps_d = S.psum("ps_d", [128, 512])
    ps_e = S.psum("ps_e", [128, 512])
    ps_f = S.psum("ps_f", [128, 512])
    ps_bu = ps_e
    ps4 = [ps_a, ps_b, ps_c, ps_d]

    S.aop("tensor", "matmul", ps_bu[:, 0:1], ones1[0:1, :], rsc_st[0:1, 0:1],
          start=True, stop=True, r=[ones1, rsc_st], w=["psbu0"])
    S.aop("scalar", "activation", rsc[:], ps_bu[:, 0:1], AF.Copy,
          r=["psbu0"], w=[rsc])

    S.aop("vector", "memset", bu_acc[:], 0.0, w=[bu_acc])
    for k in range(ND):
        winst = winst2[k % 2]
        S.dma("gpsimd", winst[:], W_in[k * 128:(k + 1) * 128, :])
        for s in range(32):
            S.aop("tensor", "matmul", ps_bu[:, s:s + 1],
                  winst[:, s * 128:(s + 1) * 128], bcol[:, k:k + 1],
                  start=True, stop=True, r=[winst, bcol], w=[f"psbu{s}"])
        S.aop("scalar", "activation", butmp[:], ps_bu[:, 0:32], AF.Copy,
              r=[f"psbu{s}" for s in range(32)], w=[butmp])
        S.aop("vector", "tensor_add", bu_acc[:], bu_acc[:], butmp[:],
              r=[bu_acc, butmp], w=[bu_acc])
        S.aop("scalar", "activation", win_bf[k][:], winst[:], AF.Identity,
              scale=gcol[:, k:k + 1], r=[winst, gcol], w=[win_bf[k]])

    x_fm = [S.sb(f"xfm{k}", [128, TL1], F32) for k in range(ND)]
    xn_bf = [S.sb(f"xnbf{k}", [128, TL1], BF16) for k in range(ND)]
    ln0tmp = [S.sb(f"ln0tmp{k}", [128, TL1], F32) for k in range(ND)]
    x2st2 = [S.sb(f"x2st{j}", [128, TL1], F32) for j in range(2)]
    u_sb = [S.sb(f"usb{j}", [128, TL1 + 3], F32) for j in range(2)]
    u_tail = S.sb("u_tail", [128, 3 * NE], F32)
    cacc = S.sb("cacc", [128, TL1], F32)
    u2f2 = [S.sb(f"u2f{j}", [128, TL1], F32) for j in range(2)]
    u2bf = [S.sb(f"u2bf{t}", [128, TL1], BF16) for t in range(NE)]
    zs2 = [S.sb(f"zs{j}", [128, TL1], F32) for j in range(2)]
    dtf2 = [S.sb(f"dtf{j}", [128, TL1], F32) for j in range(2)]
    sp_t1s = [S.sb("sp_t10", [128, TL1], F32)] * 2
    sp_t2s = [S.sb("sp_t20", [128, TL1], F32)] * 2
    ln0 = PlainLN(S, ND, TL1, "ln0", ones1, ones_col, idn_f,
                  ps_banks=[ps_c, ps_d])
    S.aop("vector", "memset", u_tail[:], 0.0, w=[u_tail])

    for c in range(NCH1):
        c0 = c * TL1
        for k in range(ND):
            S.dma("sync", x_fm[k][:], x_i[k * 128:(k + 1) * 128, c0:c0 + TL1])
        for k in range(ND):
            xb = x2st2[k % 2]
            S.aop("vector", "tensor_scalar", xb[:], x_fm[k][:], 2.0, None,
                  ALU.mult, r=[x_fm[k]], w=[xb])
            S.dma("gpsimd", sp_x2[k * 128:(k + 1) * 128, c0:c0 + TL1], xb[:])
        ln0.emit([t[:] for t in x_fm], [t[:] for t in x_fm],
                 [t[:] for t in ln0tmp], tiles_final=[t[:] for t in xn_bf])
        for t in range(NE):
            bank = ps4[t % 4]
            pairs = [(win_bf[k][:, t * 128:(t + 1) * 128], xn_bf[k][:])
                     for k in range(ND)]
            S.mm_group(bank[:], pairs)
            usb = u_sb[t % 2]
            S.aop("vector", "tensor_copy", usb[:, 0:3], u_tail[:, 3 * t:3 * t + 3],
                  r=[u_tail], w=[usb])
            S.aop("scalar", "activation", usb[:, 3:3 + TL1], bank[:],
                  AF.Identity, bias=bu_acc[:, t:t + 1], r=[bank, bu_acc],
                  w=[usb])
            S.aop("vector", "tensor_copy", u_tail[:, 3 * t:3 * t + 3],
                  usb[:, TL1:TL1 + 3], r=[usb], w=[u_tail])
            S.aop("vector", "tensor_scalar", cacc[:], usb[:, 0:TL1],
                  cw[:, 4 * t:4 * t + 1], None, ALU.mult,
                  r=[usb, cw], w=[cacc])
            for q in range(1, 4):
                S.aop("vector", "scalar_tensor_tensor", out=cacc[:],
                      in0=usb[:, q:q + TL1], scalar=cw[:, 4 * t + q:4 * t + q + 1],
                      in1=cacc[:], op0=ALU.mult, op1=ALU.add,
                      r=[usb, cw, cacc], w=[cacc])
            uf = u2f2[t % 2]
            S.aop("scalar", "activation", uf[:], cacc[:], AF.Silu,
                  bias=cb[:, t:t + 1], r=[cacc, cb], w=[uf])
            S.aop("vector", "tensor_copy", u2bf[t][:], uf[:],
                  r=[uf], w=[u2bf[t]])
            S.dma("gpsimd", sp_u2[t * 128:(t + 1) * 128, c0:c0 + TL1], uf[:])
        for t in range(NE):
            bank = ps4[t % 4]
            pairs = [(win_bf[k][:, 2048 + t * 128:2048 + (t + 1) * 128],
                      xn_bf[k][:]) for k in range(ND)]
            S.mm_group(bank[:], pairs)
            zb = zs2[t % 2]
            S.aop("scalar", "activation", zb[:], bank[:], AF.Silu,
                  bias=bu_acc[:, 16 + t:16 + t + 1], r=[bank, bu_acc], w=[zb])
            S.dma("gpsimd", sp_z[t * 128:(t + 1) * 128, c0:c0 + TL1], zb[:])
        pairs_dt = [(wxp_bf[:, 96 * t:96 * t + 64], u2bf[t][:]) for t in range(NE)]
        S.mm_group(ps_a[0:64, :], pairs_dt)
        S.aop("scalar", "activation", xdbl_dt[:, c0:c0 + TL1], ps_a[0:64, :],
              AF.Copy, r=[ps_a], w=[xdbl_dt])
        pairs_bc = [(wxp_bf[:, 96 * t + 64:96 * (t + 1)], u2bf[t][:])
                    for t in range(NE)]
        S.mm_group(ps_b[0:32, :], pairs_bc)
        S.aop("scalar", "activation", bc_sb[:, c0:c0 + TL1], ps_b[0:32, :],
              AF.Copy, r=[ps_b], w=[bc_sb])
        for t in range(NE):
            bank = ps_e if t % 2 == 0 else ps_f
            S.mm_group(bank[:], [(wdt_bf[:, t * 128:(t + 1) * 128],
                                  xdbl_dt[:, c0:c0 + TL1])])
            t1 = sp_t1s[t % 2]
            t2 = sp_t2s[t % 2]
            df = dtf2[t % 2]
            S.aop("scalar", "activation", t1[:], bank[:], AF.Abs,
                  bias=bdt[:, t:t + 1], r=[bank, bdt], w=[t1])
            S.aop("scalar", "activation", t1[:], t1[:], AF.Exp, scale=-1.0,
                  r=[t1], w=[t1])
            S.aop("scalar", "activation", t1[:], t1[:], AF.Ln, bias=1.0,
                  r=[t1], w=[t1])
            S.aop("scalar", "activation", t2[:], bank[:], AF.Relu,
                  bias=bdt[:, t:t + 1], r=[bank, bdt], w=[t2])
            S.aop("vector", "tensor_add", df[:], t1[:], t2[:],
                  r=[t1, t2], w=[df])
            S.dma("gpsimd", sp_dt[t * 128:(t + 1) * 128, c0:c0 + TL1], df[:])

    S.barrier()
    S.release(*win_bf, winst2[0], xdbl_dt, ps_a, ps_b, ps_c, ps_d,
              ps_e, ps_f,
              *x_fm, *xn_bf, *ln0tmp, *x2st2,
              u_sb[0], u_sb[1], u_tail, cacc, *u2f2, *u2bf, *zs2, *dtf2,
              sp_t1s[0], sp_t2s[0],
              ln0.ps_su, ln0.ps_tp, ln0.mu_cols,
              ln0.st_row, ln0.mu_bc, ln0.rstd_bc, ln0.var_cols, ln0.eps_col)

    wout_bf = [S.sb(f"wout{t}", [128, D_MODEL], BF16) for t in range(NE)]
    for t in range(NE):
        S.dma("gpsimd", wout_bf[t][:], W_out_h[t * 128:(t + 1) * 128, :])
    dt_c = S.sb("dt_c", [128, NE * TL2], F32)
    dtu_c = S.sb("dtu_c", [128, NE * TL2], F32)
    y_c = S.sb("y_c", [128, NE * TL2], F32)
    x2_c = S.sb("x2_c", [128, ND * TL2], F32)
    PITCH = TL2 + 1
    dA_g = [S.sb(f"dAg{j}", [128, NE * PITCH], F32) for j in range(2)]
    dtuB_g = S.sb("dtuB_g", [128, NE * PITCH], F32)
    h_sc = S.sb("h_sc", [128, NE * PITCH], F32)
    h_g = dtuB_g
    carry = S.sb("carry", [128, NST * NE], F32)
    bcn = [S.sb(f"bcn{j}", [128, 2 * TL2], F32) for j in range(4)]
    x1_c = S.sb("x1_c", [128, ND * TL2], F32)
    x1n_c = x2_c
    ps_bcn = [S.psum(f"ps_bcn{j}", [128, 2 * TL2]) for j in range(2)]
    ps_wo = [S.psum(f"ps_wo{j}", [128, TL2]) for j in range(2)]
    ln1 = PlainLN(S, ND, TL2, "ln1", ones1, ones_col, idn_f,
                  ps_banks=[ps_bcn[0], ps_bcn[1]])
    S.aop("vector", "memset", carry[:], 0.0, w=[carry])
    S.aop("vector", "memset", dA_g[0][:], 0.0, w=[dA_g[0]])
    S.aop("vector", "memset", dA_g[1][:], 0.0, w=[dA_g[1]])
    S.aop("vector", "memset", dtuB_g[:, NE * PITCH - 1:NE * PITCH], 0.0,
          w=[dtuB_g])

    def vp(t3, width=TL2):
        return t3[:].rearrange("p (a b) -> p a b", a=NE)[:, :, 0:width]
    y2bf = dtu_c

    def v3(t3):
        return t3[:].rearrange("p (a b) -> p a b", a=NE)

    for c in range(NCH2):
        c0 = c * TL2
        S.dma("sync", dt_c[:].rearrange("p (a b) -> p a b", a=NE),
              sp_dt[:, c0:c0 + TL2].rearrange("(a p) b -> p a b", p=128))
        S.dma("scalar", vp(h_g),
              sp_u2[:, c0:c0 + TL2].rearrange("(a p) b -> p a b", p=128))
        S.dma("gpsimd", x2_c[:].rearrange("p (a b) -> p a b", a=ND),
              sp_x2[:, c0:c0 + TL2].rearrange("(a p) b -> p a b", p=128))
        S.aop("vector", "tensor_mul", out=v3(dtu_c), in0=v3(dt_c),
              in1=vp(h_g), op=None, r=[dt_c, h_g], w=[dtu_c]) if False else \
        S.aop("vector", "tensor_tensor", out=v3(dtu_c), in0=v3(dt_c),
              in1=vp(h_g), op=ALU.mult, r=[dt_c, h_g], w=[dtu_c])
        for n in range(NST):
            if n % 4 == 0:
                for nn in range(n, n + 4):
                    pb = ps_bcn[nn % 2]
                    S.aop("tensor", "matmul", pb[:, 0:TL2],
                          sel_sb[:, nn * 128:(nn + 1) * 128],
                          bc_sb[:, c0:c0 + TL2], start=True, stop=True,
                          r=[sel_sb, bc_sb], w=[pb])
                    S.aop("tensor", "matmul", pb[:, TL2:2 * TL2],
                          sel_sb[:, (16 + nn) * 128:(17 + nn) * 128],
                          bc_sb[:, c0:c0 + TL2], start=True, stop=True,
                          r=[sel_sb, bc_sb], w=[pb])
                    S.aop("scalar", "activation", bcn[nn % 4][:], pb[:], AF.Copy,
                          r=[pb], w=[bcn[nn % 4]])
            bb = bcn[n % 4]
            dag = dA_g[n % 2]
            for t in range(NE):
                S.aop("scalar", "activation",
                      dag[:, t * PITCH:t * PITCH + TL2],
                      dt_c[:, t * TL2:(t + 1) * TL2], AF.Exp,
                      scale=A_sb[:, NST * t + n:NST * t + n + 1],
                      r=[dt_c, A_sb], w=[dag])
            S.aop("vector", "tensor_tensor", out=vp(dtuB_g), in0=v3(dtu_c),
                  in1=bb[:, 0:TL2].unsqueeze(1).broadcast_to([128, NE, TL2]),
                  op=ALU.mult, r=[dtu_c, bb], w=[dtuB_g])
            S.aop("vector", "tensor_copy",
                  dtuB_g[:].rearrange("p (a b) -> p a b", a=NE)[:, 0:NE - 1,
                                                               TL2:TL2 + 1],
                  carry[:, n * NE + 1:(n + 1) * NE].unsqueeze(2),
                  r=[carry], w=[dtuB_g])
            S.aop("vector", "tensor_tensor_scan",
                  h_sc[:], dag[:], dtuB_g[:],
                  carry[:, n * NE:n * NE + 1], ALU.mult, ALU.add,
                  r=[dag, dtuB_g, carry], w=[h_sc])
            S.aop("vector", "tensor_copy",
                  carry[:, n * NE:(n + 1) * NE],
                  h_sc[:].rearrange("p (a b) -> p a b", a=NE)[:, :,
                      TL2 - 1:TL2].rearrange("p a b -> p (a b)"),
                  r=[h_sc], w=[carry])
            if n == 0:
                S.aop("vector", "tensor_tensor", out=v3(y_c), in0=vp(h_sc),
                      in1=bb[:, TL2:2 * TL2].unsqueeze(1).broadcast_to(
                          [128, NE, TL2]),
                      op=ALU.mult, r=[h_sc, bb], w=[y_c])
            else:
                S.aop("vector", "tensor_tensor", out=vp(dtuB_g), in0=vp(h_sc),
                      in1=bb[:, TL2:2 * TL2].unsqueeze(1).broadcast_to(
                          [128, NE, TL2]),
                      op=ALU.mult, r=[h_sc, bb], w=[dtuB_g])
                S.aop("vector", "tensor_add", out=v3(y_c), in0=v3(y_c),
                      in1=vp(dtuB_g), r=[y_c, dtuB_g], w=[y_c])
        S.dma("scalar", vp(h_g),
              sp_u2[:, c0:c0 + TL2].rearrange("(a p) b -> p a b", p=128))
        S.dma("sync", dt_c[:].rearrange("p (a b) -> p a b", a=NE),
              sp_z[:, c0:c0 + TL2].rearrange("(a p) b -> p a b", p=128))
        for t in range(NE):
            S.aop("scalar", "activation", h_sc[:, t * TL2:(t + 1) * TL2],
                  h_g[:, t * PITCH:t * PITCH + TL2], AF.Identity,
                  scale=Dp[:, t:t + 1], r=[h_g, Dp], w=[h_sc])
        S.aop("vector", "tensor_add", y_c[:], y_c[:], h_sc[:, 0:NE * TL2],
              r=[y_c, h_sc], w=[y_c])
        y2v = y2bf[:].bitcast(BF16)[:, 0:NE * TL2]
        S.aop("vector", "tensor_mul", y2v, y_c[:], dt_c[:], r=[y_c, dt_c],
              w=["y2bf", y2bf])
        for m in range(ND):
            bank = ps_wo[m % 2]
            pairs = [(wout_bf[t][:, m * 128:(m + 1) * 128],
                      y2v[:, t * TL2:(t + 1) * TL2]) for t in range(NE)]
            S.mm_group(bank[:], pairs, r_extra=["y2bf"])
            S.aop("vector", "scalar_tensor_tensor",
                  out=x1_c[:, m * TL2:(m + 1) * TL2], in0=bank[:],
                  scalar=rsc[:, 0:1], in1=x2_c[:, m * TL2:(m + 1) * TL2],
                  op0=ALU.mult, op1=ALU.add, r=[bank, rsc, x2_c], w=[x1_c])
        ln1.emit([x1_c[:, m * TL2:(m + 1) * TL2] for m in range(ND)],
                 [x1n_c[:, m * TL2:(m + 1) * TL2] for m in range(ND)],
                 [x1_c[:, m * TL2:(m + 1) * TL2] for m in range(ND)])
        S.dma("gpsimd", x1n_o[:, c0:c0 + TL2].rearrange("(a p) b -> p a b", p=128),
              x1n_c[:].rearrange("p (a b) -> p a b", a=ND))
        S.aop("vector", "tensor_reduce", mxtmp[:], x1n_c[:], AX, ALU.max,
              apply_absolute_value=True, r=[x1n_c], w=[mxtmp])
        S.aop("vector", "tensor_tensor", mxrun[:], mxrun[:], mxtmp[:], ALU.max,
              r=[mxrun, mxtmp], w=[mxrun])

    S.aop("tensor", "transpose", ln1.ps_tp[0:1, 0:128], mxrun[:], idn_f[:],
          r=[mxrun, idn_f], w=[ln1.ps_tp])
    S.aop("scalar", "activation", mrow[:], ln1.ps_tp[0:1, 0:128], AF.Copy,
          r=[ln1.ps_tp], w=[mrow])
    S.aop("vector", "tensor_reduce", mfin[:], mrow[:], AX, ALU.max, r=[mrow],
          w=[mfin])
    S.dma("sync", mx_o[:], mfin[:])
    S.flush_engine("sync")
    return S.build()



_CACHE = {}


def _progs():
    if "p" not in _CACHE:
        _CACHE["p"] = (build_l1(), build_l2(), build_l3(), build_l4(), build_l5())
    return _CACHE["p"]


def kernel(L0_attn_x, L0_attn_ln_g, L0_attn_ln_b, L0_attn_W_in, L0_attn_conv_w,
           L0_attn_conv_b, L0_attn_W_xp, L0_attn_W_dt, L0_attn_b_dt,
           L0_attn_A_log, L0_attn_D_param, L0_attn_W_out, L0_attn_res_scale,
           L0_ffn_w_gate, L0_ffn_w_up, L0_ffn_w_down):
    n = 8
    nc1, nc2, nc3, nc4, nc5 = _progs()
    cores = list(range(n))
    c = np.ascontiguousarray

    im1 = [{
        "x": c(L0_attn_x[i].T),
        "ln_g": c(L0_attn_ln_g.reshape(1, -1)), "ln_b": c(L0_attn_ln_b.reshape(1, -1)),
        "W_in": c(L0_attn_W_in), "conv_w": c(L0_attn_conv_w),
        "conv_b": c(L0_attn_conv_b.reshape(1, -1)), "W_xp": c(L0_attn_W_xp),
        "W_dt": c(L0_attn_W_dt), "b_dt": c(L0_attn_b_dt.reshape(1, -1)),
        "A_log": c(L0_attn_A_log), "D_par": c(L0_attn_D_param.reshape(1, -1)),
        "res_sc": c(L0_attn_res_scale.reshape(1, 1)), "W_out": c(L0_attn_W_out),
    } for i in range(n)]
    r1 = run_bass_kernel_spmd(nc1, im1, core_ids=cores).results
    mx8 = np.concatenate([r1[i]["mx_o"] for i in range(n)], axis=1)

    im2 = [{"x1n": r1[i]["x1n_o"], "mx8": mx8,
            "w_up": c(L0_ffn_w_up), "w_gate": c(L0_ffn_w_gate)} for i in range(n)]
    r2 = run_bass_kernel_spmd(nc2, im2, core_ids=cores).results
    mup8 = np.concatenate([r2[i]["m_up"] for i in range(n)], axis=1)
    mg8 = np.concatenate([r2[i]["m_gate"] for i in range(n)], axis=1)

    im3 = [{"up_i": r2[i]["up_o"], "gate_i": r2[i]["gate_o"],
            "mup8": mup8, "mg8": mg8} for i in range(n)]
    r3 = run_bass_kernel_spmd(nc3, im3, core_ids=cores).results
    mg28 = np.concatenate([r3[i]["m_g2"] for i in range(n)], axis=1)

    im4 = [{"upq_i": r3[i]["upq_o"], "g2_i": r3[i]["gate2_o"], "mg28": mg28}
           for i in range(n)]
    r4 = run_bass_kernel_spmd(nc4, im4, core_ids=cores).results
    mo8 = np.concatenate([r4[i]["m_o"] for i in range(n)], axis=1)

    im5 = [{"o_i": r4[i]["o_o"], "x1n": r1[i]["x1n_o"], "mo8": mo8,
            "w_down": c(L0_ffn_w_down)} for i in range(n)]
    r5 = run_bass_kernel_spmd(nc5, im5, core_ids=cores).results

    return np.stack([r5[i]["out_o"].T for i in range(n)], axis=0)
